# version 13
# baseline (speedup 1.0000x reference)
import numpy as np
import ml_dtypes
from contextlib import ExitStack
import concourse.bass as bass
import concourse.mybir as mybir
from concourse.bass_utils import run_bass_kernel_spmd

F32 = mybir.dt.float32
BF16 = mybir.dt.bfloat16
I32 = mybir.dt.int32
AF = mybir.ActivationFunctionType
ALU = mybir.AluOpType
AXX = mybir.AxisListType.X

NCORE = 8
D = 1024
ROWS = 4096
HDR = 16
NHEAD = 16
CFAV = float(64.0 ** -0.25)
LN2_3 = float(3 * np.log(2.0))
EPS_FEAT = 1e-4
EPS_DEN = 0.01
SCALE_FFN0 = 1.0 / 127.0
STATS_ROWS = 80


class Ctx:
    def __init__(self, nc, es, parent=None, prefix="", io=None):
        self.nc = nc
        self.es = es
        self.prefix = prefix
        self.io = io or {}
        if parent is None:
            self.sems = {}
            self.cnt = {}
            for s in ['L', 'G', 'P', 'A', 'V', 'S']:
                self.sems[s] = es.enter_context(nc.semaphore(s))
                self.cnt[s] = 0
            self.solo = True
        else:
            self.sems = parent.sems
            self.cnt = parent.cnt
            self.solo = False

    def inc(self, name, instr, n=1):
        instr.then_inc(self.sems[name], n)
        self.cnt[name] += n

    def wait(self, eng, *names):
        for n in names:
            eng.wait_ge(self.sems[n], self.cnt[n])

    def sb(self, name, shape, dt):
        return self.es.enter_context(self.nc.sbuf_tensor("sb_" + self.prefix + name, shape, dt))

    def ps(self, name, shape, dt=F32):
        return self.es.enter_context(self.nc.psum_tensor(self.prefix + name, shape, dt))

    def dram(self, name, shape, dt, kind):
        if name in self.io:
            return self.io[name]
        if self.solo:
            k = {"in": "ExternalInput", "out": "ExternalOutput"}[kind]
            return self.nc.dram_tensor(name, shape, dt, kind=k).ap()
        return self.nc.dram_tensor(self.prefix + name, shape, dt).ap()


from contextlib import nullcontext


def full_barrier(c):
    nc = c.nc
    for eng in [nc.sync, nc.gpsimd, nc.tensor, nc.scalar, nc.vector]:
        for s in ['L', 'G', 'P', 'A', 'V', 'S']:
            eng.wait_ge(c.sems[s], c.cnt[s])


def begin(env, prefix):
    if env is None:
        nc = bass.Bass("TRN2", target_bir_lowering=False)
        es = ExitStack()
        c = Ctx(nc, es)
        blk = nc.Block()
    else:
        nc = env['nc']
        es = ExitStack()
        c = Ctx(nc, es, parent=env['c'], prefix=prefix, io=env.get('io', {}))
        blk = nullcontext()
    return nc, es, c, blk


def mk_ln(c, mmax, tag="ln", sq=None, ps_stat=None):
    return dict(
        mmax=mmax,
        ones=c.sb(tag + "_ones", [128, 1], BF16),
        eps=c.sb(tag + "_eps", [1, 1], F32),
        sq=sq if sq is not None else c.sb(tag + "_sq", [128, 8, mmax], BF16),
        mean=c.sb(tag + "_mean", [1, mmax], F32),
        msq=c.sb(tag + "_msq", [1, mmax], F32),
        var=c.sb(tag + "_var", [1, mmax], F32),
        rstd=c.sb(tag + "_rstd", [1, 2 * mmax], F32),
        bc=c.sb(tag + "_bc", [128, 2, mmax], F32),
        bounce=c.nc.dram_tensor(c.prefix + tag + "_bounce", [1, 2 * mmax], F32),
        ps_stat=ps_stat if ps_stat is not None else c.ps(tag + "_psstat", [128, 512], F32),
    )


def ln_init(c, ln):
    nc = c.nc
    nc.gpsimd.memset(ln['ones'][:, :], 2.0 ** -10)
    i = nc.gpsimd.memset(ln['eps'][:, :], 1e-5)
    c.inc('G', i)


def emit_ln(c, ln, xb_tiles, m):
    nc = c.nc
    mx_ = ln['mmax']
    for t in range(8):
        i = nc.scalar.activation(ln['sq'][:, t, :m], xb_tiles[t], AF.Square)
    c.inc('A', i)
    c.wait(nc.tensor, 'A')
    for t in range(8):
        i = nc.tensor.matmul(ln['ps_stat'][0:1, :m], ln['ones'][:, :], xb_tiles[t],
                             start=(t == 0), stop=(t == 7), skip_group_check=True)
    for t in range(8):
        i = nc.tensor.matmul(ln['ps_stat'][64:65, :m], ln['ones'][:, :], ln['sq'][:, t, :m],
                             start=(t == 0), stop=(t == 7), skip_group_check=True)
    c.inc('P', i)
    c.wait(nc.scalar, 'P')
    nc.scalar.copy(ln['mean'][:, :m], ln['ps_stat'][0:1, :m])
    i = nc.scalar.copy(ln['msq'][:, :m], ln['ps_stat'][64:65, :m])
    c.inc('A', i)
    c.wait(nc.vector, 'A')
    nc.vector.tensor_tensor(ln['var'][:, :m], ln['mean'][:, :m], ln['mean'][:, :m], op=ALU.mult)
    i = nc.vector.tensor_tensor(ln['var'][:, :m], ln['msq'][:, :m], ln['var'][:, :m], op=ALU.subtract)
    c.inc('V', i)
    c.wait(nc.scalar, 'V')
    nc.scalar.activation(ln['rstd'][:, 0:m], ln['var'][:, :m], AF.Ln, bias=ln['eps'][0:1, :])
    i = nc.scalar.activation(ln['rstd'][:, 0:m], ln['rstd'][:, 0:m], AF.Exp, scale=-0.5)
    c.inc('A', i)
    c.wait(nc.vector, 'A')
    i = nc.vector.scalar_tensor_tensor(ln['rstd'][:, mx_:mx_ + m], ln['mean'][:, :m], -1.0,
                                       ln['rstd'][:, 0:m], op0=ALU.mult, op1=ALU.mult)
    c.inc('V', i)
    c.wait(nc.sync, 'V', 'A')
    c.inc('L', nc.sync.dma_start(out=ln['bounce'][0:1, 0:mx_ + m], in_=ln['rstd'][:, 0:mx_ + m]), 16)
    c.wait(nc.sync, 'L')
    i = nc.sync.dma_start(
        out=ln['bc'][:, :, :m],
        in_=ln['bounce'].ap().squeeze(0).rearrange("(a x) -> a x", a=2)[:, 0:m]
             .unsqueeze(0).partition_broadcast(128).squeeze(1))
    c.inc('L', i, 16)


def emit_normalize(c, ln, src_tiles, out_tiles, m, tmpf, add_tiles=None, out3=None):
    nc = c.nc
    for t in range(8):
        nc.vector.scalar_tensor_tensor(tmpf[:, :m], src_tiles[t], 1.0,
                                       ln['bc'][:, 0, :m], op0=ALU.mult, op1=ALU.mult)
        if add_tiles is None:
            i = nc.vector.tensor_tensor(out_tiles[t], tmpf[:, :m], ln['bc'][:, 1, :m], op=ALU.add)
        else:
            nc.vector.tensor_tensor(tmpf[:, :m], tmpf[:, :m], ln['bc'][:, 1, :m], op=ALU.add)
            i = nc.vector.tensor_tensor(out3[t], tmpf[:, :m], add_tiles[t], op=ALU.add)
    c.inc('V', i)


def emit_chunk_load(c, xb, xf, m, xr=None, sl=None, xhr=None, need_f32=True):
    nc = c.nc
    if xhr is not None:
        i = nc.gpsimd.memset(xb[:, :, 0:m], 0.0)
        c.inc('G', i)
        if need_f32:
            i = nc.gpsimd.memset(xf[:, :, 0:m], 0.0)
            c.inc('G', i)
        c.inc('G', nc.gpsimd.dma_start(out=xb[:, :, 0:HDR], in_=xhr), 16)
        if need_f32:
            c.wait(nc.sync, 'G')
            c.inc('L', nc.sync.dma_start(out=xf[:, :, 0:HDR], in_=xhr), 16)
    else:
        c.inc('G', nc.gpsimd.dma_start(out=xb[:, :, :], in_=xr[:, :, sl]), 16)
        if need_f32:
            c.inc('L', nc.sync.dma_start(out=xf[:, :, :], in_=xr[:, :, sl]), 16)
    nc.scalar.wait_ge(c.sems['G'], c.cnt['G'])
    nc.tensor.wait_ge(c.sems['G'], c.cnt['G'])
    nc.vector.wait_ge(c.sems['G'], c.cnt['G'])


def emit_proj_fm(c, W, src, dst, m, ps_big, bias_col=None, scale=None):
    nc = c.nc
    kw = {}
    if scale is not None:
        kw['scale'] = np.float32(scale)
    for t in range(8):
        psl = slice((t % 2) * 512, (t % 2) * 512 + 512)
        if t >= 2:
            c.wait(nc.tensor, 'A')
        for k in range(8):
            i = nc.tensor.matmul(ps_big[:, psl][:, 0:m], W[:, k, t * 128:(t + 1) * 128],
                                 src[:, k, 0:m], start=(k == 0), stop=(k == 7),
                                 skip_group_check=True)
        c.inc('P', i)
        c.wait(nc.scalar, 'P')
        if bias_col is not None:
            i = nc.scalar.activation(dst[:, t, 0:m], ps_big[:, psl][:, 0:m], AF.Identity,
                                     bias=bias_col[:, t:t + 1], **kw)
        elif scale is not None:
            i = nc.scalar.activation(dst[:, t, 0:m], ps_big[:, psl][:, 0:m], AF.Identity, **kw)
        else:
            i = nc.scalar.copy(dst[:, t, 0:m], ps_big[:, psl][:, 0:m])
        c.inc('A', i)


def emit_proj_fm2(c, W, src, raw, rawb, m, ps_big, bias_col=None, scale=None):
    nc = c.nc
    kw = {}
    if scale is not None:
        kw['scale'] = np.float32(scale)
    for t in range(8):
        psl = slice((t % 2) * 512, (t % 2) * 512 + 512)
        if t >= 2:
            c.wait(nc.tensor, 'A', 'V')
        for k in range(8):
            i = nc.tensor.matmul(ps_big[:, psl][:, 0:m], W[:, k, t * 128:(t + 1) * 128],
                                 src[:, k, 0:m], start=(k == 0), stop=(k == 7),
                                 skip_group_check=True)
        c.inc('P', i)
        c.wait(nc.scalar, 'P')
        bkw = dict(kw)
        if bias_col is not None:
            bkw['bias'] = bias_col[:, t:t + 1]
        i = nc.scalar.activation(raw[:, t, 0:m], ps_big[:, psl][:, 0:m], AF.Identity, **bkw)
        c.inc('A', i)
        c.wait(nc.vector, 'A')
        i = nc.vector.tensor_copy(rawb[:, t, 0:m], raw[:, t, 0:m])
        c.inc('V', i)


def pe_barrier(c):
    nc = c.nc
    i = nc.tensor.nop()
    c.inc('P', i)
    nc.tensor.wait_ge(c.sems['P'], c.cnt['P'])


def emit_favorp_head_mm(c, src_bf, Pst, ps_big, msl):
    nc = c.nc
    for par_pass in range(2):
        for h in range(par_pass, NHEAD, 2):
            par = 64 * (h % 2)
            i = nc.tensor.matmul(ps_big[:, h * 64:(h + 1) * 64],
                                 src_bf[par:par + 64, h // 2, msl],
                                 Pst[par:par + 64, h * 64:(h + 1) * 64],
                                 start=(h in (0, 8)), stop=(h in (7, 15)),
                                 skip_group_check=True)
        if par_pass == 0:
            c.inc('P', i)
            nc.tensor.wait_ge(c.sems['P'], c.cnt['P'])
    c.inc('P', i)


def emit_diag(c, dia, sq_src, qbf, diag_fm, diag_tm, idf, ps_aux, ps_trf, m):
    nc = c.nc
    c.wait(nc.scalar, 'A')
    for t in range(8):
        i = nc.scalar.activation(sq_src[:, t, 0:m], qbf[:, t, 0:m], AF.Square)
    c.inc('A', i)
    c.wait(nc.tensor, 'A')
    for t in range(8):
        i = nc.tensor.matmul(ps_aux[0:16, 0:m], dia[:, t, :], sq_src[:, t, 0:m],
                             start=(t == 0), stop=(t == 7), skip_group_check=True)
    c.inc('P', i)
    c.wait(nc.scalar, 'P')
    i = nc.scalar.activation(diag_fm[:, 0:m], ps_aux[0:16, 0:m], AF.Identity,
                             bias=np.float32(2 * LN2_3))
    c.inc('A', i)
    c.wait(nc.tensor, 'A')
    nms = m // 128
    for ms in range(nms):
        i = nc.tensor.transpose(ps_trf[:, ms * 16:ms * 16 + 16],
                                diag_fm[:, ms * 128:(ms + 1) * 128], idf[0:16, 0:16])
    c.inc('P', i)
    c.wait(nc.scalar, 'P')
    for ms in range(nms):
        i = nc.scalar.copy(diag_tm[:, ms, :], ps_trf[:, ms * 16:ms * 16 + 16])
    c.inc('A', i)


def build_L0A_stats(env=None, rows=ROWS):
    nc, es, c, blk = begin(env, "pa_")
    M = 512
    NCH = rows // M
    xT = c.dram("xT", [D, rows], F32, "in")
    WkT_d = c.dram("WkT", [D, D], F32, "in")
    WvT_d = c.dram("WvT", [D, D], F32, "in")
    P_d = c.dram("Pstage", [128, D], F32, "in")
    bkc_d = c.dram("bkc", [128, 8], F32, "in")
    bvr_d = c.dram("bvr", [1, D], F32, "in")
    dia_d = c.dram("diagones", [128, 8, 16], F32, "in")
    idf_d = c.dram("idf", [128, 128], F32, "in")
    stats_d = c.dram("stats", [STATS_ROWS, D], F32, "out")

    WkT = c.sb("WkT", [128, 8, D], BF16)
    WvT = c.sb("WvT", [128, 8, D], BF16)
    Pst = c.sb("Pst", [128, D], BF16)
    bkc = c.sb("bkc", [128, 8], F32)
    bkcs = c.sb("bkcs", [128, 8], F32)
    bvr = c.sb("bvr", [1, D], BF16)
    dia = c.sb("dia", [128, 8, 16], BF16)
    idf = c.sb("idf", [128, 128], F32)
    ones_row = c.sb("ones_row", [1, 128], BF16)
    ones_col = c.sb("ones_colb", [128, 1], BF16)

    xb = c.sb("xb", [128, 8, M], BF16)
    xn = c.sb("xn", [128, 8, M], BF16)
    tmpf = c.sb("tmpf", [128, M], F32)
    kbf = c.sb("kbf", [128, 8, M], BF16)
    sqk = c.sb("sqk", [128, 8, M], BF16)
    vtm = c.sb("vtm", [128, 4, D], BF16)
    diag_fm = c.sb("diag_fm", [16, M], F32)
    diag_tm = c.sb("diag_tm", [128, 4, 16], F32)
    stab = c.sb("stab", [128, 16], F32)
    pre = c.sb("pre", [128, D], F32)
    keq = c.sb("keq", [128, 16, 65], BF16)
    kmax_acc = c.sb("kmax_acc", [128, 16], F32)
    km_t = c.sb("km_t", [128, 16], F32)
    kv_ev = c.sb("kv_ev", [128, 1024], F32)
    ks_ev = c.sb("ks_ev", [128, 8], F32)
    ks_tr = c.sb("ks_tr", [8, 128], F32)
    kmx_tr = c.sb("kmx_tr", [16, 128], F32)
    kmx_f = c.sb("kmx_f", [16, 1], F32)
    zerot = c.sb("zerot", [16, 1024], F32)

    ln = mk_ln(c, M)
    ps_kv = c.ps("ps_kv", [128, 1024])
    ps_ks = c.ps("ps_ks", [128, 512])
    ps_big = c.ps("ps_big", [128, 1024])
    ps_trf = c.ps("ps_trf", [128, 512])

    with blk:
        ln_init(c, ln)
        nc.gpsimd.memset(ones_row[:, :], 1.0)
        nc.gpsimd.memset(ones_col[:, :], 1.0)
        nc.gpsimd.memset(zerot[:, :], 0.0)
        nc.gpsimd.memset(keq[:, :, :], 1.0)
        i = nc.gpsimd.memset(kmax_acc[:, :], -1e30)
        c.inc('G', i)
        c.inc('G', nc.gpsimd.dma_start(out=WkT[:, :, :], in_=WkT_d.rearrange("(k p) n -> p k n", p=128)), 16)
        c.inc('G', nc.gpsimd.dma_start(out=WvT[:, :, :], in_=WvT_d.rearrange("(k p) n -> p k n", p=128)), 16)
        c.inc('G', nc.gpsimd.dma_start(out=Pst[:, :], in_=P_d), 16)
        c.inc('G', nc.gpsimd.dma_start(out=dia[:, :, :], in_=dia_d), 16)
        c.inc('G', nc.gpsimd.dma_start(out=bvr[:, :], in_=bvr_d), 16)
        c.inc('L', nc.sync.dma_start(out=bkc[:, :], in_=bkc_d), 16)
        c.inc('L', nc.sync.dma_start(out=idf[:, :], in_=idf_d), 16)
        nc.scalar.wait_ge(c.sems['L'], c.cnt['L'])
        i = nc.scalar.activation(bkcs[:, :], bkc[:, :], AF.Identity, scale=np.float32(CFAV))
        c.inc('A', i)

        xr = xT.rearrange("(k p) n -> p k n", p=128)
        for ch in range(NCH):
            sl = slice(ch * M, (ch + 1) * M)
            if ch:
                c.wait(nc.gpsimd, 'V', 'P')
                c.wait(nc.sync, 'V', 'P')
            emit_chunk_load(c, xb, None, M, xr=xr, sl=sl, need_f32=False)
            emit_ln(c, ln, [xb[:, t, :] for t in range(8)], M)
            c.wait(nc.vector, 'L')
            emit_normalize(c, ln, [xb[:, t, :] for t in range(8)],
                           [xn[:, t, :] for t in range(8)], M, tmpf)
            c.wait(nc.tensor, 'V', 'A')
            emit_proj_fm(c, WkT, xn, kbf, M, ps_big, bias_col=bkcs, scale=CFAV)
            for ms in range(4):
                msl = slice(ms * 128, ms * 128 + 128)
                for half in range(2):
                    psl = slice(half * 512, half * 512 + 512)
                    dsl = slice(half * 512, half * 512 + 512)
                    c.wait(nc.tensor, 'A')
                    for k in range(8):
                        nc.tensor.matmul(ps_big[:, psl], xn[:, k, msl], WvT[:, k, dsl],
                                         start=(k == 0), stop=False, skip_group_check=True)
                    i = nc.tensor.matmul(ps_big[:, psl], ones_row[:, :], bvr[:, dsl],
                                         start=False, stop=True, skip_group_check=True)
                    c.inc('P', i)
                    c.wait(nc.scalar, 'P')
                    i = nc.scalar.copy(vtm[:, ms, dsl], ps_big[:, psl])
                    c.inc('A', i)
            emit_diag(c, dia, sqk, kbf, diag_fm, diag_tm, idf, ps_trf, ps_trf, M)
            for ms in range(4):
                msl = slice(ms * 128, ms * 128 + 128)
                c.wait(nc.tensor, 'A', 'V')
                emit_favorp_head_mm(c, kbf, Pst, ps_big, msl)
                c.wait(nc.vector, 'P', 'A')
                nc.vector.reduce_max(km_t[:, :].unsqueeze(2),
                                     ps_big[:, :].rearrange("p (h j) -> p h j", h=16),
                                     axis=AXX)
                nc.vector.tensor_tensor(kmax_acc[:, :], kmax_acc[:, :], km_t[:, :], op=ALU.max)
                i = nc.vector.tensor_scalar(stab[:, :], diag_tm[:, ms, :], -0.5, 0.0,
                                            op0=ALU.mult, op1=ALU.add)
                c.inc('V', i)
                nc.vector.wait_ge(c.sems['V'], c.cnt['V'])
                i = nc.vector.tensor_tensor(
                    pre[:, :].rearrange("p (h j) -> p h j", h=16),
                    ps_big[:, :].rearrange("p (h j) -> p h j", h=16),
                    stab[:, :].unsqueeze(2).broadcast_to([128, 16, 64]), op=ALU.add)
                c.inc('V', i)
                c.wait(nc.scalar, 'V')
                i = nc.scalar.activation(keq[:, :, 0:64], pre[:, :].rearrange("p (h j) -> p h j", h=16), AF.Exp)
                c.inc('A', i)
                c.wait(nc.tensor, 'A')
                first = (ch == 0 and ms == 0)
                last = (ch == NCH - 1 and ms == 3)
                for h in range(NHEAD):
                    par = 64 * (h % 2)
                    pr = h // 2
                    nc.tensor.matmul(ps_kv[0:65, h * 64:(h + 1) * 64],
                                     keq[:, h, :],
                                     vtm[:, ms, h * 64:(h + 1) * 64],
                                     start=(first and h in (0, 8)), stop=(last and h in (7, 15)),
                                     skip_group_check=True)
                    i = nc.tensor.matmul(ps_ks[par:par + 64, pr:pr + 1],
                                         keq[:, h, 0:64], ones_col[:, :],
                                         start=(first and h == 0), stop=(last and h == 15),
                                         skip_group_check=True)
                c.inc('P', i)
        c.wait(nc.scalar, 'P')
        nc.scalar.copy(kv_ev[0:65, :], ps_kv[0:65, :])
        i = nc.scalar.copy(ks_ev[:, :], ps_ks[:, 0:8])
        c.inc('A', i)
        c.wait(nc.tensor, 'A')
        i = nc.tensor.transpose(ps_trf[0:8, 0:128], ks_ev[:, :], idf[:, :])
        c.inc('P', i)
        i = nc.tensor.transpose(ps_trf[0:16, 128:256], kmax_acc[:, :], idf[:, :])
        c.inc('P', i)
        c.wait(nc.scalar, 'P')
        nc.scalar.copy(ks_tr[:, :], ps_trf[0:8, 0:128])
        i = nc.scalar.copy(kmx_tr[:, :], ps_trf[0:16, 128:256])
        c.inc('A', i)
        c.wait(nc.vector, 'A')
        i = nc.vector.reduce_max(kmx_f[:, :], kmx_tr[:, :], axis=AXX)
        c.inc('V', i)
        c.wait(nc.sync, 'A', 'V', 'G')
        c.inc('S', nc.sync.dma_start(out=stats_d[0:64, :], in_=kv_ev[0:64, :]), 16)
        c.inc('S', nc.sync.dma_start(out=stats_d[64:72, 0:128], in_=ks_tr[:, :]), 16)
        c.inc('S', nc.sync.dma_start(out=stats_d[72:73, :], in_=kv_ev[64:65, :]), 16)
        c.inc('S', nc.sync.dma_start(out=stats_d[73:74, 0:16].transpose([1, 0]), in_=kmx_f[:, :]), 16)
        c.inc('S', nc.sync.dma_start(out=stats_d[64:72, 128:1024], in_=zerot[0:8, 0:896]), 16)
        c.inc('S', nc.sync.dma_start(out=stats_d[73:74, 16:1024], in_=zerot[0:1, 0:1008]), 16)
        c.inc('S', nc.sync.dma_start(out=stats_d[74:80, :], in_=zerot[0:6, :]), 16)
        c.wait(nc.sync, 'S')
        c.wait(nc.gpsimd, 'S')
    es.close()
    return nc


def build_L0A_apply(env=None, local_stats=False):
    nc, es, c, blk = begin(env, "pb_")
    M = 512
    xT = c.dram("xT", [D, ROWS], F32, "in")
    xhT = c.dram("xhT", [D, HDR], F32, "in")
    statsall_d = c.dram("statsall", [NCORE * STATS_ROWS, D], F32, "in")
    if not local_stats:
        w8_d = c.dram("w8", [128, 8], F32, "in")
        kmask_d = c.dram("kmask8", [8, 16], F32, "in")
    WqT_d = c.dram("WqT", [D, D], F32, "in")
    WoT_d = c.dram("WoT", [D, D], F32, "in")
    P_d = c.dram("Pstage", [128, D], F32, "in")
    bqc_d = c.dram("bqc", [128, 8], F32, "in")
    boc_d = c.dram("boc", [128, 8], F32, "in")
    dia_d = c.dram("diagones", [128, 8, 16], F32, "in")
    idf_d = c.dram("idf", [128, 128], F32, "in")
    idb_d = c.dram("idb", [128, 128], F32, "in")
    x1T_d = c.dram("x1T", [D, ROWS], F32, "out")
    x1hT_d = c.dram("x1hT", [D, HDR], F32, "out")

    f_dram = nc.dram_tensor(c.prefix + "f_dram", [1, 16], F32)
    vs_dram = nc.dram_tensor(c.prefix + "vs_dram", [1, D], F32)
    ks_dram = nc.dram_tensor(c.prefix + "ks_dram", [8, 128], F32)

    WqT = c.sb("WqT", [128, 8, D], BF16)
    WoT = c.sb("WoT", [128, 8, D], BF16)
    Pst = c.sb("Pst", [128, D], BF16)
    bqc = c.sb("bqc", [128, 8], F32)
    bqcs = c.sb("bqcs", [128, 8], F32)
    boc = c.sb("boc", [128, 8], F32)
    dia = c.sb("dia", [128, 8, 16], BF16)
    idf = c.sb("idf", [128, 128], F32)
    idb = c.sb("idb", [128, 128], BF16)
    w8 = c.sb("w8", [128, 8], F32)
    kmask = c.sb("kmask", [8, 16], F32)

    st1 = c.sb("st1", [80, D], F32)
    stsum = c.sb("stsum", [80, D], F32)
    km8 = c.sb("km8", [8, 16], F32)
    kmx_tr = c.sb("kmx_tr2", [16, 8], F32)
    stabk = c.sb("stabk", [16, 1], F32)
    fexp = c.sb("fexp", [16, 1], F32)
    kv_stage = c.sb("kv_stage", [128, 512], F32)
    f_pack = c.sb("f_pack", [128, 8], F32)
    vs_pack = c.sb("vs_pack", [128, 8, 64], F32)
    kv_sb = c.sb("kv_sb", [128, 8, 64], BF16)
    ks_t = c.sb("ks_t", [8, 128], F32)
    f_e = c.sb("f_e", [8, 1], F32)
    f_o = c.sb("f_o", [8, 1], F32)
    BDk = c.sb("BDk", [128, 8, 16], BF16)

    xb = c.sb("xb", [128, 8, M], BF16)
    xf = c.sb("xf", [128, 8, M], F32)
    xn = c.sb("xn", [128, 8, M], BF16)
    tmpf = c.sb("tmpf", [128, M], F32)
    qbf = c.sb("qbf", [128, 8, M], BF16)
    sqq = c.sb("sqq", [128, 8, M], BF16)
    diag_fm = c.sb("diag_fm", [16, M], F32)
    diag_tm = c.sb("diag_tm", [128, 4, 16], F32)
    stab = c.sb("stab", [128, 16], F32)
    pre = c.sb("pre", [128, D], F32)
    qp_tm = c.sb("qp_tm", [128, 4, D], BF16)
    qp_fm = c.sb("qp_fm", [128, 8, M], BF16)
    nrm = c.sb("nrm", [16, M], F32)
    rec_tm = c.sb("rec_tm", [128, 4, 16], F32)
    at_tm = c.sb("at_tm", [128, D], BF16)
    at_fm = c.sb("at_fm", [128, 8, M], BF16)
    raw = c.sb("raw", [128, 8, M], F32)
    rawb = c.sb("rawb", [128, 8, M], BF16)
    x1 = c.sb("x1", [128, 8, M], F32)

    ln = mk_ln(c, M)
    ps_aux = c.ps("ps_aux", [128, 512])
    ps_big = c.ps("ps_big", [128, 1024])
    ps_trb = c.ps("ps_trb", [128, 1024], BF16)
    ps_trf = c.ps("ps_trf", [128, 512])

    with blk:
        ln_init(c, ln)
        c.inc('G', nc.gpsimd.dma_start(out=WqT[:, :, :], in_=WqT_d.rearrange("(k p) n -> p k n", p=128)), 16)
        c.inc('G', nc.gpsimd.dma_start(out=WoT[:, :, :], in_=WoT_d.rearrange("(k p) n -> p k n", p=128)), 16)
        c.inc('G', nc.gpsimd.dma_start(out=Pst[:, :], in_=P_d), 16)
        c.inc('G', nc.gpsimd.dma_start(out=dia[:, :, :], in_=dia_d), 16)
        c.inc('G', nc.gpsimd.dma_start(out=idb[:, :], in_=idb_d), 16)
        c.inc('L', nc.sync.dma_start(out=idf[:, :], in_=idf_d), 16)
        c.inc('L', nc.sync.dma_start(out=bqc[:, :], in_=bqc_d), 16)
        c.inc('L', nc.sync.dma_start(out=boc[:, :], in_=boc_d), 16)
        if not local_stats:
            c.inc('L', nc.sync.dma_start(out=w8[:, :], in_=w8_d), 16)
            c.inc('L', nc.sync.dma_start(out=kmask[:, :], in_=kmask_d), 16)
        nc.scalar.wait_ge(c.sems['L'], c.cnt['L'])
        i = nc.scalar.activation(bqcs[:, :], bqc[:, :], AF.Identity, scale=np.float32(CFAV))
        c.inc('A', i)
        if local_stats:
            c.inc('L', nc.sync.dma_start(out=stsum[:, :], in_=statsall_d[0:STATS_ROWS, :]), 16)
            with nc.allow_non_contiguous_dma(reason="16-element kmax row"):
                c.inc('L', nc.sync.dma_start(out=stabk[:, :],
                                             in_=statsall_d[73:74, 0:16].transpose([1, 0])), 16)
            c.wait(nc.vector, 'L')
        else:
            sall = statsall_d.rearrange("(r s) n -> r s n", s=STATS_ROWS)
            for r in range(8):
                c.wait(nc.sync, 'V')
                c.inc('L', nc.sync.dma_start(out=st1[:, :], in_=sall[r:r + 1, :, :].squeeze(0)), 16)
                c.wait(nc.vector, 'L')
                if r == 0:
                    i = nc.vector.tensor_scalar(stsum[:, :], st1[:, :], w8[0:80, 0:1], 0.0,
                                                op0=ALU.mult, op1=ALU.add)
                else:
                    i = nc.vector.scalar_tensor_tensor(stsum[:, :], st1[:, :], w8[0:80, r:r + 1],
                                                       stsum[:, :], op0=ALU.mult, op1=ALU.add)
                c.inc('V', i)
            c.inc('L', nc.sync.dma_start(
                out=km8[:, :],
                in_=statsall_d.rearrange("(r s) n -> r s n", s=STATS_ROWS)[:, 73:74, 0:16].squeeze(1)), 16)
            c.wait(nc.vector, 'L')
            i = nc.vector.tensor_tensor(km8[:, :], km8[:, :], kmask[:, :], op=ALU.add)
            c.inc('V', i)
            c.wait(nc.tensor, 'V', 'L')
            i = nc.tensor.transpose(ps_trf[0:16, 0:8], km8[:, :], idf[0:8, 0:8])
            c.inc('P', i)
            c.wait(nc.scalar, 'P')
            i = nc.scalar.copy(kmx_tr[:, :], ps_trf[0:16, 0:8])
            c.inc('A', i)
            c.wait(nc.vector, 'A')
            i = nc.vector.reduce_max(stabk[:, :], kmx_tr[:, :], axis=AXX)
            c.inc('V', i)
        c.wait(nc.scalar, 'V', 'L')
        i = nc.scalar.activation(fexp[:, :], stabk[:, :], AF.Exp, scale=-1.0)
        c.inc('A', i)
        c.wait(nc.gpsimd, 'A', 'V')
        stk = stsum[0:64, :].rearrange("p (a q x) -> p a q x", a=8, q=2)
        c.inc('G', nc.gpsimd.dma_start(
            out=kv_stage[0:64, :].rearrange("p (a x) -> p a x", a=8),
            in_=stk[:, :, 0:1, :].squeeze(2)), 16)
        c.inc('G', nc.gpsimd.dma_start(
            out=kv_stage[64:128, :].rearrange("p (a x) -> p a x", a=8),
            in_=stk[:, :, 1:2, :].squeeze(2)), 16)
        c.inc('G', nc.gpsimd.dma_start(out=f_dram.ap().transpose([1, 0]), in_=fexp[:, :]), 16)
        c.inc('G', nc.gpsimd.dma_start(out=ks_dram[:, :], in_=stsum[64:72, 0:128]), 16)
        c.inc('G', nc.gpsimd.dma_start(out=vs_dram[:, :], in_=stsum[72:73, :]), 16)
        c.wait(nc.gpsimd, 'G')
        fap = f_dram.ap().squeeze(0).rearrange("(a p) -> a p", a=8)
        with nc.allow_non_contiguous_dma(reason="16-element parity unpack"):
            c.inc('G', nc.gpsimd.dma_start(
                out=f_pack[0:64, :].unsqueeze(2),
                in_=fap[:, 0:1].unsqueeze(0).partition_broadcast(64).squeeze(1)), 16)
            c.inc('G', nc.gpsimd.dma_start(
                out=f_pack[64:128, :].unsqueeze(2),
                in_=fap[:, 1:2].unsqueeze(0).partition_broadcast(64).squeeze(1)), 16)
            c.inc('G', nc.gpsimd.dma_start(out=f_e[:, :], in_=fap[:, 0:1]), 16)
            c.inc('G', nc.gpsimd.dma_start(out=f_o[:, :], in_=fap[:, 1:2]), 16)
        vap = vs_dram.ap().squeeze(0).rearrange("(a p x) -> a p x", a=8, p=2)
        c.inc('G', nc.gpsimd.dma_start(
            out=vs_pack[0:64, :, :], in_=vap[:, 0:1, :].squeeze(1).unsqueeze(0).partition_broadcast(64).squeeze(1)), 16)
        c.inc('G', nc.gpsimd.dma_start(
            out=vs_pack[64:128, :, :], in_=vap[:, 1:2, :].squeeze(1).unsqueeze(0).partition_broadcast(64).squeeze(1)), 16)
        c.inc('G', nc.gpsimd.dma_start(out=ks_t[:, :], in_=ks_dram[:, :]), 16)
        c.wait(nc.vector, 'G')
        nc.vector.tensor_scalar(vs_pack[:, :, :], vs_pack[:, :, :], float(EPS_FEAT), 0.0,
                                op0=ALU.mult, op1=ALU.add)
        nc.vector.tensor_tensor(
            kv_stage[:, :].rearrange("p (a x) -> p a x", a=8),
            kv_stage[:, :].rearrange("p (a x) -> p a x", a=8),
            f_pack[:, :].unsqueeze(2).broadcast_to([128, 8, 64]), op=ALU.mult)
        i = nc.vector.tensor_tensor(kv_sb[:, :, :],
                                    kv_stage[:, :].rearrange("p (a x) -> p a x", a=8),
                                    vs_pack[:, :, :], op=ALU.add)
        nc.vector.tensor_scalar(ks_t[:, 0:64], ks_t[:, 0:64], f_e[0:8, 0:1],
                                float(EPS_FEAT * 16384.0), op0=ALU.mult, op1=ALU.add)
        i = nc.vector.tensor_scalar(ks_t[:, 64:128], ks_t[:, 64:128], f_o[0:8, 0:1],
                                    float(EPS_FEAT * 16384.0), op0=ALU.mult, op1=ALU.add)
        c.inc('V', i)
        i = nc.gpsimd.memset(BDk[:, :, :], 0.0)
        c.inc('G', i)
        c.wait(nc.gpsimd, 'V')
        c.inc('G', nc.gpsimd.dma_start(out=ks_dram[:, :], in_=ks_t[:, :]), 16)
        c.wait(nc.gpsimd, 'G')
        for t in range(8):
            for par in range(2):
                i = nc.gpsimd.dma_start(
                    out=BDk[64 * par:64 * par + 64, t, 2 * t + par:2 * t + par + 1],
                    in_=ks_dram.ap()[t:t + 1, 64 * par:64 * par + 64].transpose([1, 0]))
                c.inc('G', i, 16)

        xr = xT.rearrange("(k p) n -> p k n", p=128)
        x1r = x1T_d.rearrange("(k p) n -> p k n", p=128)
        xhr = xhT.rearrange("(k p) n -> p k n", p=128)
        x1hr = x1hT_d.rearrange("(k p) n -> p k n", p=128)
        for ch in range(9):
            hdr = (ch == 8)
            m = 128 if hdr else M
            nms = m // 128
            sl = None if hdr else slice(ch * M, (ch + 1) * M)
            if ch:
                c.wait(nc.gpsimd, 'V')
                c.wait(nc.sync, 'V')
            emit_chunk_load(c, xb, xf, m, xr=xr, sl=sl, xhr=(xhr if hdr else None))
            emit_ln(c, ln, [xb[:, t, 0:m] for t in range(8)], m)
            c.wait(nc.vector, 'L')
            emit_normalize(c, ln, [xb[:, t, 0:m] for t in range(8)],
                           [xn[:, t, 0:m] for t in range(8)], m, tmpf)
            c.wait(nc.tensor, 'V', 'A')
            emit_proj_fm(c, WqT, xn, qbf, m, ps_big, bias_col=bqcs, scale=CFAV)
            emit_diag(c, dia, sqq, qbf, diag_fm, diag_tm, idf, ps_trf, ps_trf, m)
            for ms in range(nms):
                msl = slice(ms * 128, ms * 128 + 128)
                c.wait(nc.tensor, 'A', 'V')
                emit_favorp_head_mm(c, qbf, Pst, ps_big, msl)
                c.wait(nc.vector, 'P', 'A')
                nc.vector.reduce_max(stab[:, :].unsqueeze(2),
                                     ps_big[:, :].rearrange("p (h j) -> p h j", h=16),
                                     axis=AXX)
                i = nc.vector.scalar_tensor_tensor(stab[:, :], diag_tm[:, ms, :], -0.5,
                                                   stab[:, :], op0=ALU.mult, op1=ALU.subtract)
                c.inc('V', i)
                nc.vector.wait_ge(c.sems['V'], c.cnt['V'])
                i = nc.vector.tensor_tensor(
                    pre[:, :].rearrange("p (h j) -> p h j", h=16),
                    ps_big[:, :].rearrange("p (h j) -> p h j", h=16),
                    stab[:, :].unsqueeze(2).broadcast_to([128, 16, 64]), op=ALU.add)
                c.inc('V', i)
                c.wait(nc.scalar, 'V')
                i = nc.scalar.activation(pre[:, :], pre[:, :], AF.Exp)
                c.inc('A', i)
                c.wait(nc.vector, 'A')
                i = nc.vector.tensor_scalar(qp_tm[:, ms, :], pre[:, :], 1.0, float(EPS_FEAT),
                                            op0=ALU.mult, op1=ALU.add)
                c.inc('V', i)
                c.wait(nc.tensor, 'V')
                for t in range(8):
                    i = nc.tensor.transpose(ps_trb[:, t * 128:(t + 1) * 128],
                                            qp_tm[:, ms, t * 128:(t + 1) * 128], idb[:, :])
                c.inc('P', i)
                c.wait(nc.vector, 'P')
                for t in range(8):
                    i = nc.vector.tensor_copy(qp_fm[:, t, msl], ps_trb[:, t * 128:(t + 1) * 128])
                c.inc('V', i)
            c.wait(nc.tensor, 'V')
            for t in range(8):
                i = nc.tensor.matmul(ps_aux[32:48, 0:m], BDk[:, t, :], qp_fm[:, t, 0:m],
                                     start=(t == 0), stop=(t == 7), skip_group_check=True)
            c.inc('P', i)
            c.wait(nc.scalar, 'P')
            i = nc.scalar.copy(nrm[:, 0:m], ps_aux[32:48, 0:m])
            c.inc('A', i)
            c.wait(nc.vector, 'A')
            nc.vector.tensor_scalar_max(nrm[:, 0:m], nrm[:, 0:m], float(EPS_DEN))
            i = nc.vector.reciprocal(nrm[:, 0:m], nrm[:, 0:m])
            c.inc('V', i)
            c.wait(nc.tensor, 'V')
            for ms in range(nms):
                i = nc.tensor.transpose(ps_trf[:, 256 + ms * 16:256 + ms * 16 + 16],
                                        nrm[:, ms * 128:(ms + 1) * 128], idf[0:16, 0:16])
                c.inc('P', i)
            c.wait(nc.scalar, 'P')
            for ms in range(nms):
                i = nc.scalar.copy(rec_tm[:, ms, :], ps_trf[:, 256 + ms * 16:256 + ms * 16 + 16])
            c.inc('A', i)
            for ms in range(nms):
                msl = slice(ms * 128, ms * 128 + 128)
                c.wait(nc.tensor, 'A', 'V')
                for par_pass in range(2):
                    for h in range(par_pass, NHEAD, 2):
                        par = 64 * (h % 2)
                        i = nc.tensor.matmul(ps_big[:, h * 64:(h + 1) * 64],
                                             qp_fm[par:par + 64, h // 2, msl],
                                             kv_sb[par:par + 64, h // 2, :],
                                             start=(h in (0, 8)), stop=(h in (7, 15)),
                                             skip_group_check=True)
                    if par_pass == 0:
                        c.inc('P', i)
                        nc.tensor.wait_ge(c.sems['P'], c.cnt['P'])
                c.inc('P', i)
                c.wait(nc.vector, 'P', 'A')
                i = nc.vector.tensor_tensor(
                    at_tm[:, :].rearrange("p (h j) -> p h j", h=16),
                    ps_big[:, :].rearrange("p (h j) -> p h j", h=16),
                    rec_tm[:, ms, :].unsqueeze(2).broadcast_to([128, 16, 64]), op=ALU.mult)
                c.inc('V', i)
                c.wait(nc.tensor, 'V')
                for t in range(8):
                    i = nc.tensor.transpose(ps_trb[:, t * 128:(t + 1) * 128],
                                            at_tm[:, t * 128:(t + 1) * 128], idb[:, :])
                c.inc('P', i)
                c.wait(nc.vector, 'P')
                for t in range(8):
                    i = nc.vector.tensor_copy(at_fm[:, t, msl], ps_trb[:, t * 128:(t + 1) * 128])
                c.inc('V', i)
            c.wait(nc.tensor, 'V')
            emit_proj_fm2(c, WoT, at_fm, raw, rawb, m, ps_big, bias_col=boc)
            nc.vector.wait_ge(c.sems['L'], c.cnt['L'])
            c.wait(nc.scalar, 'V')
            emit_ln(c, ln, [rawb[:, t, 0:m] for t in range(8)], m)
            c.wait(nc.vector, 'L')
            emit_normalize(c, ln, [raw[:, t, 0:m] for t in range(8)], None, m, tmpf,
                           add_tiles=[xf[:, t, 0:m] for t in range(8)],
                           out3=[x1[:, t, 0:m] for t in range(8)])
            c.wait(nc.sync, 'V')
            if hdr:
                c.inc('S', nc.sync.dma_start(out=x1hr, in_=x1[:, :, 0:HDR]), 16)
            else:
                c.inc('S', nc.sync.dma_start(out=x1r[:, :, sl], in_=x1[:, :, :]), 16)
        c.wait(nc.sync, 'S')
        c.wait(nc.gpsimd, 'S')
    es.close()
    return nc


def build_L0F(env=None):
    nc, es, c, blk = begin(env, "pc_")
    M = 256
    H = 3072
    HT = H // 128
    xT = c.dram("xT", [D, ROWS], F32, "in")
    xhT = c.dram("xhT", [D, HDR], F32, "in")
    wgT_d = c.dram("wgT", [D, H], I32, "in")
    wuT_d = c.dram("wuT", [D, H], I32, "in")
    wdT_d = c.dram("wdT", [H, D], I32, "in")
    x2T_d = c.dram("x2T", [D, ROWS], F32, "out")
    x2hT_d = c.dram("x2hT", [D, HDR], F32, "out")

    wg = c.sb("wg", [128, 8, H], BF16)
    wu = c.sb("wu", [128, 8, H], BF16)
    wd = c.sb("wd", [128, HT, D], BF16)
    xb = c.sb("xb", [128, 8, M], BF16)
    xf = c.sb("xf", [128, 8, M], F32)
    xn = c.sb("xn", [128, 8, M], BF16)
    tmpf = c.sb("tmpf", [128, M], F32)
    sil = c.sb("sil", [128, 2, M], F32)
    hid = c.sb("hid", [128, HT, M], BF16)
    raw = c.sb("raw", [128, 8, M], F32)
    rawb = c.sb("rawb", [128, 8, M], BF16)
    ln = mk_ln(c, M)
    ln2 = dict(ln, bc=c.sb("ln2_bc", [128, 2, M], F32),
               ps_stat=c.ps("ln2_psstat", [128, 512], F32))
    ps_g = [c.ps("ps_g0", [128, 512]), c.ps("ps_g1", [128, 512])]
    ps_u = [c.ps("ps_u0", [128, 512]), c.ps("ps_u1", [128, 512])]
    ps_d = [c.ps("ps_d0", [128, 512]), c.ps("ps_d1", [128, 512])]

    def gate_up(m):
        vhist = []
        for ht in range(HT):
            hsl = slice(ht * 128, (ht + 1) * 128)
            pg, pu = ps_g[ht % 2], ps_u[ht % 2]
            if ht >= 2:
                nc.tensor.wait_ge(c.sems['V'], vhist[ht - 2])
            for k in range(8):
                nc.tensor.matmul(pg[:, 0:m], wg[:, k, hsl], xn[:, k, 0:m],
                                 start=(k == 0), stop=(k == 7), skip_group_check=True)
            for k in range(8):
                i = nc.tensor.matmul(pu[:, 0:m], wu[:, k, hsl], xn[:, k, 0:m],
                                     start=(k == 0), stop=(k == 7), skip_group_check=True)
            c.inc('P', i)
            c.wait(nc.scalar, 'P')
            i = nc.scalar.activation(sil[:, ht % 2, 0:m], pg[:, 0:m], AF.Silu,
                                     scale=np.float32(SCALE_FFN0))
            c.inc('A', i)
            c.wait(nc.vector, 'A')
            i = nc.vector.scalar_tensor_tensor(hid[:, ht, 0:m], pu[:, 0:m], SCALE_FFN0,
                                               sil[:, ht % 2, 0:m], op0=ALU.mult, op1=ALU.mult)
            c.inc('V', i)
            vhist.append(c.cnt['V'])

    def down(m):
        for t in range(8):
            dsl = slice(t * 128, (t + 1) * 128)
            pd = ps_d[t % 2]
            if t >= 2:
                c.wait(nc.tensor, 'A', 'V')
            for ht in range(HT):
                i = nc.tensor.matmul(pd[:, 0:m], wd[:, ht, dsl], hid[:, ht, 0:m],
                                     start=(ht == 0), stop=(ht == HT - 1), skip_group_check=True)
            c.inc('P', i)
            c.wait(nc.scalar, 'P')
            i = nc.scalar.activation(raw[:, t, 0:m], pd[:, 0:m], AF.Identity,
                                     scale=np.float32(SCALE_FFN0))
            c.inc('A', i)
            c.wait(nc.vector, 'A')
            i = nc.vector.tensor_copy(rawb[:, t, 0:m], raw[:, t, 0:m])
            c.inc('V', i)

    with blk:
        ln_init(c, ln)
        c.inc('G', nc.gpsimd.dma_start(out=wg[:, :, :], in_=wgT_d.rearrange("(k p) h -> p k h", p=128)), 16)
        c.inc('G', nc.gpsimd.dma_start(out=wu[:, :, :], in_=wuT_d.rearrange("(k p) h -> p k h", p=128)), 16)
        c.inc('G', nc.gpsimd.dma_start(out=wd[:, :, :], in_=wdT_d.rearrange("(k p) h -> p k h", p=128)), 16)

        xr = xT.rearrange("(k p) n -> p k n", p=128)
        x2r = x2T_d.rearrange("(k p) n -> p k n", p=128)
        xhr = xhT.rearrange("(k p) n -> p k n", p=128)
        x2hr = x2hT_d.rearrange("(k p) n -> p k n", p=128)
        NCH = 16
        c.inc('G', nc.gpsimd.dma_start(out=xb[:, :, :], in_=xr[:, :, 0:M]), 16)
        c.inc('L', nc.sync.dma_start(out=xf[:, :, :], in_=xr[:, :, 0:M]), 16)
        nc.scalar.wait_ge(c.sems['G'], c.cnt['G'])
        nc.tensor.wait_ge(c.sems['G'], c.cnt['G'])
        nc.vector.wait_ge(c.sems['G'], c.cnt['G'])
        emit_ln(c, ln, [xb[:, t, :] for t in range(8)], M)
        for ch in range(NCH):
            sl = slice(ch * M, (ch + 1) * M)
            c.wait(nc.vector, 'L')
            emit_normalize(c, ln, [xb[:, t, :] for t in range(8)],
                           [xn[:, t, :] for t in range(8)], M, tmpf)
            v_norm = c.cnt['V']
            c.wait(nc.tensor, 'V')
            gate_up(M)
            if ch + 1 < NCH:
                nsl = slice((ch + 1) * M, (ch + 2) * M)
                nc.gpsimd.wait_ge(c.sems['V'], v_norm)
                c.inc('G', nc.gpsimd.dma_start(out=xb[:, :, :], in_=xr[:, :, nsl]), 16)
                nc.scalar.wait_ge(c.sems['G'], c.cnt['G'])
                nc.tensor.wait_ge(c.sems['G'], c.cnt['G'])
                nc.vector.wait_ge(c.sems['G'], c.cnt['G'])
                emit_ln(c, ln, [xb[:, t, :] for t in range(8)], M)
            down(M)
            c.wait(nc.scalar, 'V', 'L')
            emit_ln(c, ln2, [rawb[:, t, :] for t in range(8)], M)
            c.wait(nc.vector, 'L')
            emit_normalize(c, ln2, [raw[:, t, :] for t in range(8)], None, M, tmpf,
                           add_tiles=[xf[:, t, :] for t in range(8)],
                           out3=[xf[:, t, :] for t in range(8)])
            c.wait(nc.sync, 'V')
            c.inc('S', nc.sync.dma_start(out=x2r[:, :, sl], in_=xf[:, :, :]), 16)
            if ch + 1 < NCH:
                c.wait(nc.sync, 'S')
                c.inc('L', nc.sync.dma_start(out=xf[:, :, :], in_=xr[:, :, nsl]), 16)
        m = HDR * 8
        c.wait(nc.sync, 'V', 'S')
        c.wait(nc.gpsimd, 'V', 'S')
        emit_chunk_load(c, xb, xf, m, xhr=xhr)
        emit_ln(c, ln, [xb[:, t, 0:m] for t in range(8)], m)
        c.wait(nc.vector, 'L')
        emit_normalize(c, ln, [xb[:, t, 0:m] for t in range(8)],
                       [xn[:, t, 0:m] for t in range(8)], m, tmpf)
        c.wait(nc.tensor, 'V')
        gate_up(m)
        down(m)
        c.wait(nc.scalar, 'V', 'L')
        emit_ln(c, ln2, [rawb[:, t, 0:m] for t in range(8)], m)
        c.wait(nc.vector, 'L')
        emit_normalize(c, ln2, [raw[:, t, 0:m] for t in range(8)], None, m, tmpf,
                       add_tiles=[xf[:, t, 0:m] for t in range(8)],
                       out3=[xf[:, t, 0:m] for t in range(8)])
        c.wait(nc.sync, 'V')
        c.inc('S', nc.sync.dma_start(out=x2hr, in_=xf[:, :, 0:HDR]), 16)
        c.wait(nc.sync, 'S')
        c.wait(nc.gpsimd, 'S')
    es.close()
    return nc


def build_L1A(env=None):
    nc, es, c, blk = begin(env, "pd_")
    M = 512
    xT = c.dram("xT", [D, ROWS], F32, "in")
    xhT = c.dram("xhT", [D, HDR], F32, "in")
    WaqT_d = c.dram("WaqT", [D, D], F32, "in")
    WakvT_d = c.dram("WakvT", [D, 2 * D], F32, "in")
    WpT_d = c.dram("WpT", [D, D], F32, "in")
    mask_d = c.dram("l1mask", [128, 256], F32, "in")
    idb_d = c.dram("idb", [128, 128], F32, "in")
    x3T_d = c.dram("x3T", [D, ROWS], F32, "out")
    vh_dram = nc.dram_tensor(c.prefix + "vh_dram", [HDR, D], F32)

    Waq = c.sb("Waq", [128, 8, D], BF16)
    WpT = c.sb("WpT", [128, 8, D], BF16)
    mask = c.sb("mask", [128, 256], F32)
    idb = c.sb("idb", [128, 128], BF16)
    khdr = c.sb("khdr", [128, 8, HDR], BF16)
    vblk = c.sb("vblk", [128, 2, 512], BF16)
    xb = c.sb("xb", [128, 8, M], BF16)
    xf = c.sb("xf", [128, 8, M], F32)
    xn = c.sb("xn", [128, 8, M], BF16)
    tmpf = c.sb("tmpf", [128, M], F32)
    qfm = c.sb("qfm", [128, 8, M], BF16)
    lg = c.sb("lg", [128, 256], F32)
    mx = c.sb("mx", [128, 16], F32)
    we = c.sb("we", [128, 256], BF16)
    sm = c.sb("sm", [128, 16], F32)
    wn = c.sb("wn", [128, 256], BF16)
    wfm = c.sb("wfm", [128, 2, 128], BF16)
    ofm = c.sb("ofm", [128, 8, M], BF16)
    raw = c.sb("raw", [128, 8, M], F32)
    rawb = c.sb("rawb", [128, 8, M], BF16)
    x3 = c.sb("x3", [128, 8, M], F32)
    ln = mk_ln(c, M)
    ps_big = c.ps("ps_big", [128, 1024])
    ps_lg = c.ps("ps_lg", [128, 512])
    ps_trb = c.ps("ps_trb", [128, 1024], BF16)
    ps_o = c.ps("ps_o", [128, 1024])

    with blk:
        ln_init(c, ln)
        c.inc('G', nc.gpsimd.dma_start(out=Waq[:, :, :], in_=WaqT_d.rearrange("(k p) n -> p k n", p=128)), 16)
        c.inc('G', nc.gpsimd.dma_start(out=WpT[:, :, :], in_=WpT_d.rearrange("(k p) n -> p k n", p=128)), 16)
        c.inc('G', nc.gpsimd.dma_start(out=idb[:, :], in_=idb_d), 16)
        c.inc('L', nc.sync.dma_start(out=mask[:, :], in_=mask_d), 16)

        xr = xT.rearrange("(k p) n -> p k n", p=128)
        x3r = x3T_d.rearrange("(k p) n -> p k n", p=128)
        xhr = xhT.rearrange("(k p) n -> p k n", p=128)

        with nc.sbuf_tensor(c.prefix + "Wakv", [128, 8, 2 * D], BF16) as Wakv:
            c.inc('G', nc.gpsimd.dma_start(out=Wakv[:, :, :],
                                           in_=WakvT_d.rearrange("(k p) n -> p k n", p=128)), 16)
            emit_chunk_load(c, xb, xf, 128, xhr=xhr)
            emit_ln(c, ln, [xb[:, t, 0:128] for t in range(8)], 128)
            c.wait(nc.vector, 'L')
            emit_normalize(c, ln, [xb[:, t, 0:128] for t in range(8)],
                           [xn[:, t, 0:128] for t in range(8)], 128, tmpf)
            c.wait(nc.tensor, 'V')
            for t in range(8):
                for k in range(8):
                    i = nc.tensor.matmul(ps_big[:, t * 64:t * 64 + HDR],
                                         Wakv[:, k, t * 128:(t + 1) * 128],
                                         xn[:, k, 0:HDR], start=(k == 0), stop=(k == 7),
                                         skip_group_check=True)
                c.inc('P', i)
            c.wait(nc.scalar, 'P')
            for t in range(8):
                i = nc.scalar.copy(khdr[:, t, :], ps_big[:, t * 64:t * 64 + HDR])
            c.inc('A', i)
            c.wait(nc.tensor, 'A')
            for half in range(2):
                for k in range(8):
                    i = nc.tensor.matmul(ps_lg[0:HDR, :], xn[:, k, 0:HDR],
                                         Wakv[:, k, D + half * 512: D + half * 512 + 512],
                                         start=(k == 0), stop=(k == 7), skip_group_check=True)
                c.inc('P', i)
                c.wait(nc.scalar, 'P')
                i = nc.scalar.copy(tmpf[0:HDR, 0:512], ps_lg[0:HDR, :])
                c.inc('A', i)
                c.wait(nc.sync, 'A')
                c.inc('S', nc.sync.dma_start(out=vh_dram[:, half * 512:half * 512 + 512],
                                             in_=tmpf[0:HDR, 0:512]), 16)
                nc.tensor.wait_ge(c.sems['S'], c.cnt['S'])
            i = nc.gpsimd.memset(vblk[:, :, :], 0.0)
            c.inc('G', i)
            c.wait(nc.gpsimd, 'S')
            vap = vh_dram.ap().rearrange("t (h x) -> t h x", h=16)
            for g in range(2):
                for hp in range(8):
                    i = nc.gpsimd.dma_start(
                        out=vblk[16 * hp:16 * hp + 16, g, 64 * hp:64 * hp + 64],
                        in_=vap[:, 8 * g + hp, :])
                    c.inc('G', i, 16)

        for ch in range(8):
            sl = slice(ch * M, (ch + 1) * M)
            c.wait(nc.gpsimd, 'V')
            c.wait(nc.sync, 'V')
            emit_chunk_load(c, xb, xf, M, xr=xr, sl=sl)
            emit_ln(c, ln, [xb[:, t, :] for t in range(8)], M)
            c.wait(nc.vector, 'L')
            emit_normalize(c, ln, [xb[:, t, :] for t in range(8)],
                           [xn[:, t, :] for t in range(8)], M, tmpf)
            c.wait(nc.tensor, 'V', 'A')
            emit_proj_fm(c, Waq, xn, qfm, M, ps_big)
            for ms in range(4):
                msl = slice(ms * 128, ms * 128 + 128)
                c.wait(nc.tensor, 'A', 'V')
                for par_pass in range(2):
                    for h in range(par_pass, NHEAD, 2):
                        par = 64 * (h % 2)
                        i = nc.tensor.matmul(ps_lg[:, h * 16:(h + 1) * 16],
                                             qfm[par:par + 64, h // 2, msl],
                                             khdr[par:par + 64, h // 2, :],
                                             start=(h == 0), stop=(h == 15),
                                             skip_group_check=True)
                    if par_pass == 0:
                        c.inc('P', i)
                        nc.tensor.wait_ge(c.sems['P'], c.cnt['P'])
                c.inc('P', i)
                c.wait(nc.vector, 'P')
                if ch == 0 and ms == 0:
                    i = nc.vector.tensor_tensor(lg[:, :], ps_lg[:, 0:256], mask[:, :], op=ALU.add)
                else:
                    i = nc.vector.tensor_copy(lg[:, :], ps_lg[:, 0:256])
                c.inc('V', i)
                i = nc.vector.reduce_max(mx[:, :].unsqueeze(2),
                                         lg[:, :].rearrange("p (h j) -> p h j", h=16), axis=AXX)
                c.inc('V', i)
                nc.vector.wait_ge(c.sems['V'], c.cnt['V'])
                i = nc.vector.tensor_tensor(
                    lg[:, :].rearrange("p (h j) -> p h j", h=16),
                    lg[:, :].rearrange("p (h j) -> p h j", h=16),
                    mx[:, :].unsqueeze(2).broadcast_to([128, 16, 16]), op=ALU.subtract)
                c.inc('V', i)
                c.wait(nc.scalar, 'V')
                i = nc.scalar.activation(we[:, :], lg[:, :], AF.Exp, scale=0.125)
                c.inc('A', i)
                c.wait(nc.vector, 'A')
                i = nc.vector.reduce_sum(sm[:, :].unsqueeze(2),
                                         we[:, :].rearrange("p (h j) -> p h j", h=16), axis=AXX)
                c.inc('V', i)
                nc.vector.wait_ge(c.sems['V'], c.cnt['V'])
                i = nc.vector.reciprocal(sm[:, :], sm[:, :])
                c.inc('V', i)
                nc.vector.wait_ge(c.sems['V'], c.cnt['V'])
                i = nc.vector.tensor_tensor(
                    wn[:, :].rearrange("p (h j) -> p h j", h=16),
                    we[:, :].rearrange("p (h j) -> p h j", h=16),
                    sm[:, :].unsqueeze(2).broadcast_to([128, 16, 16]), op=ALU.mult)
                c.inc('V', i)
                c.wait(nc.tensor, 'V')
                for g in range(2):
                    i = nc.tensor.transpose(ps_trb[:, g * 128:(g + 1) * 128],
                                            wn[:, g * 128:(g + 1) * 128], idb[:, :])
                c.inc('P', i)
                c.wait(nc.vector, 'P')
                for g in range(2):
                    i = nc.vector.tensor_copy(wfm[:, g, :], ps_trb[:, g * 128:(g + 1) * 128])
                c.inc('V', i)
                c.wait(nc.tensor, 'V')
                for t in range(8):
                    g = t // 4
                    i = nc.tensor.matmul(ps_o[:, t * 128:(t + 1) * 128],
                                         vblk[:, g, (t % 4) * 128:(t % 4) * 128 + 128],
                                         wfm[:, g, :],
                                         start=(t in (0, 4)), stop=(t in (3, 7)),
                                         skip_group_check=True)
                c.inc('P', i)
                c.wait(nc.scalar, 'P')
                for t in range(8):
                    i = nc.scalar.copy(ofm[:, t, msl], ps_o[:, t * 128:(t + 1) * 128])
                c.inc('A', i)
                nc.tensor.wait_ge(c.sems['A'], c.cnt['A'])
            c.wait(nc.tensor, 'A')
            emit_proj_fm2(c, WpT, ofm, raw, rawb, M, ps_big)
            nc.vector.wait_ge(c.sems['L'], c.cnt['L'])
            c.wait(nc.scalar, 'V')
            emit_ln(c, ln, [rawb[:, t, :] for t in range(8)], M)
            c.wait(nc.vector, 'L')
            emit_normalize(c, ln, [raw[:, t, :] for t in range(8)], None, M, tmpf,
                           add_tiles=[xf[:, t, :] for t in range(8)],
                           out3=[x3[:, t, :] for t in range(8)])
            c.wait(nc.sync, 'V')
            c.inc('S', nc.sync.dma_start(out=x3r[:, :, sl], in_=x3[:, :, :]), 16)
        c.wait(nc.sync, 'S')
        c.wait(nc.gpsimd, 'S')
    es.close()
    return nc


def build_L1F(env=None):
    nc, es, c, blk = begin(env, "pe_")
    M = 256
    H = 4096
    HT = H // 128
    xT = c.dram("xT", [D, ROWS], F32, "in")
    w1T_d = c.dram("w1T", [D, H], F32, "in")
    w2T_d = c.dram("w2T", [H, D], F32, "in")
    b1c_d = c.dram("b1c", [128, 32], F32, "in")
    b2c_d = c.dram("b2c", [128, 8], F32, "in")
    idf_d = c.dram("idf", [128, 128], F32, "in")
    y_d = c.dram("y", [ROWS, D], F32, "out")
    a1_dram = nc.dram_tensor(c.prefix + "a1_dram", [32, 128], F32)
    a2_dram = nc.dram_tensor(c.prefix + "a2_dram", [8, 128], F32)

    w1s = c.sb("w1s", [128, 8, H], BF16)
    w2s = c.sb("w2s", [128, HT, D], BF16)
    b1c = c.sb("b1c", [128, 32], F32)
    b2c = c.sb("b2c", [128, 8], F32)
    a1c = c.sb("a1c", [128, 32], F32)
    a2c = c.sb("a2c", [128, 8], F32)
    idf = c.sb("idf", [128, 128], F32)
    ones_col = c.sb("ones_colc", [128, 1], BF16)
    stage = c.sb("stage", [1, 512], F32)
    stage8 = c.sb("stage8", [8, 512], F32)
    sel8 = c.sb("sel8", [128, 8, 8], BF16)
    a1t = c.sb("a1t", [32, 128], F32)
    a2t = c.sb("a2t", [8, 128], F32)
    xb = c.sb("xb", [128, 8, M], BF16)
    xf = c.sb("xf", [128, 8, M], F32)
    xn = c.sb("xn", [128, 8, M], BF16)
    tmpf = c.sb("tmpf", [128, M], F32)
    hid = c.sb("hid", [128, HT, M], BF16)
    raw = c.sb("raw", [128, 8, M], F32)
    rawb = c.sb("rawb", [128, 8, M], BF16)
    x4 = c.sb("x4", [128, 8, M], F32)
    ytm = c.sb("ytm", [128, D], F32)
    ln = mk_ln(c, M)
    ps_h = [c.ps("ps_h0", [128, 512]), c.ps("ps_h1", [128, 512])]
    ps_d = [c.ps("ps_d0", [128, 512]), c.ps("ps_d1", [128, 512])]
    ps_tf = c.ps("ps_tf", [128, 512])

    with blk:
        ln_init(c, ln)
        i = nc.gpsimd.memset(ones_col[:, :], 2.0 ** -10)
        c.inc('G', i)
        c.inc('L', nc.sync.dma_start(out=b1c[:, :], in_=b1c_d), 16)
        c.inc('L', nc.sync.dma_start(out=b2c[:, :], in_=b2c_d), 16)
        c.inc('L', nc.sync.dma_start(out=idf[:, :], in_=idf_d), 16)
        wst = [raw[:, :, :].rearrange("p a b -> p (a b)"),
               x4[:, :, :].rearrange("p a b -> p (a b)")]
        wab = [hid[:, 0:8, :].rearrange("p a b -> p (a b)"),
               hid[:, 8:16, :].rearrange("p a b -> p (a b)")]
        nc.gpsimd.memset(sel8[:, :, :], 0.0)
        for s in range(8):
            i = nc.gpsimd.memset(sel8[:, s, s:s + 1], 2.0 ** -10)
        c.inc('G', i)
        w1r = w1T_d.rearrange("(k p) h -> p k h", p=128)
        w2r = w2T_d.rearrange("(k p) h -> p k h", p=128)
        ahist, vhist, phist = [], [], []
        qsn = []
        for kk in range(16 + HT):
            w1p = kk < 16
            if w1p:
                k, half = kk // 2, kk % 2
                hs = slice(half * 2048, half * 2048 + 2048)
                wsl = slice(0, 2048)
            else:
                k = kk - 16
                wsl = slice(0, D)
            b = kk % 2
            deng, dsem = (nc.gpsimd, 'G') if b == 0 else (nc.sync, 'L')
            if kk >= 2:
                deng.wait_ge(c.sems['A'], ahist[kk - 2])
                deng.wait_ge(c.sems['V'], vhist[kk - 2])
            src = w1r[:, k, hs] if w1p else w2r[:, k, :]
            c.inc(dsem, deng.dma_start(out=wst[b][:, wsl], in_=src), 16)
            qsn.append((dsem, c.cnt[dsem]))
            nc.scalar.wait_ge(c.sems[dsem], c.cnt[dsem])
            dst = w1s[:, k, hs] if w1p else w2s[:, k, :]
            i = nc.scalar.activation(dst, wst[b][:, wsl], AF.Sign)
            c.inc('A', i)
            ahist.append(c.cnt['A'])
            nc.vector.wait_ge(c.sems[dsem], qsn[kk][1])
            if kk >= 2:
                nc.vector.wait_ge(c.sems['P'], phist[kk - 2])
            i = nc.vector.scalar_tensor_tensor(wab[b][:, wsl], wst[b][:, wsl], -1.0,
                                               wst[b][:, wsl], op0=ALU.mult, op1=ALU.max)
            c.inc('V', i)
            vhist.append(c.cnt['V'])
            nc.tensor.wait_ge(c.sems['V'], vhist[kk])
            if w1p:
                for q in range(4):
                    s = half * 4 + q
                    i = nc.tensor.matmul(ps_h[0][0:8, :], sel8[:, s, :],
                                         wab[b][:, q * 512:(q + 1) * 512],
                                         start=(kk == 0 and q == 0),
                                         stop=(kk == 15 and q == 3), skip_group_check=True)
            else:
                nc.tensor.matmul(ps_d[0][0:1, :], ones_col[:, :], wab[b][:, 0:512],
                                 start=(k == 0), stop=(k == HT - 1), skip_group_check=True)
                i = nc.tensor.matmul(ps_d[0][64:65, :], ones_col[:, :], wab[b][:, 512:1024],
                                     start=(k == 0), stop=(k == HT - 1), skip_group_check=True)
            c.inc('P', i)
            phist.append(c.cnt['P'])
            if kk == 15:
                c.wait(nc.scalar, 'P')
                i = nc.scalar.copy(stage8[:, :], ps_h[0][0:8, :])
                c.inc('A', i)
                ahist[-1] = c.cnt['A']
                c.wait(nc.sync, 'A')
                c.inc('S', nc.sync.dma_start(
                    out=a1_dram.ap().rearrange("(s f) cc -> s f cc", s=8),
                    in_=stage8[:, :].rearrange("p (f cc) -> p f cc", f=4)), 16)
        c.wait(nc.scalar, 'P')
        i = nc.scalar.activation(stage[:, :], ps_d[0][0:1, :], AF.Identity, scale=0.25)
        c.inc('A', i)
        c.wait(nc.gpsimd, 'A')
        c.inc('G', nc.gpsimd.dma_start(out=a2_dram[0:4, :].unsqueeze(0),
                                       in_=stage[:, :].rearrange("o (a b) -> o a b", a=4)), 16)
        nc.scalar.wait_ge(c.sems['G'], c.cnt['G'])
        i = nc.scalar.activation(stage[:, :], ps_d[0][64:65, :], AF.Identity, scale=0.25)
        c.inc('A', i)
        c.wait(nc.gpsimd, 'A')
        c.inc('G', nc.gpsimd.dma_start(out=a2_dram[4:8, :].unsqueeze(0),
                                       in_=stage[:, :].rearrange("o (a b) -> o a b", a=4)), 16)
        c.wait(nc.gpsimd, 'G', 'S')
        c.inc('G', nc.gpsimd.dma_start(out=a1t[:, :], in_=a1_dram.ap()), 16)
        c.inc('G', nc.gpsimd.dma_start(out=a2t[:, :], in_=a2_dram.ap()), 16)
        c.wait(nc.tensor, 'G')
        nc.tensor.transpose(ps_tf[:, 0:32], a1t[:, :], idf[0:32, 0:32])
        i = nc.tensor.transpose(ps_tf[:, 32:40], a2t[:, :], idf[0:8, 0:8])
        c.inc('P', i)
        c.wait(nc.scalar, 'P')
        nc.scalar.copy(a1c[:, :], ps_tf[:, 0:32])
        i = nc.scalar.copy(a2c[:, :], ps_tf[:, 32:40])
        c.inc('A', i)

        xr = xT.rearrange("(k p) n -> p k n", p=128)
        for ch in range(16):
            sl = slice(ch * M, (ch + 1) * M)
            c.wait(nc.sync, 'V')
            c.wait(nc.gpsimd, 'V')
            emit_chunk_load(c, xb, xf, M, xr=xr, sl=sl)
            emit_ln(c, ln, [xb[:, t, :] for t in range(8)], M)
            c.wait(nc.vector, 'L')
            emit_normalize(c, ln, [xb[:, t, :] for t in range(8)],
                           [xn[:, t, :] for t in range(8)], M, tmpf)
            c.wait(nc.tensor, 'V')
            for ht in range(HT):
                hsl = slice(ht * 128, (ht + 1) * 128)
                ph = ps_h[ht % 2]
                if ht >= 2:
                    c.wait(nc.tensor, 'A')
                for k in range(8):
                    i = nc.tensor.matmul(ph[:, 0:M], w1s[:, k, hsl],
                                         xn[:, k, :], start=(k == 0), stop=(k == 7),
                                         skip_group_check=True)
                c.inc('P', i)
                c.wait(nc.scalar, 'P')
                i = nc.scalar.activation(hid[:, ht, :], ph[:, 0:M], AF.Gelu,
                                         scale=a1c[:, ht:ht + 1], bias=b1c[:, ht:ht + 1])
                c.inc('A', i)
            for t in range(8):
                dsl = slice(t * 128, (t + 1) * 128)
                pd = ps_d[t % 2]
                if t >= 2:
                    c.wait(nc.tensor, 'A', 'V')
                for ht in range(HT):
                    i = nc.tensor.matmul(pd[:, 0:M], w2s[:, ht, dsl], hid[:, ht, :],
                                         start=(ht == 0), stop=(ht == HT - 1), skip_group_check=True)
                c.inc('P', i)
                c.wait(nc.scalar, 'P')
                i = nc.scalar.activation(raw[:, t, :], pd[:, 0:M], AF.Identity,
                                         scale=a2c[:, t:t + 1], bias=b2c[:, t:t + 1])
                c.inc('A', i)
                c.wait(nc.vector, 'A')
                i = nc.vector.tensor_copy(rawb[:, t, :], raw[:, t, :])
                c.inc('V', i)
            nc.vector.wait_ge(c.sems['L'], c.cnt['L'])
            c.wait(nc.scalar, 'V')
            emit_ln(c, ln, [rawb[:, t, :] for t in range(8)], M)
            c.wait(nc.vector, 'L')
            emit_normalize(c, ln, [raw[:, t, :] for t in range(8)], None, M, tmpf,
                           add_tiles=[xf[:, t, :] for t in range(8)],
                           out3=[x4[:, t, :] for t in range(8)])
            for ms in range(2):
                c.wait(nc.tensor, 'V', 'S')
                for t in range(4):
                    i = nc.tensor.transpose(ps_tf[:, t * 128:(t + 1) * 128],
                                            x4[:, t, ms * 128:(ms + 1) * 128], idf[:, :])
                c.inc('P', i)
                c.wait(nc.scalar, 'P')
                for t in range(4):
                    i = nc.scalar.copy(ytm[:, t * 128:(t + 1) * 128], ps_tf[:, t * 128:(t + 1) * 128])
                c.inc('A', i)
                c.wait(nc.tensor, 'A')
                for t in range(4, 8):
                    i = nc.tensor.transpose(ps_tf[:, (t - 4) * 128:(t - 3) * 128],
                                            x4[:, t, ms * 128:(ms + 1) * 128], idf[:, :])
                c.inc('P', i)
                c.wait(nc.scalar, 'P')
                for t in range(4, 8):
                    i = nc.scalar.copy(ytm[:, t * 128:(t + 1) * 128],
                                       ps_tf[:, (t - 4) * 128:(t - 3) * 128])
                c.inc('A', i)
                c.wait(nc.sync, 'A')
                c.inc('S', nc.sync.dma_start(
                    out=y_d[ch * M + ms * 128: ch * M + ms * 128 + 128, :], in_=ytm[:, :]), 16)
        c.wait(nc.sync, 'S')
        c.wait(nc.gpsimd, 'S')
    es.close()
    return nc


def _cols(b, n):
    return np.ascontiguousarray(b.reshape(n, 128).T)


def build_exchange(env):
    nc, es, c, blk = begin(env, "ex_")
    pes = env['c'].es if env is not None else es
    stats_in = c.dram("stats_in", [STATS_ROWS, D], F32, "in")
    stats_out = c.dram("stats_out", [STATS_ROWS, D], F32, "out")
    par_d = c.dram("parin", [1, 1], I32, "in")
    non_d = c.dram("noncein", [1, 1], I32, "in")
    sh_st = nc.dram_tensor(c.prefix + "sh_st", [2 * STATS_ROWS, D], F32, addr_space="Shared")
    sh_fl = nc.dram_tensor(c.prefix + "sh_fl", [2, 128], I32, addr_space="Shared")
    st8 = c.sb("st8", [STATS_ROWS, D], F32)
    pst = c.sb("pst", [STATS_ROWS, D], F32)
    kmA = c.sb("kmA", [1, 16], F32)
    kmB = c.sb("kmB", [1, 16], F32)
    pari = c.sb("pari", [1, 1], I32)
    nonc = c.sb("nonc", [1, 1], I32)
    flagb = c.sb("flagb", [1, 1], I32)
    exs = pes.enter_context(nc.semaphore("exs"))
    exp = pes.enter_context(nc.semaphore("exp"))
    g = nc.gpsimd
    with blk:
        c.inc('L', nc.sync.dma_start(out=st8[:, :], in_=stats_in), 16)
        c.inc('L', nc.sync.dma_start(out=kmA[:, :], in_=stats_in[73:74, 0:16]), 16)
        c.inc('G', g.dma_start(out=pari[:, :], in_=par_d), 16)
        c.inc('G', g.dma_start(out=nonc[:, :], in_=non_d), 16)
        c.wait(g, 'G', 'L')
        with (
            g.register("preg") as preg,
            g.register("df") as df,
            g.register("tgt") as tgt,
        ):
            g.reg_load(preg, pari[0:1, 0:1])
            want_v = g.value_load(nonc[0:1, 0:1])
            for parval in (0, 1):
                my0 = parval * STATS_ROWS
                pe0 = (1 - parval) * STATS_ROWS
                guard = g.If_eq(preg, 0) if parval == 0 else g.Else()
                with guard:
                    g.dma_start(out=sh_st.ap()[my0:my0 + STATS_ROWS, :],
                                in_=st8[:, :]).then_inc(exs, 16)
                    g.wait_ge(exs, 16)
                    g.dma_start(out=sh_fl.ap()[parval:parval + 1, 0:1],
                                in_=nonc[0:1, 0:1]).then_inc(exs, 16)
                    g.wait_ge(exs, 32)
                    g.reg_mov(tgt, 0)
                    g.reg_mov(df, 1)
                    with g.While(df):
                        g.reg_alu(tgt, tgt, 16, ALU.add)
                        g.dma_start(out=flagb[:, :],
                                    in_=sh_fl.ap()[1 - parval:2 - parval, 0:1]).then_inc(exp, 16)
                        g.wait_ge(exp, g.snap(tgt))
                        got_v = g.value_load(flagb[0:1, 0:1])
                        g.reg_alu(df, got_v, want_v, ALU.subtract)
                    g.dma_start(out=pst[:, :],
                                in_=sh_st.ap()[pe0:pe0 + STATS_ROWS, :]).then_inc(exs, 16)
                    g.dma_start(out=kmB[:, :],
                                in_=sh_st.ap()[pe0 + 73:pe0 + 74, 0:16]).then_inc(exs, 16)
                    g.wait_ge(exs, 64)
            i = g.nop()
            c.inc('G', i)
        c.wait(nc.vector, 'G')
        nc.vector.tensor_tensor(st8[0:64, :], st8[0:64, :], pst[0:64, :], op=ALU.add)
        i = nc.vector.tensor_tensor(st8[64:80, :], st8[64:80, :], pst[64:80, :], op=ALU.add)
        c.inc('V', i)
        i = nc.vector.tensor_tensor(kmA[:, :], kmA[:, :], kmB[:, :], op=ALU.max)
        c.inc('V', i)
        c.wait(nc.sync, 'V')
        c.inc('S', nc.sync.dma_start(out=stats_out[:, :], in_=st8[:, :]), 16)
        c.wait(nc.sync, 'S')
        c.inc('S', nc.sync.dma_start(out=stats_out[73:74, 0:16], in_=kmA[:, :]), 16)
        c.wait(nc.sync, 'S')
    es.close()
    return nc


def build_exch_write(env, sh_st, sh_fl, exs):
    nc, es, c, blk = begin(env, "ew_")
    stats_in = c.dram("stats_in", [STATS_ROWS, D], F32, "in")
    par_d = c.dram("parin", [1, 1], I32, "in")
    non_d = c.dram("noncein", [1, 1], I32, "in")
    st8 = c.sb("st8", [STATS_ROWS, D], F32)
    pari = c.sb("pari", [1, 1], I32)
    nonc = c.sb("nonc", [1, 1], I32)
    g = nc.gpsimd
    with blk:
        c.inc('L', nc.sync.dma_start(out=st8[:, :], in_=stats_in), 16)
        c.inc('G', g.dma_start(out=pari[:, :], in_=par_d), 16)
        c.inc('G', g.dma_start(out=nonc[:, :], in_=non_d), 16)
        c.wait(g, 'G', 'L')
        with g.register("ewpreg") as preg:
            g.reg_load(preg, pari[0:1, 0:1])
            for parval in (0, 1):
                my0 = parval * STATS_ROWS
                guard = g.If_eq(preg, 0) if parval == 0 else g.Else()
                with guard:
                    g.dma_start(out=sh_st.ap()[my0:my0 + STATS_ROWS, :],
                                in_=st8[:, :]).then_inc(exs, 16)
                    g.wait_ge(exs, 16)
                    g.dma_start(out=sh_fl.ap()[parval:parval + 1, 0:1],
                                in_=nonc[0:1, 0:1]).then_inc(exs, 16)
            i = g.nop()
            c.inc('G', i)
    es.close()
    return nc


def build_exch_poll(env, sh_st, sh_fl, exs, exp):
    nc, es, c, blk = begin(env, "ep_")
    stats_in = c.dram("stats_in", [STATS_ROWS, D], F32, "in")
    stats_out = c.dram("stats_out", [STATS_ROWS, D], F32, "out")
    par_d = c.dram("parin", [1, 1], I32, "in")
    non_d = c.dram("noncein", [1, 1], I32, "in")
    st8 = c.sb("st8", [STATS_ROWS, D], F32)
    pst = c.sb("pst", [STATS_ROWS, D], F32)
    kmA = c.sb("kmA", [1, 16], F32)
    kmB = c.sb("kmB", [1, 16], F32)
    pari = c.sb("pari", [1, 1], I32)
    nonc = c.sb("nonc", [1, 1], I32)
    flagb = c.sb("flagb", [1, 1], I32)
    g = nc.gpsimd
    with blk:
        c.inc('L', nc.sync.dma_start(out=st8[:, :], in_=stats_in), 16)
        c.inc('L', nc.sync.dma_start(out=kmA[:, :], in_=stats_in[73:74, 0:16]), 16)
        c.inc('G', g.dma_start(out=pari[:, :], in_=par_d), 16)
        c.inc('G', g.dma_start(out=nonc[:, :], in_=non_d), 16)
        c.wait(g, 'G', 'L')
        with (
            g.register("eppreg") as preg,
            g.register("epdf") as df,
            g.register("eptgt") as tgt,
        ):
            g.reg_load(preg, pari[0:1, 0:1])
            want_v = g.value_load(nonc[0:1, 0:1])
            for parval in (0, 1):
                pe0 = (1 - parval) * STATS_ROWS
                guard = g.If_eq(preg, 0) if parval == 0 else g.Else()
                with guard:
                    g.wait_ge(exs, 32)
                    g.reg_mov(tgt, 0)
                    g.reg_mov(df, 1)
                    with g.While(df):
                        g.reg_alu(tgt, tgt, 16, ALU.add)
                        g.dma_start(out=flagb[:, :],
                                    in_=sh_fl.ap()[1 - parval:2 - parval, 0:1]).then_inc(exp, 16)
                        g.wait_ge(exp, g.snap(tgt))
                        got_v = g.value_load(flagb[0:1, 0:1])
                        g.reg_alu(df, got_v, want_v, ALU.subtract)
                    g.dma_start(out=pst[:, :],
                                in_=sh_st.ap()[pe0:pe0 + STATS_ROWS, :]).then_inc(exs, 16)
                    g.dma_start(out=kmB[:, :],
                                in_=sh_st.ap()[pe0 + 73:pe0 + 74, 0:16]).then_inc(exs, 16)
                    g.wait_ge(exs, 64)
            i = g.nop()
            c.inc('G', i)
        c.wait(nc.vector, 'G')
        nc.vector.tensor_tensor(st8[0:64, :], st8[0:64, :], pst[0:64, :], op=ALU.add)
        i = nc.vector.tensor_tensor(st8[64:80, :], st8[64:80, :], pst[64:80, :], op=ALU.add)
        c.inc('V', i)
        i = nc.vector.tensor_tensor(kmA[:, :], kmA[:, :], kmB[:, :], op=ALU.max)
        c.inc('V', i)
        c.wait(nc.sync, 'V')
        c.inc('S', nc.sync.dma_start(out=stats_out[:, :], in_=st8[:, :]), 16)
        c.wait(nc.sync, 'S')
        c.inc('S', nc.sync.dma_start(out=stats_out[73:74, 0:16], in_=kmA[:, :]), 16)
        c.wait(nc.sync, 'S')
    es.close()
    return nc


def build_APPLY1(env, qp_all):
    nc, es, c, blk = begin(env, "p1_")
    M = 512
    xT = c.dram("xT", [D, ROWS], F32, "in")
    xhT = c.dram("xhT", [D, HDR], F32, "in")
    WqT_d = c.dram("WqT", [D, D], F32, "in")
    P_d = c.dram("Pstage", [128, D], F32, "in")
    bqc_d = c.dram("bqc", [128, 8], F32, "in")
    dia_d = c.dram("diagones", [128, 8, 16], F32, "in")
    idf_d = c.dram("idf", [128, 128], F32, "in")
    idb_d = c.dram("idb", [128, 128], F32, "in")

    WqT = c.sb("WqT", [128, 8, D], BF16)
    Pst = c.sb("Pst", [128, D], BF16)
    bqc = c.sb("bqc", [128, 8], F32)
    bqcs = c.sb("bqcs", [128, 8], F32)
    dia = c.sb("dia", [128, 8, 16], BF16)
    idf = c.sb("idf", [128, 128], F32)
    idb = c.sb("idb", [128, 128], BF16)
    xb = c.sb("xb", [128, 8, M], BF16)
    xn = c.sb("xn", [128, 8, M], BF16)
    tmpf = c.sb("tmpf", [128, M], F32)
    qbf = c.sb("qbf", [128, 8, M], BF16)
    sqq = c.sb("sqq", [128, 8, M], BF16)
    diag_fm = c.sb("diag_fm", [16, M], F32)
    diag_tm = c.sb("diag_tm", [128, 4, 16], F32)
    stab = c.sb("stab", [128, 16], F32)
    pre = c.sb("pre", [128, D], F32)
    qp_tm = c.sb("qp_tm", [128, 4, D], BF16)
    ln = mk_ln(c, M)
    ps_big = c.ps("ps_big", [128, 1024])
    ps_trb = c.ps("ps_trb", [128, 1024], BF16)
    ps_trf = c.ps("ps_trf", [128, 512])

    with blk:
        ln_init(c, ln)
        c.inc('G', nc.gpsimd.dma_start(out=WqT[:, :, :], in_=WqT_d.rearrange("(k p) n -> p k n", p=128)), 16)
        c.inc('G', nc.gpsimd.dma_start(out=Pst[:, :], in_=P_d), 16)
        c.inc('G', nc.gpsimd.dma_start(out=dia[:, :, :], in_=dia_d), 16)
        c.inc('G', nc.gpsimd.dma_start(out=idb[:, :], in_=idb_d), 16)
        c.inc('L', nc.sync.dma_start(out=idf[:, :], in_=idf_d), 16)
        c.inc('L', nc.sync.dma_start(out=bqc[:, :], in_=bqc_d), 16)
        nc.scalar.wait_ge(c.sems['L'], c.cnt['L'])
        i = nc.scalar.activation(bqcs[:, :], bqc[:, :], AF.Identity, scale=np.float32(CFAV))
        c.inc('A', i)

        xr = xT.rearrange("(k p) n -> p k n", p=128)
        xhr = xhT.rearrange("(k p) n -> p k n", p=128)
        for ch in range(9):
            hdr = (ch == 8)
            m = 128 if hdr else M
            nms = m // 128
            base = ch * M
            sl = None if hdr else slice(ch * M, (ch + 1) * M)
            if ch:
                c.wait(nc.gpsimd, 'V')
                c.wait(nc.sync, 'V')
            emit_chunk_load(c, xb, None, m, xr=xr, sl=sl, xhr=(xhr if hdr else None),
                            need_f32=False)
            emit_ln(c, ln, [xb[:, t, 0:m] for t in range(8)], m)
            c.wait(nc.vector, 'L')
            emit_normalize(c, ln, [xb[:, t, 0:m] for t in range(8)],
                           [xn[:, t, 0:m] for t in range(8)], m, tmpf)
            c.wait(nc.tensor, 'V', 'A')
            emit_proj_fm(c, WqT, xn, qbf, m, ps_big, bias_col=bqcs, scale=CFAV)
            emit_diag(c, dia, sqq, qbf, diag_fm, diag_tm, idf, ps_trf, ps_trf, m)
            for ms in range(nms):
                msl = slice(ms * 128, ms * 128 + 128)
                c.wait(nc.tensor, 'A', 'V')
                emit_favorp_head_mm(c, qbf, Pst, ps_big, msl)
                c.wait(nc.vector, 'P', 'A')
                nc.vector.reduce_max(stab[:, :].unsqueeze(2),
                                     ps_big[:, :].rearrange("p (h j) -> p h j", h=16),
                                     axis=AXX)
                i = nc.vector.scalar_tensor_tensor(stab[:, :], diag_tm[:, ms, :], -0.5,
                                                   stab[:, :], op0=ALU.mult, op1=ALU.subtract)
                c.inc('V', i)
                nc.vector.wait_ge(c.sems['V'], c.cnt['V'])
                i = nc.vector.tensor_tensor(
                    pre[:, :].rearrange("p (h j) -> p h j", h=16),
                    ps_big[:, :].rearrange("p (h j) -> p h j", h=16),
                    stab[:, :].unsqueeze(2).broadcast_to([128, 16, 64]), op=ALU.add)
                c.inc('V', i)
                c.wait(nc.scalar, 'V')
                i = nc.scalar.activation(pre[:, :], pre[:, :], AF.Exp)
                c.inc('A', i)
                c.wait(nc.vector, 'A')
                i = nc.vector.tensor_scalar(qp_tm[:, ms, :], pre[:, :], 1.0, float(EPS_FEAT),
                                            op0=ALU.mult, op1=ALU.add)
                c.inc('V', i)
                c.wait(nc.tensor, 'V')
                for t in range(8):
                    i = nc.tensor.transpose(ps_trb[:, t * 128:(t + 1) * 128],
                                            qp_tm[:, ms, t * 128:(t + 1) * 128], idb[:, :])
                c.inc('P', i)
                c.wait(nc.vector, 'P')
                for t in range(8):
                    i = nc.vector.tensor_copy(
                        qp_all[:, t, base + ms * 128:base + ms * 128 + 128],
                        ps_trb[:, t * 128:(t + 1) * 128])
                c.inc('V', i)
    es.close()
    return nc


def build_APPLY2(env, qp_all):
    nc, es, c, blk = begin(env, "p2_")
    M = 512
    xT = c.dram("xT", [D, ROWS], F32, "in")
    xhT = c.dram("xhT", [D, HDR], F32, "in")
    statsall_d = c.dram("statsall", [NCORE * STATS_ROWS, D], F32, "in")
    WoT_d = c.dram("WoT", [D, D], F32, "in")
    boc_d = c.dram("boc", [128, 8], F32, "in")
    idf_d = c.dram("idf", [128, 128], F32, "in")
    idb_d = c.dram("idb", [128, 128], F32, "in")
    x1T_d = c.dram("x1T", [D, ROWS], F32, "out")
    x1hT_d = c.dram("x1hT", [D, HDR], F32, "out")

    f_dram = nc.dram_tensor(c.prefix + "f_dram", [1, 16], F32)
    vs_dram = nc.dram_tensor(c.prefix + "vs_dram", [1, D], F32)
    ks_dram = nc.dram_tensor(c.prefix + "ks_dram", [8, 128], F32)

    WoT = c.sb("WoT", [128, 8, D], BF16)
    boc = c.sb("boc", [128, 8], F32)
    idf = c.sb("idf", [128, 128], F32)
    idb = c.sb("idb", [128, 128], BF16)
    stsum = c.sb("stsum", [80, D], F32)
    stabk = c.sb("stabk", [16, 1], F32)
    fexp = c.sb("fexp", [16, 1], F32)
    kv_stage = c.sb("kv_stage", [128, 512], F32)
    f_pack = c.sb("f_pack", [128, 8], F32)
    vs_pack = c.sb("vs_pack", [128, 8, 64], F32)
    kv_sb = c.sb("kv_sb", [128, 8, 64], BF16)
    ks_t = c.sb("ks_t", [8, 128], F32)
    f_e = c.sb("f_e", [8, 1], F32)
    f_o = c.sb("f_o", [8, 1], F32)
    BDk = c.sb("BDk", [128, 8, 16], BF16)

    xf = c.sb("xf", [128, 8, M], F32)
    tmpf = c.sb("tmpf", [128, M], F32)
    nrm = c.sb("nrm", [16, M], F32)
    rec_tm = c.sb("rec_tm", [128, 4, 16], F32)
    at_tm = c.sb("at_tm", [128, D], BF16)
    at_fm = c.sb("at_fm", [128, 8, M], BF16)
    raw = c.sb("raw", [128, 8, M], F32)
    rawb = c.sb("rawb", [128, 8, M], BF16)
    x1 = c.sb("x1", [128, 8, M], F32)
    ln = mk_ln(c, M)
    ps_aux = c.ps("ps_aux", [128, 512])
    ps_big = c.ps("ps_big", [128, 1024])
    ps_trb = c.ps("ps_trb", [128, 1024], BF16)
    ps_trf = c.ps("ps_trf", [128, 512])

    with blk:
        ln_init(c, ln)
        c.inc('G', nc.gpsimd.dma_start(out=WoT[:, :, :], in_=WoT_d.rearrange("(k p) n -> p k n", p=128)), 16)
        c.inc('G', nc.gpsimd.dma_start(out=idb[:, :], in_=idb_d), 16)
        c.inc('L', nc.sync.dma_start(out=idf[:, :], in_=idf_d), 16)
        c.inc('L', nc.sync.dma_start(out=boc[:, :], in_=boc_d), 16)
        c.inc('L', nc.sync.dma_start(out=stsum[:, :], in_=statsall_d[0:STATS_ROWS, :]), 16)
        with nc.allow_non_contiguous_dma(reason="16-element kmax row"):
            c.inc('L', nc.sync.dma_start(out=stabk[:, :],
                                         in_=statsall_d[73:74, 0:16].transpose([1, 0])), 16)
        c.wait(nc.vector, 'L')
        c.wait(nc.scalar, 'V', 'L')
        i = nc.scalar.activation(fexp[:, :], stabk[:, :], AF.Exp, scale=-1.0)
        c.inc('A', i)
        c.wait(nc.gpsimd, 'A', 'V')
        stk = stsum[0:64, :].rearrange("p (a q x) -> p a q x", a=8, q=2)
        c.inc('G', nc.gpsimd.dma_start(
            out=kv_stage[0:64, :].rearrange("p (a x) -> p a x", a=8),
            in_=stk[:, :, 0:1, :].squeeze(2)), 16)
        c.inc('G', nc.gpsimd.dma_start(
            out=kv_stage[64:128, :].rearrange("p (a x) -> p a x", a=8),
            in_=stk[:, :, 1:2, :].squeeze(2)), 16)
        c.inc('G', nc.gpsimd.dma_start(out=f_dram.ap().transpose([1, 0]), in_=fexp[:, :]), 16)
        c.inc('G', nc.gpsimd.dma_start(out=ks_dram[:, :], in_=stsum[64:72, 0:128]), 16)
        c.inc('G', nc.gpsimd.dma_start(out=vs_dram[:, :], in_=stsum[72:73, :]), 16)
        c.wait(nc.gpsimd, 'G')
        fap = f_dram.ap().squeeze(0).rearrange("(a p) -> a p", a=8)
        with nc.allow_non_contiguous_dma(reason="16-element parity unpack"):
            c.inc('G', nc.gpsimd.dma_start(
                out=f_pack[0:64, :].unsqueeze(2),
                in_=fap[:, 0:1].unsqueeze(0).partition_broadcast(64).squeeze(1)), 16)
            c.inc('G', nc.gpsimd.dma_start(
                out=f_pack[64:128, :].unsqueeze(2),
                in_=fap[:, 1:2].unsqueeze(0).partition_broadcast(64).squeeze(1)), 16)
            c.inc('G', nc.gpsimd.dma_start(out=f_e[:, :], in_=fap[:, 0:1]), 16)
            c.inc('G', nc.gpsimd.dma_start(out=f_o[:, :], in_=fap[:, 1:2]), 16)
        vap = vs_dram.ap().squeeze(0).rearrange("(a p x) -> a p x", a=8, p=2)
        c.inc('G', nc.gpsimd.dma_start(
            out=vs_pack[0:64, :, :], in_=vap[:, 0:1, :].squeeze(1).unsqueeze(0).partition_broadcast(64).squeeze(1)), 16)
        c.inc('G', nc.gpsimd.dma_start(
            out=vs_pack[64:128, :, :], in_=vap[:, 1:2, :].squeeze(1).unsqueeze(0).partition_broadcast(64).squeeze(1)), 16)
        c.inc('G', nc.gpsimd.dma_start(out=ks_t[:, :], in_=ks_dram[:, :]), 16)
        c.wait(nc.vector, 'G')
        nc.vector.tensor_scalar(vs_pack[:, :, :], vs_pack[:, :, :], float(EPS_FEAT), 0.0,
                                op0=ALU.mult, op1=ALU.add)
        nc.vector.tensor_tensor(
            kv_stage[:, :].rearrange("p (a x) -> p a x", a=8),
            kv_stage[:, :].rearrange("p (a x) -> p a x", a=8),
            f_pack[:, :].unsqueeze(2).broadcast_to([128, 8, 64]), op=ALU.mult)
        i = nc.vector.tensor_tensor(kv_sb[:, :, :],
                                    kv_stage[:, :].rearrange("p (a x) -> p a x", a=8),
                                    vs_pack[:, :, :], op=ALU.add)
        nc.vector.tensor_scalar(ks_t[:, 0:64], ks_t[:, 0:64], f_e[0:8, 0:1],
                                float(EPS_FEAT * 16384.0), op0=ALU.mult, op1=ALU.add)
        i = nc.vector.tensor_scalar(ks_t[:, 64:128], ks_t[:, 64:128], f_o[0:8, 0:1],
                                    float(EPS_FEAT * 16384.0), op0=ALU.mult, op1=ALU.add)
        c.inc('V', i)
        i = nc.gpsimd.memset(BDk[:, :, :], 0.0)
        c.inc('G', i)
        c.wait(nc.gpsimd, 'V')
        c.inc('G', nc.gpsimd.dma_start(out=ks_dram[:, :], in_=ks_t[:, :]), 16)
        c.wait(nc.gpsimd, 'G')
        for t in range(8):
            for par in range(2):
                i = nc.gpsimd.dma_start(
                    out=BDk[64 * par:64 * par + 64, t, 2 * t + par:2 * t + par + 1],
                    in_=ks_dram.ap()[t:t + 1, 64 * par:64 * par + 64].transpose([1, 0]))
                c.inc('G', i, 16)

        xr = xT.rearrange("(k p) n -> p k n", p=128)
        x1r = x1T_d.rearrange("(k p) n -> p k n", p=128)
        xhr = xhT.rearrange("(k p) n -> p k n", p=128)
        x1hr = x1hT_d.rearrange("(k p) n -> p k n", p=128)
        c.wait(nc.tensor, 'G')
        for ch in range(9):
            hdr = (ch == 8)
            m = 128 if hdr else M
            nms = m // 128
            base = ch * M
            sl = None if hdr else slice(ch * M, (ch + 1) * M)
            if ch:
                c.wait(nc.gpsimd, 'V')
                c.wait(nc.sync, 'V')
            if hdr:
                i = nc.gpsimd.memset(xf[:, :, 0:m], 0.0)
                c.inc('G', i)
                c.wait(nc.sync, 'G')
                c.inc('L', nc.sync.dma_start(out=xf[:, :, 0:HDR], in_=xhr), 16)
            else:
                c.inc('L', nc.sync.dma_start(out=xf[:, :, :], in_=xr[:, :, sl]), 16)
            c.wait(nc.tensor, 'A', 'V')
            for t in range(8):
                i = nc.tensor.matmul(ps_aux[32:48, 0:m], BDk[:, t, :],
                                     qp_all[:, t, base:base + m],
                                     start=(t == 0), stop=(t == 7), skip_group_check=True)
            c.inc('P', i)
            c.wait(nc.scalar, 'P')
            i = nc.scalar.copy(nrm[:, 0:m], ps_aux[32:48, 0:m])
            c.inc('A', i)
            c.wait(nc.vector, 'A')
            nc.vector.tensor_scalar_max(nrm[:, 0:m], nrm[:, 0:m], float(EPS_DEN))
            i = nc.vector.reciprocal(nrm[:, 0:m], nrm[:, 0:m])
            c.inc('V', i)
            c.wait(nc.tensor, 'V')
            for ms in range(nms):
                i = nc.tensor.transpose(ps_trf[:, 256 + ms * 16:256 + ms * 16 + 16],
                                        nrm[:, ms * 128:(ms + 1) * 128], idf[0:16, 0:16])
                c.inc('P', i)
            c.wait(nc.scalar, 'P')
            for ms in range(nms):
                i = nc.scalar.copy(rec_tm[:, ms, :], ps_trf[:, 256 + ms * 16:256 + ms * 16 + 16])
            c.inc('A', i)
            for ms in range(nms):
                msl = slice(ms * 128, ms * 128 + 128)
                gsl = slice(base + ms * 128, base + ms * 128 + 128)
                c.wait(nc.tensor, 'A', 'V')
                for par_pass in range(2):
                    for h in range(par_pass, NHEAD, 2):
                        par = 64 * (h % 2)
                        i = nc.tensor.matmul(ps_big[:, h * 64:(h + 1) * 64],
                                             qp_all[par:par + 64, h // 2, gsl],
                                             kv_sb[par:par + 64, h // 2, :],
                                             start=(h in (0, 8)), stop=(h in (7, 15)),
                                             skip_group_check=True)
                    if par_pass == 0:
                        c.inc('P', i)
                        nc.tensor.wait_ge(c.sems['P'], c.cnt['P'])
                c.inc('P', i)
                c.wait(nc.vector, 'P', 'A')
                i = nc.vector.tensor_tensor(
                    at_tm[:, :].rearrange("p (h j) -> p h j", h=16),
                    ps_big[:, :].rearrange("p (h j) -> p h j", h=16),
                    rec_tm[:, ms, :].unsqueeze(2).broadcast_to([128, 16, 64]), op=ALU.mult)
                c.inc('V', i)
                c.wait(nc.tensor, 'V')
                for t in range(8):
                    i = nc.tensor.transpose(ps_trb[:, t * 128:(t + 1) * 128],
                                            at_tm[:, t * 128:(t + 1) * 128], idb[:, :])
                c.inc('P', i)
                c.wait(nc.vector, 'P')
                for t in range(8):
                    i = nc.vector.tensor_copy(at_fm[:, t, msl], ps_trb[:, t * 128:(t + 1) * 128])
                c.inc('V', i)
            c.wait(nc.tensor, 'V')
            emit_proj_fm2(c, WoT, at_fm, raw, rawb, m, ps_big, bias_col=boc)
            nc.vector.wait_ge(c.sems['L'], c.cnt['L'])
            c.wait(nc.scalar, 'V')
            emit_ln(c, ln, [rawb[:, t, 0:m] for t in range(8)], m)
            c.wait(nc.vector, 'L')
            emit_normalize(c, ln, [raw[:, t, 0:m] for t in range(8)], None, m, tmpf,
                           add_tiles=[xf[:, t, 0:m] for t in range(8)],
                           out3=[x1[:, t, 0:m] for t in range(8)])
            c.wait(nc.sync, 'V')
            if hdr:
                c.inc('S', nc.sync.dma_start(out=x1hr, in_=x1[:, :, 0:HDR]), 16)
            else:
                c.inc('S', nc.sync.dma_start(out=x1r[:, :, sl], in_=x1[:, :, :]), 16)
        c.wait(nc.sync, 'S')
        c.wait(nc.gpsimd, 'S')
    es.close()
    return nc


def build_fused():
    nc = bass.Bass("TRN2", target_bir_lowering=False)
    es = ExitStack()
    c = Ctx(nc, es)
    BROWS = 2 * ROWS
    EI = lambda n, sh, dt=F32: nc.dram_tensor(n, sh, dt, kind="ExternalInput").ap()
    IN = lambda n, sh, dt=F32: nc.dram_tensor(n, sh, dt).ap()
    xT = EI("xT", [D, ROWS])
    xbT = EI("xbT", [D, BROWS])
    xhdrT = EI("xhdrT", [D, HDR])
    parin = EI("parin", [1, 1], I32)
    noncein = EI("noncein", [1, 1], I32)
    WkT = EI("WkT", [D, D]); WvT = EI("WvT", [D, D])
    WqT = EI("WqT", [D, D]); WoT = EI("WoT", [D, D])
    Pstage = EI("Pstage", [128, D])
    bkc = EI("bkc", [128, 8]); bqc = EI("bqc", [128, 8]); boc = EI("boc", [128, 8])
    bvr = EI("bvr", [1, D])
    dia = EI("diagones", [128, 8, 16])
    idf = EI("idf", [128, 128])
    wgT = EI("wgT", [D, 3072], I32); wuT = EI("wuT", [D, 3072], I32)
    wdT = EI("wdT", [3072, D], I32)
    WaqT = EI("WaqT", [D, D]); WakvT = EI("WakvT", [D, 2 * D]); WpT = EI("WpT", [D, D])
    l1mask = EI("l1mask", [128, 256])
    w1T = EI("w1T", [D, 4096]); w2T = EI("w2T", [4096, D])
    b1c = EI("b1c", [128, 32]); b2c = EI("b2c", [128, 8])
    y = nc.dram_tensor("y", [ROWS, D], F32, kind="ExternalOutput").ap()
    stats = IN("i_stats", [STATS_ROWS, D])
    stats2 = IN("i_stats2", [STATS_ROWS, D])
    x1T = IN("i_x1T", [D, ROWS]); x1hT = IN("i_x1hT", [D, HDR])
    x2T = IN("i_x2T", [D, ROWS]); x2hT = IN("i_x2hT", [D, HDR])
    x3T = IN("i_x3T", [D, ROWS])
    sh_st = nc.dram_tensor("fx_sh_st", [2 * STATS_ROWS, D], F32, addr_space="Shared")
    sh_fl = nc.dram_tensor("fx_sh_fl", [2, 128], I32, addr_space="Shared")
    exs = es.enter_context(nc.semaphore("fx_exs"))
    exp = es.enter_context(nc.semaphore("fx_exp"))
    with nc.Block():
        env = {'nc': nc, 'c': c}
        build_L0A_stats({**env, 'io': dict(xT=xbT, WkT=WkT, WvT=WvT, Pstage=Pstage, bkc=bkc,
                                           bvr=bvr, diagones=dia, idf=idf, stats=stats)},
                        rows=BROWS)
        full_barrier(c)
        build_exch_write({**env, 'io': dict(stats_in=stats, parin=parin, noncein=noncein)},
                         sh_st, sh_fl, exs)
        full_barrier(c)
        with nc.sbuf_tensor("fx_qp_all", [128, 8, 4224], BF16) as qp_all:
            build_APPLY1({**env, 'io': dict(xT=xT, xhT=xhdrT, WqT=WqT, Pstage=Pstage,
                                            bqc=bqc, diagones=dia, idf=idf, idb=idf)},
                         qp_all)
            full_barrier(c)
            build_exch_poll({**env, 'io': dict(stats_in=stats, stats_out=stats2,
                                               parin=parin, noncein=noncein)},
                            sh_st, sh_fl, exs, exp)
            full_barrier(c)
            build_APPLY2({**env, 'io': dict(xT=xT, xhT=xhdrT, statsall=stats2, WoT=WoT,
                                            boc=boc, idf=idf, idb=idf,
                                            x1T=x1T, x1hT=x1hT)},
                         qp_all)
        full_barrier(c)
        build_L0F({**env, 'io': dict(xT=x1T, xhT=x1hT, wgT=wgT, wuT=wuT, wdT=wdT,
                                     x2T=x2T, x2hT=x2hT)})
        full_barrier(c)
        build_L1A({**env, 'io': dict(xT=x2T, xhT=x2hT, WaqT=WaqT, WakvT=WakvT, WpT=WpT,
                                     l1mask=l1mask, idb=idf, x3T=x3T)})
        full_barrier(c)
        build_L1F({**env, 'io': dict(xT=x3T, w1T=w1T, w2T=w2T, b1c=b1c, b2c=b2c, idf=idf, y=y)})
    es.close()
    return nc


def kernel(L0_attn_query, L0_attn_Wq, L0_attn_bq, L0_attn_Wk, L0_attn_bk,
           L0_attn_Wv, L0_attn_bv, L0_attn_Wo, L0_attn_bo, L0_attn_proj,
           L0_ffn_w_gate, L0_ffn_w_up, L0_ffn_w_down,
           L1_attn_W_attn, L1_attn_W_proj, L1_attn_W_rel,
           L1_ffn_w1, L1_ffn_b1, L1_ffn_w2, L1_ffn_b2):
    _ = L1_attn_W_rel
    x = np.ascontiguousarray(np.asarray(L0_attn_query, dtype=np.float32).reshape(NCORE * ROWS, D))
    cores = list(range(NCORE))
    xT = [np.ascontiguousarray(x[c * ROWS:(c + 1) * ROWS].T) for c in cores]
    xbT = {(b, p): np.ascontiguousarray(
               x[b * 4 * ROWS + p * 2 * ROWS:b * 4 * ROWS + (p + 1) * 2 * ROWS].T)
           for b in range(2) for p in range(2)}
    xhdrT = [np.ascontiguousarray(x[b * 4 * ROWS:b * 4 * ROWS + HDR].T) for b in range(2)]
    nonce = int(np.random.default_rng().integers(1, 2**31 - 1))
    idf = np.eye(128, dtype=np.float32)
    P_stage = np.zeros((128, D), np.float32)
    proj = np.asarray(L0_attn_proj, dtype=np.float32)
    for h in range(NHEAD):
        par = 64 * (h % 2)
        P_stage[par:par + 64, h * 64:(h + 1) * 64] = proj[h].T
    diagones = np.zeros((128, 8, 16), np.float32)
    for t in range(8):
        diagones[0:64, t, 2 * t] = 1.0
        diagones[64:128, t, 2 * t + 1] = 1.0

    def T(w):
        return np.ascontiguousarray(np.asarray(w).T)

    Wa = np.asarray(L1_attn_W_attn, np.float32)
    l1mask = []
    for cc in cores:
        mk = np.zeros((128, 256), np.float32)
        if cc % 4 == 0:
            t_ar = np.arange(128)[:, None]
            kk = np.tile(np.arange(16)[None, :], (1, 16)).reshape(1, 256)
            mk = np.where(kk <= t_ar, 0.0, -1e30).astype(np.float32)
        l1mask.append(np.ascontiguousarray(mk))
    common = {
        "WkT": T(L0_attn_Wk), "WvT": T(L0_attn_Wv), "WqT": T(L0_attn_Wq), "WoT": T(L0_attn_Wo),
        "Pstage": P_stage, "diagones": diagones, "idf": idf,
        "bkc": _cols(np.asarray(L0_attn_bk, np.float32), 8),
        "bqc": _cols(np.asarray(L0_attn_bq, np.float32), 8),
        "boc": _cols(np.asarray(L0_attn_bo, np.float32), 8),
        "bvr": np.asarray(L0_attn_bv, np.float32).reshape(1, D),
        "wgT": T(L0_ffn_w_gate), "wuT": T(L0_ffn_w_up), "wdT": T(L0_ffn_w_down),
        "WaqT": T(Wa[0:D]), "WakvT": T(Wa[D:3 * D]), "WpT": T(L1_attn_W_proj),
        "w1T": T(L1_ffn_w1), "w2T": T(L1_ffn_w2),
        "b1c": _cols(np.asarray(L1_ffn_b1, np.float32), 32),
        "b2c": _cols(np.asarray(L1_ffn_b2, np.float32), 8),
    }
    nc = build_fused()
    in_maps = [dict(common, xT=xT[cc], xbT=xbT[(cc // 4, cc % 2)],
                    xhdrT=xhdrT[cc // 4], l1mask=l1mask[cc],
                    parin=np.array([[cc % 2]], np.int32),
                    noncein=np.array([[nonce]], np.int32)) for cc in cores]
    res = run_bass_kernel_spmd(nc, in_maps, core_ids=cores)
    yy = np.concatenate([res.results[cc]["y"] for cc in cores], axis=0)
    return yy.reshape(2, 16384, 1024)


def kernel_unfused(L0_attn_query, L0_attn_Wq, L0_attn_bq, L0_attn_Wk, L0_attn_bk,
           L0_attn_Wv, L0_attn_bv, L0_attn_Wo, L0_attn_bo, L0_attn_proj,
           L0_ffn_w_gate, L0_ffn_w_up, L0_ffn_w_down,
           L1_attn_W_attn, L1_attn_W_proj, L1_attn_W_rel,
           L1_ffn_w1, L1_ffn_b1, L1_ffn_w2, L1_ffn_b2):
    x = np.ascontiguousarray(np.asarray(L0_attn_query, dtype=np.float32).reshape(NCORE * ROWS, D))
    cores = list(range(NCORE))
    xT = [np.ascontiguousarray(x[c * ROWS:(c + 1) * ROWS].T) for c in cores]
    xhT = [np.ascontiguousarray(x[(c // 4) * 4 * ROWS:(c // 4) * 4 * ROWS + HDR].T) for c in cores]

    idf = np.eye(128, dtype=np.float32)
    P_stage = np.zeros((128, D), np.float32)
    proj = np.asarray(L0_attn_proj, dtype=np.float32)
    for h in range(NHEAD):
        par = 64 * (h % 2)
        P_stage[par:par + 64, h * 64:(h + 1) * 64] = proj[h].T
    diagones = np.zeros((128, 8, 16), np.float32)
    for t in range(8):
        diagones[0:64, t, 2 * t] = 1.0
        diagones[64:128, t, 2 * t + 1] = 1.0

    def T(w):
        return np.ascontiguousarray(np.asarray(w).T)

    nc = build_L0A_stats()
    common_a = {"WkT": T(L0_attn_Wk), "WvT": T(L0_attn_Wv), "Pstage": P_stage,
                "bkc": _cols(np.asarray(L0_attn_bk, np.float32), 8),
                "bvr": np.asarray(L0_attn_bv, np.float32).reshape(1, D),
                "diagones": diagones, "idf": idf}
    res = run_bass_kernel_spmd(nc, [dict(common_a, xT=xT[c]) for c in cores], core_ids=cores)
    statsall = np.ascontiguousarray(
        np.concatenate([res.results[c]["stats"] for c in cores], axis=0))

    nc = build_L0A_apply()
    w8 = []
    kmask = []
    for c in cores:
        g = c // 4
        w = np.zeros((128, 8), np.float32)
        w[:, 4 * g:4 * g + 4] = 1.0
        w8.append(w)
        km = np.full((8, 16), -1e30, np.float32)
        km[4 * g:4 * g + 4, :] = 0.0
        kmask.append(km)
    common_b = {"statsall": statsall, "WqT": T(L0_attn_Wq), "WoT": T(L0_attn_Wo),
                "Pstage": P_stage, "bqc": _cols(np.asarray(L0_attn_bq, np.float32), 8),
                "boc": _cols(np.asarray(L0_attn_bo, np.float32), 8),
                "diagones": diagones, "idf": idf, "idb": idf}
    res = run_bass_kernel_spmd(
        nc, [dict(common_b, xT=xT[c], xhT=xhT[c], w8=w8[c], kmask8=kmask[c]) for c in cores],
        core_ids=cores)
    x1T = [res.results[c]["x1T"] for c in cores]
    x1hT = [res.results[c]["x1hT"] for c in cores]

    nc = build_L0F()
    common_c = {"wgT": T(L0_ffn_w_gate), "wuT": T(L0_ffn_w_up), "wdT": T(L0_ffn_w_down)}
    res = run_bass_kernel_spmd(
        nc, [dict(common_c, xT=x1T[c], xhT=x1hT[c]) for c in cores], core_ids=cores)
    x2T = [res.results[c]["x2T"] for c in cores]
    x2hT = [res.results[c]["x2hT"] for c in cores]

    nc = build_L1A()
    Wa = np.asarray(L1_attn_W_attn, np.float32)
    l1mask = []
    for c in cores:
        mk = np.zeros((128, 256), np.float32)
        if c % 4 == 0:
            t_ar = np.arange(128)[:, None]
            kk = np.tile(np.arange(16)[None, :], (1, 16)).reshape(1, 256)
            mk = np.where(kk <= t_ar, 0.0, -1e30).astype(np.float32)
        l1mask.append(np.ascontiguousarray(mk))
    common_d = {"WaqT": T(Wa[0:D]), "WakvT": T(Wa[D:3 * D]), "WpT": T(L1_attn_W_proj),
                "idb": idf}
    res = run_bass_kernel_spmd(
        nc, [dict(common_d, xT=x2T[c], xhT=x2hT[c], l1mask=l1mask[c]) for c in cores],
        core_ids=cores)
    x3T = [res.results[c]["x3T"] for c in cores]

    nc = build_L1F()
    common_e = {"w1T": T(L1_ffn_w1), "w2T": T(L1_ffn_w2),
                "b1c": _cols(np.asarray(L1_ffn_b1, np.float32), 32),
                "b2c": _cols(np.asarray(L1_ffn_b2, np.float32), 8), "idf": idf}
    res = run_bass_kernel_spmd(
        nc, [dict(common_e, xT=x3T[c]) for c in cores], core_ids=cores)
    y = np.concatenate([res.results[c]["y"] for c in cores], axis=0)
    return y.reshape(2, 16384, 1024)


# version 14
# speedup vs baseline: 1.1946x; 1.1946x over previous
import numpy as np
import ml_dtypes
from contextlib import ExitStack
import concourse.bass as bass
import concourse.mybir as mybir
from concourse.bass_utils import run_bass_kernel_spmd

F32 = mybir.dt.float32
BF16 = mybir.dt.bfloat16
I32 = mybir.dt.int32
AF = mybir.ActivationFunctionType
ALU = mybir.AluOpType
AXX = mybir.AxisListType.X

NCORE = 8
D = 1024
ROWS = 4096
HDR = 16
NHEAD = 16
CFAV = float(64.0 ** -0.25)
LN2_3 = float(3 * np.log(2.0))
EPS_FEAT = 1e-4
EPS_DEN = 0.01
SCALE_FFN0 = 1.0 / 127.0
STATS_ROWS = 80


class Ctx:
    def __init__(self, nc, es, parent=None, prefix="", io=None):
        self.nc = nc
        self.es = es
        self.prefix = prefix
        self.io = io or {}
        if parent is None:
            self.sems = {}
            self.cnt = {}
            for s in ['L', 'G', 'P', 'A', 'V', 'S']:
                self.sems[s] = es.enter_context(nc.semaphore(s))
                self.cnt[s] = 0
            self.solo = True
        else:
            self.sems = parent.sems
            self.cnt = parent.cnt
            self.solo = False

    def inc(self, name, instr, n=1):
        instr.then_inc(self.sems[name], n)
        self.cnt[name] += n

    def wait(self, eng, *names):
        for n in names:
            eng.wait_ge(self.sems[n], self.cnt[n])

    def sb(self, name, shape, dt):
        return self.es.enter_context(self.nc.sbuf_tensor("sb_" + self.prefix + name, shape, dt))

    def ps(self, name, shape, dt=F32):
        return self.es.enter_context(self.nc.psum_tensor(self.prefix + name, shape, dt))

    def dram(self, name, shape, dt, kind):
        if name in self.io:
            return self.io[name]
        if self.solo:
            k = {"in": "ExternalInput", "out": "ExternalOutput"}[kind]
            return self.nc.dram_tensor(name, shape, dt, kind=k).ap()
        return self.nc.dram_tensor(self.prefix + name, shape, dt).ap()


from contextlib import nullcontext


def full_barrier(c):
    nc = c.nc
    for eng in [nc.sync, nc.gpsimd, nc.tensor, nc.scalar, nc.vector]:
        for s in ['L', 'G', 'P', 'A', 'V', 'S']:
            eng.wait_ge(c.sems[s], c.cnt[s])


def begin(env, prefix):
    if env is None:
        nc = bass.Bass("TRN2", target_bir_lowering=False)
        es = ExitStack()
        c = Ctx(nc, es)
        blk = nc.Block()
    else:
        nc = env['nc']
        es = ExitStack()
        c = Ctx(nc, es, parent=env['c'], prefix=prefix, io=env.get('io', {}))
        blk = nullcontext()
    return nc, es, c, blk


def mk_ln(c, mmax, tag="ln", sq=None, ps_stat=None):
    return dict(
        mmax=mmax,
        ones=c.sb(tag + "_ones", [128, 1], BF16),
        eps=c.sb(tag + "_eps", [1, 1], F32),
        sq=sq if sq is not None else c.sb(tag + "_sq", [128, 8, mmax], BF16),
        mean=c.sb(tag + "_mean", [1, mmax], F32),
        msq=c.sb(tag + "_msq", [1, mmax], F32),
        var=c.sb(tag + "_var", [1, mmax], F32),
        rstd=c.sb(tag + "_rstd", [1, 2 * mmax], F32),
        bc=c.sb(tag + "_bc", [128, 2, mmax], F32),
        bounce=c.nc.dram_tensor(c.prefix + tag + "_bounce", [1, 2 * mmax], F32),
        ps_stat=ps_stat if ps_stat is not None else c.ps(tag + "_psstat", [128, 512], F32),
    )


def ln_init(c, ln):
    nc = c.nc
    nc.gpsimd.memset(ln['ones'][:, :], 2.0 ** -10)
    i = nc.gpsimd.memset(ln['eps'][:, :], 1e-5)
    c.inc('G', i)


def emit_ln(c, ln, xb_tiles, m):
    nc = c.nc
    mx_ = ln['mmax']
    for t in range(8):
        i = nc.scalar.activation(ln['sq'][:, t, :m], xb_tiles[t], AF.Square)
    c.inc('A', i)
    c.wait(nc.tensor, 'A')
    for t in range(8):
        i = nc.tensor.matmul(ln['ps_stat'][0:1, :m], ln['ones'][:, :], xb_tiles[t],
                             start=(t == 0), stop=(t == 7), skip_group_check=True)
    for t in range(8):
        i = nc.tensor.matmul(ln['ps_stat'][64:65, :m], ln['ones'][:, :], ln['sq'][:, t, :m],
                             start=(t == 0), stop=(t == 7), skip_group_check=True)
    c.inc('P', i)
    c.wait(nc.scalar, 'P')
    nc.scalar.copy(ln['mean'][:, :m], ln['ps_stat'][0:1, :m])
    i = nc.scalar.copy(ln['msq'][:, :m], ln['ps_stat'][64:65, :m])
    c.inc('A', i)
    c.wait(nc.vector, 'A')
    nc.vector.tensor_tensor(ln['var'][:, :m], ln['mean'][:, :m], ln['mean'][:, :m], op=ALU.mult)
    i = nc.vector.tensor_tensor(ln['var'][:, :m], ln['msq'][:, :m], ln['var'][:, :m], op=ALU.subtract)
    c.inc('V', i)
    c.wait(nc.scalar, 'V')
    nc.scalar.activation(ln['rstd'][:, 0:m], ln['var'][:, :m], AF.Ln, bias=ln['eps'][0:1, :])
    i = nc.scalar.activation(ln['rstd'][:, 0:m], ln['rstd'][:, 0:m], AF.Exp, scale=-0.5)
    c.inc('A', i)
    c.wait(nc.vector, 'A')
    i = nc.vector.scalar_tensor_tensor(ln['rstd'][:, mx_:mx_ + m], ln['mean'][:, :m], -1.0,
                                       ln['rstd'][:, 0:m], op0=ALU.mult, op1=ALU.mult)
    c.inc('V', i)
    c.wait(nc.sync, 'V', 'A')
    c.inc('L', nc.sync.dma_start(out=ln['bounce'][0:1, 0:mx_ + m], in_=ln['rstd'][:, 0:mx_ + m]), 16)
    c.wait(nc.sync, 'L')
    i = nc.sync.dma_start(
        out=ln['bc'][:, :, :m],
        in_=ln['bounce'].ap().squeeze(0).rearrange("(a x) -> a x", a=2)[:, 0:m]
             .unsqueeze(0).partition_broadcast(128).squeeze(1))
    c.inc('L', i, 16)


def emit_normalize(c, ln, src_tiles, out_tiles, m, tmpf, add_tiles=None, out3=None):
    nc = c.nc
    for t in range(8):
        nc.vector.scalar_tensor_tensor(tmpf[:, :m], src_tiles[t], 1.0,
                                       ln['bc'][:, 0, :m], op0=ALU.mult, op1=ALU.mult)
        if add_tiles is None:
            i = nc.vector.tensor_tensor(out_tiles[t], tmpf[:, :m], ln['bc'][:, 1, :m], op=ALU.add)
        else:
            nc.vector.tensor_tensor(tmpf[:, :m], tmpf[:, :m], ln['bc'][:, 1, :m], op=ALU.add)
            i = nc.vector.tensor_tensor(out3[t], tmpf[:, :m], add_tiles[t], op=ALU.add)
    c.inc('V', i)


def emit_chunk_load(c, xb, xf, m, xr=None, sl=None, xhr=None, need_f32=True):
    nc = c.nc
    if xhr is not None:
        i = nc.gpsimd.memset(xb[:, :, 0:m], 0.0)
        c.inc('G', i)
        if need_f32:
            i = nc.gpsimd.memset(xf[:, :, 0:m], 0.0)
            c.inc('G', i)
        c.inc('G', nc.gpsimd.dma_start(out=xb[:, :, 0:HDR], in_=xhr), 16)
        if need_f32:
            c.wait(nc.sync, 'G')
            c.inc('L', nc.sync.dma_start(out=xf[:, :, 0:HDR], in_=xhr), 16)
    else:
        c.inc('G', nc.gpsimd.dma_start(out=xb[:, :, :], in_=xr[:, :, sl]), 16)
        if need_f32:
            c.inc('L', nc.sync.dma_start(out=xf[:, :, :], in_=xr[:, :, sl]), 16)
    nc.scalar.wait_ge(c.sems['G'], c.cnt['G'])
    nc.tensor.wait_ge(c.sems['G'], c.cnt['G'])
    nc.vector.wait_ge(c.sems['G'], c.cnt['G'])


def emit_proj_fm(c, W, src, dst, m, ps_big, bias_col=None, scale=None):
    nc = c.nc
    kw = {}
    if scale is not None:
        kw['scale'] = np.float32(scale)
    ah = []
    for t in range(8):
        psl = slice((t % 2) * 512, (t % 2) * 512 + 512)
        if t >= 2:
            nc.tensor.wait_ge(c.sems['A'], ah[t - 2])
        for k in range(8):
            i = nc.tensor.matmul(ps_big[:, psl][:, 0:m], W[:, k, t * 128:(t + 1) * 128],
                                 src[:, k, 0:m], start=(k == 0), stop=(k == 7),
                                 skip_group_check=True)
        c.inc('P', i)
        c.wait(nc.scalar, 'P')
        if bias_col is not None:
            i = nc.scalar.activation(dst[:, t, 0:m], ps_big[:, psl][:, 0:m], AF.Identity,
                                     bias=bias_col[:, t:t + 1], **kw)
        elif scale is not None:
            i = nc.scalar.activation(dst[:, t, 0:m], ps_big[:, psl][:, 0:m], AF.Identity, **kw)
        else:
            i = nc.scalar.copy(dst[:, t, 0:m], ps_big[:, psl][:, 0:m])
        c.inc('A', i)
        ah.append(c.cnt['A'])


def emit_proj_fm2(c, W, src, raw, rawb, m, ps_big, bias_col=None, scale=None):
    nc = c.nc
    kw = {}
    if scale is not None:
        kw['scale'] = np.float32(scale)
    vh = []
    for t in range(8):
        psl = slice((t % 2) * 512, (t % 2) * 512 + 512)
        if t >= 2:
            nc.tensor.wait_ge(c.sems['V'], vh[t - 2])
        for k in range(8):
            i = nc.tensor.matmul(ps_big[:, psl][:, 0:m], W[:, k, t * 128:(t + 1) * 128],
                                 src[:, k, 0:m], start=(k == 0), stop=(k == 7),
                                 skip_group_check=True)
        c.inc('P', i)
        c.wait(nc.scalar, 'P')
        bkw = dict(kw)
        if bias_col is not None:
            bkw['bias'] = bias_col[:, t:t + 1]
        i = nc.scalar.activation(raw[:, t, 0:m], ps_big[:, psl][:, 0:m], AF.Identity, **bkw)
        c.inc('A', i)
        c.wait(nc.vector, 'A')
        i = nc.vector.tensor_copy(rawb[:, t, 0:m], raw[:, t, 0:m])
        c.inc('V', i)
        vh.append(c.cnt['V'])


def pe_barrier(c):
    nc = c.nc
    i = nc.tensor.nop()
    c.inc('P', i)
    nc.tensor.wait_ge(c.sems['P'], c.cnt['P'])


def emit_favorp_head_mm(c, src_bf, Pst, ps_big, msl):
    nc = c.nc
    for par_pass in range(2):
        for h in range(par_pass, NHEAD, 2):
            par = 64 * (h % 2)
            i = nc.tensor.matmul(ps_big[:, h * 64:(h + 1) * 64],
                                 src_bf[par:par + 64, h // 2, msl],
                                 Pst[par:par + 64, h * 64:(h + 1) * 64],
                                 start=(h in (0, 8)), stop=(h in (7, 15)),
                                 skip_group_check=True)
        if par_pass == 0:
            c.inc('P', i)
            nc.tensor.wait_ge(c.sems['P'], c.cnt['P'])
    c.inc('P', i)


def emit_diag(c, dia, sq_src, qbf, diag_fm, diag_tm, idf, ps_aux, ps_trf, m):
    nc = c.nc
    c.wait(nc.scalar, 'A')
    for t in range(8):
        i = nc.scalar.activation(sq_src[:, t, 0:m], qbf[:, t, 0:m], AF.Square)
    c.inc('A', i)
    c.wait(nc.tensor, 'A')
    for t in range(8):
        i = nc.tensor.matmul(ps_aux[0:16, 0:m], dia[:, t, :], sq_src[:, t, 0:m],
                             start=(t == 0), stop=(t == 7), skip_group_check=True)
    c.inc('P', i)
    c.wait(nc.scalar, 'P')
    i = nc.scalar.activation(diag_fm[:, 0:m], ps_aux[0:16, 0:m], AF.Identity,
                             bias=np.float32(2 * LN2_3))
    c.inc('A', i)
    c.wait(nc.tensor, 'A')
    nms = m // 128
    for ms in range(nms):
        i = nc.tensor.transpose(ps_trf[:, ms * 16:ms * 16 + 16],
                                diag_fm[:, ms * 128:(ms + 1) * 128], idf[0:16, 0:16])
    c.inc('P', i)
    c.wait(nc.scalar, 'P')
    for ms in range(nms):
        i = nc.scalar.copy(diag_tm[:, ms, :], ps_trf[:, ms * 16:ms * 16 + 16])
    c.inc('A', i)


def build_L0A_stats(env=None, rows=ROWS):
    nc, es, c, blk = begin(env, "pa_")
    M = 512
    NCH = rows // M
    xT = c.dram("xT", [D, rows], F32, "in")
    WkT_d = c.dram("WkT", [D, D], F32, "in")
    WvT_d = c.dram("WvT", [D, D], F32, "in")
    P_d = c.dram("Pstage", [128, D], F32, "in")
    bkc_d = c.dram("bkc", [128, 8], F32, "in")
    bvr_d = c.dram("bvr", [1, D], F32, "in")
    dia_d = c.dram("diagones", [128, 8, 16], F32, "in")
    idf_d = c.dram("idf", [128, 128], F32, "in")
    stats_d = c.dram("stats", [STATS_ROWS, D], F32, "out")

    WkT = c.sb("WkT", [128, 8, D], BF16)
    WvT = c.sb("WvT", [128, 8, D], BF16)
    Pst = c.sb("Pst", [128, D], BF16)
    bkc = c.sb("bkc", [128, 8], F32)
    bkcs = c.sb("bkcs", [128, 8], F32)
    bvr = c.sb("bvr", [1, D], BF16)
    dia = c.sb("dia", [128, 8, 16], BF16)
    idf = c.sb("idf", [128, 128], F32)
    ones_row = c.sb("ones_row", [1, 128], BF16)
    ones_col = c.sb("ones_colb", [128, 1], BF16)

    xb = c.sb("xb", [128, 8, M], BF16)
    xn = c.sb("xn", [128, 8, M], BF16)
    tmpf = c.sb("tmpf", [128, M], F32)
    kbf = c.sb("kbf", [128, 8, M], BF16)
    sqk = c.sb("sqk", [128, 8, M], BF16)
    vtm = c.sb("vtm", [128, 4, D], BF16)
    diag_fm = c.sb("diag_fm", [16, M], F32)
    diag_tm = c.sb("diag_tm", [128, 4, 16], F32)
    stab = c.sb("stab", [128, 16], F32)
    pre = c.sb("pre", [128, D], F32)
    keq = c.sb("keq", [128, 16, 65], BF16)
    kmax_acc = c.sb("kmax_acc", [128, 16], F32)
    km_t = c.sb("km_t", [128, 16], F32)
    kv_ev = c.sb("kv_ev", [128, 1024], F32)
    ks_ev = c.sb("ks_ev", [128, 8], F32)
    ks_tr = c.sb("ks_tr", [8, 128], F32)
    kmx_tr = c.sb("kmx_tr", [16, 128], F32)
    kmx_f = c.sb("kmx_f", [16, 1], F32)
    zerot = c.sb("zerot", [16, 1024], F32)

    ln = mk_ln(c, M)
    ps_kv = c.ps("ps_kv", [128, 1024])
    ps_ks = c.ps("ps_ks", [128, 512])
    ps_big = c.ps("ps_big", [128, 1024])
    ps_trf = c.ps("ps_trf", [128, 512])

    with blk:
        ln_init(c, ln)
        nc.gpsimd.memset(ones_row[:, :], 1.0)
        nc.gpsimd.memset(ones_col[:, :], 1.0)
        nc.gpsimd.memset(zerot[:, :], 0.0)
        nc.gpsimd.memset(keq[:, :, :], 1.0)
        i = nc.gpsimd.memset(kmax_acc[:, :], -1e30)
        c.inc('G', i)
        c.inc('G', nc.gpsimd.dma_start(out=WkT[:, :, :], in_=WkT_d.rearrange("(k p) n -> p k n", p=128)), 16)
        c.inc('G', nc.gpsimd.dma_start(out=WvT[:, :, :], in_=WvT_d.rearrange("(k p) n -> p k n", p=128)), 16)
        c.inc('G', nc.gpsimd.dma_start(out=Pst[:, :], in_=P_d), 16)
        c.inc('G', nc.gpsimd.dma_start(out=dia[:, :, :], in_=dia_d), 16)
        c.inc('G', nc.gpsimd.dma_start(out=bvr[:, :], in_=bvr_d), 16)
        c.inc('L', nc.sync.dma_start(out=bkc[:, :], in_=bkc_d), 16)
        c.inc('L', nc.sync.dma_start(out=idf[:, :], in_=idf_d), 16)
        nc.scalar.wait_ge(c.sems['L'], c.cnt['L'])
        i = nc.scalar.activation(bkcs[:, :], bkc[:, :], AF.Identity, scale=np.float32(CFAV))
        c.inc('A', i)

        xr = xT.rearrange("(k p) n -> p k n", p=128)
        for ch in range(NCH):
            sl = slice(ch * M, (ch + 1) * M)
            if ch:
                c.wait(nc.gpsimd, 'V', 'P')
                c.wait(nc.sync, 'V', 'P')
            emit_chunk_load(c, xb, None, M, xr=xr, sl=sl, need_f32=False)
            emit_ln(c, ln, [xb[:, t, :] for t in range(8)], M)
            c.wait(nc.vector, 'L')
            emit_normalize(c, ln, [xb[:, t, :] for t in range(8)],
                           [xn[:, t, :] for t in range(8)], M, tmpf)
            c.wait(nc.tensor, 'V', 'A')
            emit_proj_fm(c, WkT, xn, kbf, M, ps_big, bias_col=bkcs, scale=CFAV)
            vah = []
            va_base = c.cnt['A']
            for ms in range(4):
                msl = slice(ms * 128, ms * 128 + 128)
                for half in range(2):
                    g = ms * 2 + half
                    psl = slice(half * 512, half * 512 + 512)
                    dsl = slice(half * 512, half * 512 + 512)
                    nc.tensor.wait_ge(c.sems['A'], vah[g - 2] if g >= 2 else va_base)
                    for k in range(8):
                        nc.tensor.matmul(ps_big[:, psl], xn[:, k, msl], WvT[:, k, dsl],
                                         start=(k == 0), stop=False, skip_group_check=True)
                    i = nc.tensor.matmul(ps_big[:, psl], ones_row[:, :], bvr[:, dsl],
                                         start=False, stop=True, skip_group_check=True)
                    c.inc('P', i)
                    c.wait(nc.scalar, 'P')
                    i = nc.scalar.copy(vtm[:, ms, dsl], ps_big[:, psl])
                    c.inc('A', i)
                    vah.append(c.cnt['A'])
            emit_diag(c, dia, sqk, kbf, diag_fm, diag_tm, idf, ps_trf, ps_trf, M)
            for ms in range(4):
                msl = slice(ms * 128, ms * 128 + 128)
                c.wait(nc.tensor, 'A', 'V')
                emit_favorp_head_mm(c, kbf, Pst, ps_big, msl)
                c.wait(nc.vector, 'P', 'A')
                nc.vector.reduce_max(km_t[:, :].unsqueeze(2),
                                     ps_big[:, :].rearrange("p (h j) -> p h j", h=16),
                                     axis=AXX)
                nc.vector.tensor_tensor(kmax_acc[:, :], kmax_acc[:, :], km_t[:, :], op=ALU.max)
                i = nc.vector.tensor_scalar(stab[:, :], diag_tm[:, ms, :], -0.5, 0.0,
                                            op0=ALU.mult, op1=ALU.add)
                c.inc('V', i)
                nc.vector.wait_ge(c.sems['V'], c.cnt['V'])
                i = nc.vector.tensor_tensor(
                    pre[:, :].rearrange("p (h j) -> p h j", h=16),
                    ps_big[:, :].rearrange("p (h j) -> p h j", h=16),
                    stab[:, :].unsqueeze(2).broadcast_to([128, 16, 64]), op=ALU.add)
                c.inc('V', i)
                c.wait(nc.scalar, 'V')
                i = nc.scalar.activation(keq[:, :, 0:64], pre[:, :].rearrange("p (h j) -> p h j", h=16), AF.Exp)
                c.inc('A', i)
                c.wait(nc.tensor, 'A')
                first = (ch == 0 and ms == 0)
                last = (ch == NCH - 1 and ms == 3)
                for h in range(NHEAD):
                    par = 64 * (h % 2)
                    pr = h // 2
                    nc.tensor.matmul(ps_kv[0:65, h * 64:(h + 1) * 64],
                                     keq[:, h, :],
                                     vtm[:, ms, h * 64:(h + 1) * 64],
                                     start=(first and h in (0, 8)), stop=(last and h in (7, 15)),
                                     skip_group_check=True)
                    i = nc.tensor.matmul(ps_ks[par:par + 64, pr:pr + 1],
                                         keq[:, h, 0:64], ones_col[:, :],
                                         start=(first and h == 0), stop=(last and h == 15),
                                         skip_group_check=True)
                c.inc('P', i)
        c.wait(nc.scalar, 'P')
        nc.scalar.copy(kv_ev[0:65, :], ps_kv[0:65, :])
        i = nc.scalar.copy(ks_ev[:, :], ps_ks[:, 0:8])
        c.inc('A', i)
        c.wait(nc.tensor, 'A')
        i = nc.tensor.transpose(ps_trf[0:8, 0:128], ks_ev[:, :], idf[:, :])
        c.inc('P', i)
        i = nc.tensor.transpose(ps_trf[0:16, 128:256], kmax_acc[:, :], idf[:, :])
        c.inc('P', i)
        c.wait(nc.scalar, 'P')
        nc.scalar.copy(ks_tr[:, :], ps_trf[0:8, 0:128])
        i = nc.scalar.copy(kmx_tr[:, :], ps_trf[0:16, 128:256])
        c.inc('A', i)
        c.wait(nc.vector, 'A')
        i = nc.vector.reduce_max(kmx_f[:, :], kmx_tr[:, :], axis=AXX)
        c.inc('V', i)
        c.wait(nc.sync, 'A', 'V', 'G')
        c.inc('S', nc.sync.dma_start(out=stats_d[0:64, :], in_=kv_ev[0:64, :]), 16)
        c.inc('S', nc.sync.dma_start(out=stats_d[64:72, 0:128], in_=ks_tr[:, :]), 16)
        c.inc('S', nc.sync.dma_start(out=stats_d[72:73, :], in_=kv_ev[64:65, :]), 16)
        c.inc('S', nc.sync.dma_start(out=stats_d[73:74, 0:16].transpose([1, 0]), in_=kmx_f[:, :]), 16)
        c.inc('S', nc.sync.dma_start(out=stats_d[64:72, 128:1024], in_=zerot[0:8, 0:896]), 16)
        c.inc('S', nc.sync.dma_start(out=stats_d[73:74, 16:1024], in_=zerot[0:1, 0:1008]), 16)
        c.inc('S', nc.sync.dma_start(out=stats_d[74:80, :], in_=zerot[0:6, :]), 16)
        c.wait(nc.sync, 'S')
        c.wait(nc.gpsimd, 'S')
    es.close()
    return nc


def build_L0A_apply(env=None, local_stats=False):
    nc, es, c, blk = begin(env, "pb_")
    M = 512
    xT = c.dram("xT", [D, ROWS], F32, "in")
    xhT = c.dram("xhT", [D, HDR], F32, "in")
    statsall_d = c.dram("statsall", [NCORE * STATS_ROWS, D], F32, "in")
    if not local_stats:
        w8_d = c.dram("w8", [128, 8], F32, "in")
        kmask_d = c.dram("kmask8", [8, 16], F32, "in")
    WqT_d = c.dram("WqT", [D, D], F32, "in")
    WoT_d = c.dram("WoT", [D, D], F32, "in")
    P_d = c.dram("Pstage", [128, D], F32, "in")
    bqc_d = c.dram("bqc", [128, 8], F32, "in")
    boc_d = c.dram("boc", [128, 8], F32, "in")
    dia_d = c.dram("diagones", [128, 8, 16], F32, "in")
    idf_d = c.dram("idf", [128, 128], F32, "in")
    idb_d = c.dram("idb", [128, 128], F32, "in")
    x1T_d = c.dram("x1T", [D, ROWS], F32, "out")
    x1hT_d = c.dram("x1hT", [D, HDR], F32, "out")

    f_dram = nc.dram_tensor(c.prefix + "f_dram", [1, 16], F32)
    vs_dram = nc.dram_tensor(c.prefix + "vs_dram", [1, D], F32)
    ks_dram = nc.dram_tensor(c.prefix + "ks_dram", [8, 128], F32)

    WqT = c.sb("WqT", [128, 8, D], BF16)
    WoT = c.sb("WoT", [128, 8, D], BF16)
    Pst = c.sb("Pst", [128, D], BF16)
    bqc = c.sb("bqc", [128, 8], F32)
    bqcs = c.sb("bqcs", [128, 8], F32)
    boc = c.sb("boc", [128, 8], F32)
    dia = c.sb("dia", [128, 8, 16], BF16)
    idf = c.sb("idf", [128, 128], F32)
    idb = c.sb("idb", [128, 128], BF16)
    w8 = c.sb("w8", [128, 8], F32)
    kmask = c.sb("kmask", [8, 16], F32)

    st1 = c.sb("st1", [80, D], F32)
    stsum = c.sb("stsum", [80, D], F32)
    km8 = c.sb("km8", [8, 16], F32)
    kmx_tr = c.sb("kmx_tr2", [16, 8], F32)
    stabk = c.sb("stabk", [16, 1], F32)
    fexp = c.sb("fexp", [16, 1], F32)
    kv_stage = c.sb("kv_stage", [128, 512], F32)
    f_pack = c.sb("f_pack", [128, 8], F32)
    vs_pack = c.sb("vs_pack", [128, 8, 64], F32)
    kv_sb = c.sb("kv_sb", [128, 8, 64], BF16)
    ks_t = c.sb("ks_t", [8, 128], F32)
    f_e = c.sb("f_e", [8, 1], F32)
    f_o = c.sb("f_o", [8, 1], F32)
    BDk = c.sb("BDk", [128, 8, 16], BF16)

    xb = c.sb("xb", [128, 8, M], BF16)
    xf = c.sb("xf", [128, 8, M], F32)
    xn = c.sb("xn", [128, 8, M], BF16)
    tmpf = c.sb("tmpf", [128, M], F32)
    qbf = c.sb("qbf", [128, 8, M], BF16)
    sqq = c.sb("sqq", [128, 8, M], BF16)
    diag_fm = c.sb("diag_fm", [16, M], F32)
    diag_tm = c.sb("diag_tm", [128, 4, 16], F32)
    stab = c.sb("stab", [128, 16], F32)
    pre = c.sb("pre", [128, D], F32)
    qp_tm = c.sb("qp_tm", [128, 4, D], BF16)
    qp_fm = c.sb("qp_fm", [128, 8, M], BF16)
    nrm = c.sb("nrm", [16, M], F32)
    rec_tm = c.sb("rec_tm", [128, 4, 16], F32)
    at_tm = c.sb("at_tm", [128, D], BF16)
    at_fm = c.sb("at_fm", [128, 8, M], BF16)
    raw = c.sb("raw", [128, 8, M], F32)
    rawb = c.sb("rawb", [128, 8, M], BF16)
    x1 = c.sb("x1", [128, 8, M], F32)

    ln = mk_ln(c, M)
    ps_aux = c.ps("ps_aux", [128, 512])
    ps_big = c.ps("ps_big", [128, 1024])
    ps_trb = c.ps("ps_trb", [128, 1024], BF16)
    ps_trf = c.ps("ps_trf", [128, 512])

    with blk:
        ln_init(c, ln)
        c.inc('G', nc.gpsimd.dma_start(out=WqT[:, :, :], in_=WqT_d.rearrange("(k p) n -> p k n", p=128)), 16)
        c.inc('G', nc.gpsimd.dma_start(out=WoT[:, :, :], in_=WoT_d.rearrange("(k p) n -> p k n", p=128)), 16)
        c.inc('G', nc.gpsimd.dma_start(out=Pst[:, :], in_=P_d), 16)
        c.inc('G', nc.gpsimd.dma_start(out=dia[:, :, :], in_=dia_d), 16)
        c.inc('G', nc.gpsimd.dma_start(out=idb[:, :], in_=idb_d), 16)
        c.inc('L', nc.sync.dma_start(out=idf[:, :], in_=idf_d), 16)
        c.inc('L', nc.sync.dma_start(out=bqc[:, :], in_=bqc_d), 16)
        c.inc('L', nc.sync.dma_start(out=boc[:, :], in_=boc_d), 16)
        if not local_stats:
            c.inc('L', nc.sync.dma_start(out=w8[:, :], in_=w8_d), 16)
            c.inc('L', nc.sync.dma_start(out=kmask[:, :], in_=kmask_d), 16)
        nc.scalar.wait_ge(c.sems['L'], c.cnt['L'])
        i = nc.scalar.activation(bqcs[:, :], bqc[:, :], AF.Identity, scale=np.float32(CFAV))
        c.inc('A', i)
        if local_stats:
            c.inc('L', nc.sync.dma_start(out=stsum[:, :], in_=statsall_d[0:STATS_ROWS, :]), 16)
            with nc.allow_non_contiguous_dma(reason="16-element kmax row"):
                c.inc('L', nc.sync.dma_start(out=stabk[:, :],
                                             in_=statsall_d[73:74, 0:16].transpose([1, 0])), 16)
            c.wait(nc.vector, 'L')
        else:
            sall = statsall_d.rearrange("(r s) n -> r s n", s=STATS_ROWS)
            for r in range(8):
                c.wait(nc.sync, 'V')
                c.inc('L', nc.sync.dma_start(out=st1[:, :], in_=sall[r:r + 1, :, :].squeeze(0)), 16)
                c.wait(nc.vector, 'L')
                if r == 0:
                    i = nc.vector.tensor_scalar(stsum[:, :], st1[:, :], w8[0:80, 0:1], 0.0,
                                                op0=ALU.mult, op1=ALU.add)
                else:
                    i = nc.vector.scalar_tensor_tensor(stsum[:, :], st1[:, :], w8[0:80, r:r + 1],
                                                       stsum[:, :], op0=ALU.mult, op1=ALU.add)
                c.inc('V', i)
            c.inc('L', nc.sync.dma_start(
                out=km8[:, :],
                in_=statsall_d.rearrange("(r s) n -> r s n", s=STATS_ROWS)[:, 73:74, 0:16].squeeze(1)), 16)
            c.wait(nc.vector, 'L')
            i = nc.vector.tensor_tensor(km8[:, :], km8[:, :], kmask[:, :], op=ALU.add)
            c.inc('V', i)
            c.wait(nc.tensor, 'V', 'L')
            i = nc.tensor.transpose(ps_trf[0:16, 0:8], km8[:, :], idf[0:8, 0:8])
            c.inc('P', i)
            c.wait(nc.scalar, 'P')
            i = nc.scalar.copy(kmx_tr[:, :], ps_trf[0:16, 0:8])
            c.inc('A', i)
            c.wait(nc.vector, 'A')
            i = nc.vector.reduce_max(stabk[:, :], kmx_tr[:, :], axis=AXX)
            c.inc('V', i)
        c.wait(nc.scalar, 'V', 'L')
        i = nc.scalar.activation(fexp[:, :], stabk[:, :], AF.Exp, scale=-1.0)
        c.inc('A', i)
        c.wait(nc.gpsimd, 'A', 'V')
        stk = stsum[0:64, :].rearrange("p (a q x) -> p a q x", a=8, q=2)
        c.inc('G', nc.gpsimd.dma_start(
            out=kv_stage[0:64, :].rearrange("p (a x) -> p a x", a=8),
            in_=stk[:, :, 0:1, :].squeeze(2)), 16)
        c.inc('G', nc.gpsimd.dma_start(
            out=kv_stage[64:128, :].rearrange("p (a x) -> p a x", a=8),
            in_=stk[:, :, 1:2, :].squeeze(2)), 16)
        c.inc('G', nc.gpsimd.dma_start(out=f_dram.ap().transpose([1, 0]), in_=fexp[:, :]), 16)
        c.inc('G', nc.gpsimd.dma_start(out=ks_dram[:, :], in_=stsum[64:72, 0:128]), 16)
        c.inc('G', nc.gpsimd.dma_start(out=vs_dram[:, :], in_=stsum[72:73, :]), 16)
        c.wait(nc.gpsimd, 'G')
        fap = f_dram.ap().squeeze(0).rearrange("(a p) -> a p", a=8)
        with nc.allow_non_contiguous_dma(reason="16-element parity unpack"):
            c.inc('G', nc.gpsimd.dma_start(
                out=f_pack[0:64, :].unsqueeze(2),
                in_=fap[:, 0:1].unsqueeze(0).partition_broadcast(64).squeeze(1)), 16)
            c.inc('G', nc.gpsimd.dma_start(
                out=f_pack[64:128, :].unsqueeze(2),
                in_=fap[:, 1:2].unsqueeze(0).partition_broadcast(64).squeeze(1)), 16)
            c.inc('G', nc.gpsimd.dma_start(out=f_e[:, :], in_=fap[:, 0:1]), 16)
            c.inc('G', nc.gpsimd.dma_start(out=f_o[:, :], in_=fap[:, 1:2]), 16)
        vap = vs_dram.ap().squeeze(0).rearrange("(a p x) -> a p x", a=8, p=2)
        c.inc('G', nc.gpsimd.dma_start(
            out=vs_pack[0:64, :, :], in_=vap[:, 0:1, :].squeeze(1).unsqueeze(0).partition_broadcast(64).squeeze(1)), 16)
        c.inc('G', nc.gpsimd.dma_start(
            out=vs_pack[64:128, :, :], in_=vap[:, 1:2, :].squeeze(1).unsqueeze(0).partition_broadcast(64).squeeze(1)), 16)
        c.inc('G', nc.gpsimd.dma_start(out=ks_t[:, :], in_=ks_dram[:, :]), 16)
        c.wait(nc.vector, 'G')
        nc.vector.tensor_scalar(vs_pack[:, :, :], vs_pack[:, :, :], float(EPS_FEAT), 0.0,
                                op0=ALU.mult, op1=ALU.add)
        nc.vector.tensor_tensor(
            kv_stage[:, :].rearrange("p (a x) -> p a x", a=8),
            kv_stage[:, :].rearrange("p (a x) -> p a x", a=8),
            f_pack[:, :].unsqueeze(2).broadcast_to([128, 8, 64]), op=ALU.mult)
        i = nc.vector.tensor_tensor(kv_sb[:, :, :],
                                    kv_stage[:, :].rearrange("p (a x) -> p a x", a=8),
                                    vs_pack[:, :, :], op=ALU.add)
        nc.vector.tensor_scalar(ks_t[:, 0:64], ks_t[:, 0:64], f_e[0:8, 0:1],
                                float(EPS_FEAT * 16384.0), op0=ALU.mult, op1=ALU.add)
        i = nc.vector.tensor_scalar(ks_t[:, 64:128], ks_t[:, 64:128], f_o[0:8, 0:1],
                                    float(EPS_FEAT * 16384.0), op0=ALU.mult, op1=ALU.add)
        c.inc('V', i)
        i = nc.gpsimd.memset(BDk[:, :, :], 0.0)
        c.inc('G', i)
        c.wait(nc.gpsimd, 'V')
        c.inc('G', nc.gpsimd.dma_start(out=ks_dram[:, :], in_=ks_t[:, :]), 16)
        c.wait(nc.gpsimd, 'G')
        for t in range(8):
            for par in range(2):
                i = nc.gpsimd.dma_start(
                    out=BDk[64 * par:64 * par + 64, t, 2 * t + par:2 * t + par + 1],
                    in_=ks_dram.ap()[t:t + 1, 64 * par:64 * par + 64].transpose([1, 0]))
                c.inc('G', i, 16)

        xr = xT.rearrange("(k p) n -> p k n", p=128)
        x1r = x1T_d.rearrange("(k p) n -> p k n", p=128)
        xhr = xhT.rearrange("(k p) n -> p k n", p=128)
        x1hr = x1hT_d.rearrange("(k p) n -> p k n", p=128)
        for ch in range(9):
            hdr = (ch == 8)
            m = 128 if hdr else M
            nms = m // 128
            sl = None if hdr else slice(ch * M, (ch + 1) * M)
            if ch:
                c.wait(nc.gpsimd, 'V')
                c.wait(nc.sync, 'V')
            emit_chunk_load(c, xb, xf, m, xr=xr, sl=sl, xhr=(xhr if hdr else None))
            emit_ln(c, ln, [xb[:, t, 0:m] for t in range(8)], m)
            c.wait(nc.vector, 'L')
            emit_normalize(c, ln, [xb[:, t, 0:m] for t in range(8)],
                           [xn[:, t, 0:m] for t in range(8)], m, tmpf)
            c.wait(nc.tensor, 'V', 'A')
            emit_proj_fm(c, WqT, xn, qbf, m, ps_big, bias_col=bqcs, scale=CFAV)
            emit_diag(c, dia, sqq, qbf, diag_fm, diag_tm, idf, ps_trf, ps_trf, m)
            for ms in range(nms):
                msl = slice(ms * 128, ms * 128 + 128)
                c.wait(nc.tensor, 'A', 'V')
                emit_favorp_head_mm(c, qbf, Pst, ps_big, msl)
                c.wait(nc.vector, 'P', 'A')
                nc.vector.reduce_max(stab[:, :].unsqueeze(2),
                                     ps_big[:, :].rearrange("p (h j) -> p h j", h=16),
                                     axis=AXX)
                i = nc.vector.scalar_tensor_tensor(stab[:, :], diag_tm[:, ms, :], -0.5,
                                                   stab[:, :], op0=ALU.mult, op1=ALU.subtract)
                c.inc('V', i)
                nc.vector.wait_ge(c.sems['V'], c.cnt['V'])
                i = nc.vector.tensor_tensor(
                    pre[:, :].rearrange("p (h j) -> p h j", h=16),
                    ps_big[:, :].rearrange("p (h j) -> p h j", h=16),
                    stab[:, :].unsqueeze(2).broadcast_to([128, 16, 64]), op=ALU.add)
                c.inc('V', i)
                c.wait(nc.scalar, 'V')
                i = nc.scalar.activation(pre[:, :], pre[:, :], AF.Exp)
                c.inc('A', i)
                c.wait(nc.vector, 'A')
                i = nc.vector.tensor_scalar(qp_tm[:, ms, :], pre[:, :], 1.0, float(EPS_FEAT),
                                            op0=ALU.mult, op1=ALU.add)
                c.inc('V', i)
                c.wait(nc.tensor, 'V')
                for t in range(8):
                    i = nc.tensor.transpose(ps_trb[:, t * 128:(t + 1) * 128],
                                            qp_tm[:, ms, t * 128:(t + 1) * 128], idb[:, :])
                c.inc('P', i)
                c.wait(nc.vector, 'P')
                for t in range(8):
                    i = nc.vector.tensor_copy(qp_fm[:, t, msl], ps_trb[:, t * 128:(t + 1) * 128])
                c.inc('V', i)
            c.wait(nc.tensor, 'V')
            for t in range(8):
                i = nc.tensor.matmul(ps_aux[32:48, 0:m], BDk[:, t, :], qp_fm[:, t, 0:m],
                                     start=(t == 0), stop=(t == 7), skip_group_check=True)
            c.inc('P', i)
            c.wait(nc.scalar, 'P')
            i = nc.scalar.copy(nrm[:, 0:m], ps_aux[32:48, 0:m])
            c.inc('A', i)
            c.wait(nc.vector, 'A')
            nc.vector.tensor_scalar_max(nrm[:, 0:m], nrm[:, 0:m], float(EPS_DEN))
            i = nc.vector.reciprocal(nrm[:, 0:m], nrm[:, 0:m])
            c.inc('V', i)
            c.wait(nc.tensor, 'V')
            for ms in range(nms):
                i = nc.tensor.transpose(ps_trf[:, 256 + ms * 16:256 + ms * 16 + 16],
                                        nrm[:, ms * 128:(ms + 1) * 128], idf[0:16, 0:16])
                c.inc('P', i)
            c.wait(nc.scalar, 'P')
            for ms in range(nms):
                i = nc.scalar.copy(rec_tm[:, ms, :], ps_trf[:, 256 + ms * 16:256 + ms * 16 + 16])
            c.inc('A', i)
            for ms in range(nms):
                msl = slice(ms * 128, ms * 128 + 128)
                c.wait(nc.tensor, 'A', 'V')
                for par_pass in range(2):
                    for h in range(par_pass, NHEAD, 2):
                        par = 64 * (h % 2)
                        i = nc.tensor.matmul(ps_big[:, h * 64:(h + 1) * 64],
                                             qp_fm[par:par + 64, h // 2, msl],
                                             kv_sb[par:par + 64, h // 2, :],
                                             start=(h in (0, 8)), stop=(h in (7, 15)),
                                             skip_group_check=True)
                    if par_pass == 0:
                        c.inc('P', i)
                        nc.tensor.wait_ge(c.sems['P'], c.cnt['P'])
                c.inc('P', i)
                c.wait(nc.vector, 'P', 'A')
                i = nc.vector.tensor_tensor(
                    at_tm[:, :].rearrange("p (h j) -> p h j", h=16),
                    ps_big[:, :].rearrange("p (h j) -> p h j", h=16),
                    rec_tm[:, ms, :].unsqueeze(2).broadcast_to([128, 16, 64]), op=ALU.mult)
                c.inc('V', i)
                c.wait(nc.tensor, 'V')
                for t in range(8):
                    i = nc.tensor.transpose(ps_trb[:, t * 128:(t + 1) * 128],
                                            at_tm[:, t * 128:(t + 1) * 128], idb[:, :])
                c.inc('P', i)
                c.wait(nc.vector, 'P')
                for t in range(8):
                    i = nc.vector.tensor_copy(at_fm[:, t, msl], ps_trb[:, t * 128:(t + 1) * 128])
                c.inc('V', i)
            c.wait(nc.tensor, 'V')
            emit_proj_fm2(c, WoT, at_fm, raw, rawb, m, ps_big, bias_col=boc)
            nc.vector.wait_ge(c.sems['L'], c.cnt['L'])
            c.wait(nc.scalar, 'V')
            emit_ln(c, ln, [rawb[:, t, 0:m] for t in range(8)], m)
            c.wait(nc.vector, 'L')
            emit_normalize(c, ln, [raw[:, t, 0:m] for t in range(8)], None, m, tmpf,
                           add_tiles=[xf[:, t, 0:m] for t in range(8)],
                           out3=[x1[:, t, 0:m] for t in range(8)])
            c.wait(nc.sync, 'V')
            if hdr:
                c.inc('S', nc.sync.dma_start(out=x1hr, in_=x1[:, :, 0:HDR]), 16)
            else:
                c.inc('S', nc.sync.dma_start(out=x1r[:, :, sl], in_=x1[:, :, :]), 16)
        c.wait(nc.sync, 'S')
        c.wait(nc.gpsimd, 'S')
    es.close()
    return nc


def build_L0F(env=None):
    nc, es, c, blk = begin(env, "pc_")
    M = 256
    H = 3072
    HT = H // 128
    xT = c.dram("xT", [D, ROWS], F32, "in")
    xhT = c.dram("xhT", [D, HDR], F32, "in")
    wgT_d = c.dram("wgT", [D, H], I32, "in")
    wuT_d = c.dram("wuT", [D, H], I32, "in")
    wdT_d = c.dram("wdT", [H, D], I32, "in")
    x2T_d = c.dram("x2T", [D, ROWS], F32, "out")
    x2hT_d = c.dram("x2hT", [D, HDR], F32, "out")

    wg = c.sb("wg", [128, 8, H], BF16)
    wu = c.sb("wu", [128, 8, H], BF16)
    wd = c.sb("wd", [128, HT, D], BF16)
    xb = c.sb("xb", [128, 8, M], BF16)
    xf = c.sb("xf", [128, 8, M], F32)
    xn = c.sb("xn", [128, 8, M], BF16)
    tmpf = c.sb("tmpf", [128, M], F32)
    sil = c.sb("sil", [128, 2, M], F32)
    hid = c.sb("hid", [128, HT, M], BF16)
    raw = c.sb("raw", [128, 8, M], F32)
    rawb = c.sb("rawb", [128, 8, M], BF16)
    ln = mk_ln(c, M)
    ln2 = dict(ln, bc=c.sb("ln2_bc", [128, 2, M], F32),
               ps_stat=c.ps("ln2_psstat", [128, 512], F32))
    ps_g = [c.ps("ps_g0", [128, 512]), c.ps("ps_g1", [128, 512])]
    ps_u = [c.ps("ps_u0", [128, 512]), c.ps("ps_u1", [128, 512])]
    ps_d = [c.ps("ps_d0", [128, 512]), c.ps("ps_d1", [128, 512])]

    def gate_up(m):
        vhist = []
        for ht in range(HT):
            hsl = slice(ht * 128, (ht + 1) * 128)
            pg, pu = ps_g[ht % 2], ps_u[ht % 2]
            if ht >= 2:
                nc.tensor.wait_ge(c.sems['V'], vhist[ht - 2])
            for k in range(8):
                nc.tensor.matmul(pg[:, 0:m], wg[:, k, hsl], xn[:, k, 0:m],
                                 start=(k == 0), stop=(k == 7), skip_group_check=True)
            for k in range(8):
                i = nc.tensor.matmul(pu[:, 0:m], wu[:, k, hsl], xn[:, k, 0:m],
                                     start=(k == 0), stop=(k == 7), skip_group_check=True)
            c.inc('P', i)
            c.wait(nc.scalar, 'P')
            i = nc.scalar.activation(sil[:, ht % 2, 0:m], pg[:, 0:m], AF.Silu,
                                     scale=np.float32(SCALE_FFN0))
            c.inc('A', i)
            c.wait(nc.vector, 'A')
            i = nc.vector.scalar_tensor_tensor(hid[:, ht, 0:m], pu[:, 0:m], SCALE_FFN0,
                                               sil[:, ht % 2, 0:m], op0=ALU.mult, op1=ALU.mult)
            c.inc('V', i)
            vhist.append(c.cnt['V'])

    def down(m):
        c.wait(nc.tensor, 'V')
        dvh = []
        for t in range(8):
            dsl = slice(t * 128, (t + 1) * 128)
            pd = ps_d[t % 2]
            if t >= 2:
                nc.tensor.wait_ge(c.sems['V'], dvh[t - 2])
            for ht in range(HT):
                i = nc.tensor.matmul(pd[:, 0:m], wd[:, ht, dsl], hid[:, ht, 0:m],
                                     start=(ht == 0), stop=(ht == HT - 1), skip_group_check=True)
            c.inc('P', i)
            c.wait(nc.scalar, 'P')
            i = nc.scalar.activation(raw[:, t, 0:m], pd[:, 0:m], AF.Identity,
                                     scale=np.float32(SCALE_FFN0))
            c.inc('A', i)
            c.wait(nc.vector, 'A')
            i = nc.vector.tensor_copy(rawb[:, t, 0:m], raw[:, t, 0:m])
            c.inc('V', i)
            dvh.append(c.cnt['V'])

    with blk:
        ln_init(c, ln)
        c.inc('G', nc.gpsimd.dma_start(out=wg[:, :, :], in_=wgT_d.rearrange("(k p) h -> p k h", p=128)), 16)
        c.inc('G', nc.gpsimd.dma_start(out=wu[:, :, :], in_=wuT_d.rearrange("(k p) h -> p k h", p=128)), 16)
        c.inc('G', nc.gpsimd.dma_start(out=wd[:, :, :], in_=wdT_d.rearrange("(k p) h -> p k h", p=128)), 16)

        xr = xT.rearrange("(k p) n -> p k n", p=128)
        x2r = x2T_d.rearrange("(k p) n -> p k n", p=128)
        xhr = xhT.rearrange("(k p) n -> p k n", p=128)
        x2hr = x2hT_d.rearrange("(k p) n -> p k n", p=128)
        NCH = 16
        c.inc('G', nc.gpsimd.dma_start(out=xb[:, :, :], in_=xr[:, :, 0:M]), 16)
        c.inc('L', nc.sync.dma_start(out=xf[:, :, :], in_=xr[:, :, 0:M]), 16)
        nc.scalar.wait_ge(c.sems['G'], c.cnt['G'])
        nc.tensor.wait_ge(c.sems['G'], c.cnt['G'])
        nc.vector.wait_ge(c.sems['G'], c.cnt['G'])
        emit_ln(c, ln, [xb[:, t, :] for t in range(8)], M)
        for ch in range(NCH):
            sl = slice(ch * M, (ch + 1) * M)
            c.wait(nc.vector, 'L')
            emit_normalize(c, ln, [xb[:, t, :] for t in range(8)],
                           [xn[:, t, :] for t in range(8)], M, tmpf)
            v_norm = c.cnt['V']
            c.wait(nc.tensor, 'V')
            gate_up(M)
            if ch + 1 < NCH:
                nsl = slice((ch + 1) * M, (ch + 2) * M)
                nc.gpsimd.wait_ge(c.sems['V'], v_norm)
                c.inc('G', nc.gpsimd.dma_start(out=xb[:, :, :], in_=xr[:, :, nsl]), 16)
                nc.scalar.wait_ge(c.sems['G'], c.cnt['G'])
                nc.tensor.wait_ge(c.sems['G'], c.cnt['G'])
                nc.vector.wait_ge(c.sems['G'], c.cnt['G'])
                emit_ln(c, ln, [xb[:, t, :] for t in range(8)], M)
            down(M)
            c.wait(nc.scalar, 'V', 'L')
            emit_ln(c, ln2, [rawb[:, t, :] for t in range(8)], M)
            c.wait(nc.vector, 'L')
            emit_normalize(c, ln2, [raw[:, t, :] for t in range(8)], None, M, tmpf,
                           add_tiles=[xf[:, t, :] for t in range(8)],
                           out3=[xf[:, t, :] for t in range(8)])
            c.wait(nc.sync, 'V')
            c.inc('S', nc.sync.dma_start(out=x2r[:, :, sl], in_=xf[:, :, :]), 16)
            if ch + 1 < NCH:
                c.wait(nc.sync, 'S')
                c.inc('L', nc.sync.dma_start(out=xf[:, :, :], in_=xr[:, :, nsl]), 16)
        m = HDR * 8
        c.wait(nc.sync, 'V', 'S')
        c.wait(nc.gpsimd, 'V', 'S')
        emit_chunk_load(c, xb, xf, m, xhr=xhr)
        emit_ln(c, ln, [xb[:, t, 0:m] for t in range(8)], m)
        c.wait(nc.vector, 'L')
        emit_normalize(c, ln, [xb[:, t, 0:m] for t in range(8)],
                       [xn[:, t, 0:m] for t in range(8)], m, tmpf)
        c.wait(nc.tensor, 'V')
        gate_up(m)
        down(m)
        c.wait(nc.scalar, 'V', 'L')
        emit_ln(c, ln2, [rawb[:, t, 0:m] for t in range(8)], m)
        c.wait(nc.vector, 'L')
        emit_normalize(c, ln2, [raw[:, t, 0:m] for t in range(8)], None, m, tmpf,
                       add_tiles=[xf[:, t, 0:m] for t in range(8)],
                       out3=[xf[:, t, 0:m] for t in range(8)])
        c.wait(nc.sync, 'V')
        c.inc('S', nc.sync.dma_start(out=x2hr, in_=xf[:, :, 0:HDR]), 16)
        c.wait(nc.sync, 'S')
        c.wait(nc.gpsimd, 'S')
    es.close()
    return nc


def build_L1A(env=None):
    nc, es, c, blk = begin(env, "pd_")
    M = 512
    xT = c.dram("xT", [D, ROWS], F32, "in")
    xhT = c.dram("xhT", [D, HDR], F32, "in")
    WaqT_d = c.dram("WaqT", [D, D], F32, "in")
    WakvT_d = c.dram("WakvT", [D, 2 * D], F32, "in")
    WpT_d = c.dram("WpT", [D, D], F32, "in")
    mask_d = c.dram("l1mask", [128, 256], F32, "in")
    idb_d = c.dram("idb", [128, 128], F32, "in")
    x3T_d = c.dram("x3T", [D, ROWS], F32, "out")
    vh_dram = nc.dram_tensor(c.prefix + "vh_dram", [HDR, D], F32)

    Waq = c.sb("Waq", [128, 8, D], BF16)
    WpT = c.sb("WpT", [128, 8, D], BF16)
    mask = c.sb("mask", [128, 256], F32)
    idb = c.sb("idb", [128, 128], BF16)
    khdr = c.sb("khdr", [128, 8, HDR], BF16)
    vblk = c.sb("vblk", [128, 2, 512], BF16)
    xb = c.sb("xb", [128, 8, M], BF16)
    xf = c.sb("xf", [128, 8, M], F32)
    xn = c.sb("xn", [128, 8, M], BF16)
    tmpf = c.sb("tmpf", [128, M], F32)
    qfm = c.sb("qfm", [128, 8, M], BF16)
    lg = c.sb("lg", [128, 256], F32)
    mx = c.sb("mx", [128, 16], F32)
    we = c.sb("we", [128, 256], BF16)
    sm = c.sb("sm", [128, 16], F32)
    wn = c.sb("wn", [128, 256], BF16)
    wfm = c.sb("wfm", [128, 2, 128], BF16)
    ofm = c.sb("ofm", [128, 8, M], BF16)
    raw = c.sb("raw", [128, 8, M], F32)
    rawb = c.sb("rawb", [128, 8, M], BF16)
    x3 = c.sb("x3", [128, 8, M], F32)
    ln = mk_ln(c, M)
    ps_big = c.ps("ps_big", [128, 1024])
    ps_lg = c.ps("ps_lg", [128, 512])
    ps_trb = c.ps("ps_trb", [128, 1024], BF16)
    ps_o = c.ps("ps_o", [128, 1024])

    with blk:
        ln_init(c, ln)
        c.inc('G', nc.gpsimd.dma_start(out=Waq[:, :, :], in_=WaqT_d.rearrange("(k p) n -> p k n", p=128)), 16)
        c.inc('G', nc.gpsimd.dma_start(out=WpT[:, :, :], in_=WpT_d.rearrange("(k p) n -> p k n", p=128)), 16)
        c.inc('G', nc.gpsimd.dma_start(out=idb[:, :], in_=idb_d), 16)
        c.inc('L', nc.sync.dma_start(out=mask[:, :], in_=mask_d), 16)

        xr = xT.rearrange("(k p) n -> p k n", p=128)
        x3r = x3T_d.rearrange("(k p) n -> p k n", p=128)
        xhr = xhT.rearrange("(k p) n -> p k n", p=128)

        with nc.sbuf_tensor(c.prefix + "Wakv", [128, 8, 2 * D], BF16) as Wakv:
            c.inc('G', nc.gpsimd.dma_start(out=Wakv[:, :, :],
                                           in_=WakvT_d.rearrange("(k p) n -> p k n", p=128)), 16)
            emit_chunk_load(c, xb, xf, 128, xhr=xhr)
            emit_ln(c, ln, [xb[:, t, 0:128] for t in range(8)], 128)
            c.wait(nc.vector, 'L')
            emit_normalize(c, ln, [xb[:, t, 0:128] for t in range(8)],
                           [xn[:, t, 0:128] for t in range(8)], 128, tmpf)
            c.wait(nc.tensor, 'V')
            for t in range(8):
                for k in range(8):
                    i = nc.tensor.matmul(ps_big[:, t * 64:t * 64 + HDR],
                                         Wakv[:, k, t * 128:(t + 1) * 128],
                                         xn[:, k, 0:HDR], start=(k == 0), stop=(k == 7),
                                         skip_group_check=True)
                c.inc('P', i)
            c.wait(nc.scalar, 'P')
            for t in range(8):
                i = nc.scalar.copy(khdr[:, t, :], ps_big[:, t * 64:t * 64 + HDR])
            c.inc('A', i)
            c.wait(nc.tensor, 'A')
            for half in range(2):
                for k in range(8):
                    i = nc.tensor.matmul(ps_lg[0:HDR, :], xn[:, k, 0:HDR],
                                         Wakv[:, k, D + half * 512: D + half * 512 + 512],
                                         start=(k == 0), stop=(k == 7), skip_group_check=True)
                c.inc('P', i)
                c.wait(nc.scalar, 'P')
                i = nc.scalar.copy(tmpf[0:HDR, 0:512], ps_lg[0:HDR, :])
                c.inc('A', i)
                c.wait(nc.sync, 'A')
                c.inc('S', nc.sync.dma_start(out=vh_dram[:, half * 512:half * 512 + 512],
                                             in_=tmpf[0:HDR, 0:512]), 16)
                nc.tensor.wait_ge(c.sems['S'], c.cnt['S'])
            i = nc.gpsimd.memset(vblk[:, :, :], 0.0)
            c.inc('G', i)
            c.wait(nc.gpsimd, 'S')
            vap = vh_dram.ap().rearrange("t (h x) -> t h x", h=16)
            for g in range(2):
                for hp in range(8):
                    i = nc.gpsimd.dma_start(
                        out=vblk[16 * hp:16 * hp + 16, g, 64 * hp:64 * hp + 64],
                        in_=vap[:, 8 * g + hp, :])
                    c.inc('G', i, 16)

        for ch in range(8):
            sl = slice(ch * M, (ch + 1) * M)
            c.wait(nc.gpsimd, 'V')
            c.wait(nc.sync, 'V')
            emit_chunk_load(c, xb, xf, M, xr=xr, sl=sl)
            emit_ln(c, ln, [xb[:, t, :] for t in range(8)], M)
            c.wait(nc.vector, 'L')
            emit_normalize(c, ln, [xb[:, t, :] for t in range(8)],
                           [xn[:, t, :] for t in range(8)], M, tmpf)
            c.wait(nc.tensor, 'V', 'A')
            emit_proj_fm(c, Waq, xn, qfm, M, ps_big)
            for ms in range(4):
                msl = slice(ms * 128, ms * 128 + 128)
                c.wait(nc.tensor, 'A', 'V')
                for par_pass in range(2):
                    for h in range(par_pass, NHEAD, 2):
                        par = 64 * (h % 2)
                        i = nc.tensor.matmul(ps_lg[:, h * 16:(h + 1) * 16],
                                             qfm[par:par + 64, h // 2, msl],
                                             khdr[par:par + 64, h // 2, :],
                                             start=(h == 0), stop=(h == 15),
                                             skip_group_check=True)
                    if par_pass == 0:
                        c.inc('P', i)
                        nc.tensor.wait_ge(c.sems['P'], c.cnt['P'])
                c.inc('P', i)
                c.wait(nc.vector, 'P')
                if ch == 0 and ms == 0:
                    i = nc.vector.tensor_tensor(lg[:, :], ps_lg[:, 0:256], mask[:, :], op=ALU.add)
                else:
                    i = nc.vector.tensor_copy(lg[:, :], ps_lg[:, 0:256])
                c.inc('V', i)
                i = nc.vector.reduce_max(mx[:, :].unsqueeze(2),
                                         lg[:, :].rearrange("p (h j) -> p h j", h=16), axis=AXX)
                c.inc('V', i)
                nc.vector.wait_ge(c.sems['V'], c.cnt['V'])
                i = nc.vector.tensor_tensor(
                    lg[:, :].rearrange("p (h j) -> p h j", h=16),
                    lg[:, :].rearrange("p (h j) -> p h j", h=16),
                    mx[:, :].unsqueeze(2).broadcast_to([128, 16, 16]), op=ALU.subtract)
                c.inc('V', i)
                c.wait(nc.scalar, 'V')
                i = nc.scalar.activation(we[:, :], lg[:, :], AF.Exp, scale=0.125)
                c.inc('A', i)
                c.wait(nc.vector, 'A')
                i = nc.vector.reduce_sum(sm[:, :].unsqueeze(2),
                                         we[:, :].rearrange("p (h j) -> p h j", h=16), axis=AXX)
                c.inc('V', i)
                nc.vector.wait_ge(c.sems['V'], c.cnt['V'])
                i = nc.vector.reciprocal(sm[:, :], sm[:, :])
                c.inc('V', i)
                nc.vector.wait_ge(c.sems['V'], c.cnt['V'])
                i = nc.vector.tensor_tensor(
                    wn[:, :].rearrange("p (h j) -> p h j", h=16),
                    we[:, :].rearrange("p (h j) -> p h j", h=16),
                    sm[:, :].unsqueeze(2).broadcast_to([128, 16, 16]), op=ALU.mult)
                c.inc('V', i)
                c.wait(nc.tensor, 'V')
                for g in range(2):
                    i = nc.tensor.transpose(ps_trb[:, g * 128:(g + 1) * 128],
                                            wn[:, g * 128:(g + 1) * 128], idb[:, :])
                c.inc('P', i)
                c.wait(nc.vector, 'P')
                for g in range(2):
                    i = nc.vector.tensor_copy(wfm[:, g, :], ps_trb[:, g * 128:(g + 1) * 128])
                c.inc('V', i)
                c.wait(nc.tensor, 'V')
                for t in range(8):
                    g = t // 4
                    i = nc.tensor.matmul(ps_o[:, t * 128:(t + 1) * 128],
                                         vblk[:, g, (t % 4) * 128:(t % 4) * 128 + 128],
                                         wfm[:, g, :],
                                         start=(t in (0, 4)), stop=(t in (3, 7)),
                                         skip_group_check=True)
                c.inc('P', i)
                c.wait(nc.scalar, 'P')
                for t in range(8):
                    i = nc.scalar.copy(ofm[:, t, msl], ps_o[:, t * 128:(t + 1) * 128])
                c.inc('A', i)
                nc.tensor.wait_ge(c.sems['A'], c.cnt['A'])
            c.wait(nc.tensor, 'A')
            emit_proj_fm2(c, WpT, ofm, raw, rawb, M, ps_big)
            nc.vector.wait_ge(c.sems['L'], c.cnt['L'])
            c.wait(nc.scalar, 'V')
            emit_ln(c, ln, [rawb[:, t, :] for t in range(8)], M)
            c.wait(nc.vector, 'L')
            emit_normalize(c, ln, [raw[:, t, :] for t in range(8)], None, M, tmpf,
                           add_tiles=[xf[:, t, :] for t in range(8)],
                           out3=[x3[:, t, :] for t in range(8)])
            c.wait(nc.sync, 'V')
            c.inc('S', nc.sync.dma_start(out=x3r[:, :, sl], in_=x3[:, :, :]), 16)
        c.wait(nc.sync, 'S')
        c.wait(nc.gpsimd, 'S')
    es.close()
    return nc


def build_L1F(env=None):
    nc, es, c, blk = begin(env, "pe_")
    M = 256
    H = 4096
    HT = H // 128
    xT = c.dram("xT", [D, ROWS], F32, "in")
    w1T_d = c.dram("w1T", [D, H], F32, "in")
    w2T_d = c.dram("w2T", [H, D], F32, "in")
    b1c_d = c.dram("b1c", [128, 32], F32, "in")
    b2c_d = c.dram("b2c", [128, 8], F32, "in")
    idf_d = c.dram("idf", [128, 128], F32, "in")
    y_d = c.dram("y", [ROWS, D], F32, "out")
    a1_dram = nc.dram_tensor(c.prefix + "a1_dram", [32, 128], F32)
    a2_dram = nc.dram_tensor(c.prefix + "a2_dram", [8, 128], F32)

    w1s = c.sb("w1s", [128, 8, H], BF16)
    w2s = c.sb("w2s", [128, HT, D], BF16)
    b1c = c.sb("b1c", [128, 32], F32)
    b2c = c.sb("b2c", [128, 8], F32)
    a1c = c.sb("a1c", [128, 32], F32)
    a2c = c.sb("a2c", [128, 8], F32)
    idf = c.sb("idf", [128, 128], F32)
    ones_col = c.sb("ones_colc", [128, 1], BF16)
    stage = c.sb("stage", [1, 512], F32)
    stage8 = c.sb("stage8", [8, 512], F32)
    sel8 = c.sb("sel8", [128, 8, 8], BF16)
    a1t = c.sb("a1t", [32, 128], F32)
    a2t = c.sb("a2t", [8, 128], F32)
    xb = c.sb("xb", [128, 8, M], BF16)
    xf = c.sb("xf", [128, 8, M], F32)
    xn = c.sb("xn", [128, 8, M], BF16)
    tmpf = c.sb("tmpf", [128, M], F32)
    hid = c.sb("hid", [128, HT, M], BF16)
    raw = c.sb("raw", [128, 8, M], F32)
    rawb = c.sb("rawb", [128, 8, M], BF16)
    x4 = c.sb("x4", [128, 8, M], F32)
    ytm = c.sb("ytm", [128, D], F32)
    ln = mk_ln(c, M)
    ps_h = [c.ps("ps_h0", [128, 512]), c.ps("ps_h1", [128, 512]),
            c.ps("ps_h2", [128, 512]), c.ps("ps_h3", [128, 512])]
    ps_d = [c.ps("ps_d0", [128, 512]), c.ps("ps_d1", [128, 512])]
    ps_tf = c.ps("ps_tf", [128, 512])
    ln2 = dict(ln, bc=c.sb("ln2_bc", [128, 2, M], F32), ps_stat=ps_tf)

    with blk:
        ln_init(c, ln)
        i = nc.gpsimd.memset(ones_col[:, :], 2.0 ** -10)
        c.inc('G', i)
        c.inc('L', nc.sync.dma_start(out=b1c[:, :], in_=b1c_d), 16)
        c.inc('L', nc.sync.dma_start(out=b2c[:, :], in_=b2c_d), 16)
        c.inc('L', nc.sync.dma_start(out=idf[:, :], in_=idf_d), 16)
        wst = [raw[:, :, :].rearrange("p a b -> p (a b)"),
               x4[:, :, :].rearrange("p a b -> p (a b)")]
        wab = [hid[:, 0:8, :].rearrange("p a b -> p (a b)"),
               hid[:, 8:16, :].rearrange("p a b -> p (a b)")]
        nc.gpsimd.memset(sel8[:, :, :], 0.0)
        for s in range(8):
            i = nc.gpsimd.memset(sel8[:, s, s:s + 1], 2.0 ** -10)
        c.inc('G', i)
        w1r = w1T_d.rearrange("(k p) h -> p k h", p=128)
        w2r = w2T_d.rearrange("(k p) h -> p k h", p=128)
        ahist, vhist, phist = [], [], []
        qsn = []
        for kk in range(16 + HT):
            w1p = kk < 16
            if w1p:
                k, half = kk // 2, kk % 2
                hs = slice(half * 2048, half * 2048 + 2048)
                wsl = slice(0, 2048)
            else:
                k = kk - 16
                wsl = slice(0, D)
            b = kk % 2
            deng, dsem = (nc.gpsimd, 'G') if b == 0 else (nc.sync, 'L')
            if kk >= 2:
                deng.wait_ge(c.sems['A'], ahist[kk - 2])
                deng.wait_ge(c.sems['V'], vhist[kk - 2])
            src = w1r[:, k, hs] if w1p else w2r[:, k, :]
            c.inc(dsem, deng.dma_start(out=wst[b][:, wsl], in_=src), 16)
            qsn.append((dsem, c.cnt[dsem]))
            nc.scalar.wait_ge(c.sems[dsem], c.cnt[dsem])
            dst = w1s[:, k, hs] if w1p else w2s[:, k, :]
            i = nc.scalar.activation(dst, wst[b][:, wsl], AF.Sign)
            c.inc('A', i)
            ahist.append(c.cnt['A'])
            nc.vector.wait_ge(c.sems[dsem], qsn[kk][1])
            if kk >= 2:
                nc.vector.wait_ge(c.sems['P'], phist[kk - 2])
            i = nc.vector.scalar_tensor_tensor(wab[b][:, wsl], wst[b][:, wsl], -1.0,
                                               wst[b][:, wsl], op0=ALU.mult, op1=ALU.max)
            c.inc('V', i)
            vhist.append(c.cnt['V'])
            nc.tensor.wait_ge(c.sems['V'], vhist[kk])
            if w1p:
                for q in range(4):
                    s = half * 4 + q
                    i = nc.tensor.matmul(ps_h[0][0:8, :], sel8[:, s, :],
                                         wab[b][:, q * 512:(q + 1) * 512],
                                         start=(kk == 0 and q == 0),
                                         stop=(kk == 15 and q == 3), skip_group_check=True)
            else:
                nc.tensor.matmul(ps_d[0][0:1, :], ones_col[:, :], wab[b][:, 0:512],
                                 start=(k == 0), stop=(k == HT - 1), skip_group_check=True)
                i = nc.tensor.matmul(ps_d[0][64:65, :], ones_col[:, :], wab[b][:, 512:1024],
                                     start=(k == 0), stop=(k == HT - 1), skip_group_check=True)
            c.inc('P', i)
            phist.append(c.cnt['P'])
            if kk == 15:
                c.wait(nc.scalar, 'P')
                i = nc.scalar.copy(stage8[:, :], ps_h[0][0:8, :])
                c.inc('A', i)
                ahist[-1] = c.cnt['A']
                c.wait(nc.sync, 'A')
                c.inc('S', nc.sync.dma_start(
                    out=a1_dram.ap().rearrange("(s f) cc -> s f cc", s=8),
                    in_=stage8[:, :].rearrange("p (f cc) -> p f cc", f=4)), 16)
        c.wait(nc.scalar, 'P')
        i = nc.scalar.activation(stage[:, :], ps_d[0][0:1, :], AF.Identity, scale=0.25)
        c.inc('A', i)
        c.wait(nc.gpsimd, 'A')
        c.inc('G', nc.gpsimd.dma_start(out=a2_dram[0:4, :].unsqueeze(0),
                                       in_=stage[:, :].rearrange("o (a b) -> o a b", a=4)), 16)
        nc.scalar.wait_ge(c.sems['G'], c.cnt['G'])
        i = nc.scalar.activation(stage[:, :], ps_d[0][64:65, :], AF.Identity, scale=0.25)
        c.inc('A', i)
        c.wait(nc.gpsimd, 'A')
        c.inc('G', nc.gpsimd.dma_start(out=a2_dram[4:8, :].unsqueeze(0),
                                       in_=stage[:, :].rearrange("o (a b) -> o a b", a=4)), 16)
        c.wait(nc.gpsimd, 'G', 'S')
        c.inc('G', nc.gpsimd.dma_start(out=a1t[:, :], in_=a1_dram.ap()), 16)
        c.inc('G', nc.gpsimd.dma_start(out=a2t[:, :], in_=a2_dram.ap()), 16)
        c.wait(nc.tensor, 'G')
        nc.tensor.transpose(ps_tf[:, 0:32], a1t[:, :], idf[0:32, 0:32])
        i = nc.tensor.transpose(ps_tf[:, 32:40], a2t[:, :], idf[0:8, 0:8])
        c.inc('P', i)
        c.wait(nc.scalar, 'P')
        nc.scalar.copy(a1c[:, :], ps_tf[:, 0:32])
        i = nc.scalar.copy(a2c[:, :], ps_tf[:, 32:40])
        c.inc('A', i)

        xr = xT.rearrange("(k p) n -> p k n", p=128)
        c.inc('G', nc.gpsimd.dma_start(out=xb[:, :, :], in_=xr[:, :, 0:M]), 16)
        c.inc('L', nc.sync.dma_start(out=xf[:, :, :], in_=xr[:, :, 0:M]), 16)
        nc.scalar.wait_ge(c.sems['G'], c.cnt['G'])
        nc.tensor.wait_ge(c.sems['G'], c.cnt['G'])
        nc.vector.wait_ge(c.sems['G'], c.cnt['G'])
        emit_ln(c, ln, [xb[:, t, :] for t in range(8)], M)
        for ch in range(16):
            sl = slice(ch * M, (ch + 1) * M)
            c.wait(nc.vector, 'L')
            emit_normalize(c, ln, [xb[:, t, :] for t in range(8)],
                           [xn[:, t, :] for t in range(8)], M, tmpf)
            v_norm = c.cnt['V']
            c.wait(nc.tensor, 'V', 'A')
            ah = []
            for ht in range(HT):
                hsl = slice(ht * 128, (ht + 1) * 128)
                ph = ps_h[ht % 4]
                if ht >= 4:
                    nc.tensor.wait_ge(c.sems['A'], ah[ht - 4])
                for k in range(8):
                    i = nc.tensor.matmul(ph[:, 0:M], w1s[:, k, hsl],
                                         xn[:, k, :], start=(k == 0), stop=(k == 7),
                                         skip_group_check=True)
                c.inc('P', i)
                c.wait(nc.scalar, 'P')
                i = nc.scalar.activation(hid[:, ht, :], ph[:, 0:M], AF.Gelu,
                                         scale=a1c[:, ht:ht + 1], bias=b1c[:, ht:ht + 1])
                c.inc('A', i)
                ah.append(c.cnt['A'])
            if ch + 1 < 16:
                nsl = slice((ch + 1) * M, (ch + 2) * M)
                nc.gpsimd.wait_ge(c.sems['V'], v_norm)
                c.inc('G', nc.gpsimd.dma_start(out=xb[:, :, :], in_=xr[:, :, nsl]), 16)
                nc.scalar.wait_ge(c.sems['G'], c.cnt['G'])
                nc.tensor.wait_ge(c.sems['G'], c.cnt['G'])
                nc.vector.wait_ge(c.sems['G'], c.cnt['G'])
                emit_ln(c, ln, [xb[:, t, :] for t in range(8)], M)
            c.wait(nc.tensor, 'A')
            wvh = []
            for t in range(8):
                dsl = slice(t * 128, (t + 1) * 128)
                pd = ps_d[t % 2]
                if t >= 2:
                    nc.tensor.wait_ge(c.sems['V'], wvh[t - 2])
                for ht in range(HT):
                    i = nc.tensor.matmul(pd[:, 0:M], w2s[:, ht, dsl], hid[:, ht, :],
                                         start=(ht == 0), stop=(ht == HT - 1), skip_group_check=True)
                c.inc('P', i)
                c.wait(nc.scalar, 'P')
                i = nc.scalar.activation(raw[:, t, :], pd[:, 0:M], AF.Identity,
                                         scale=a2c[:, t:t + 1], bias=b2c[:, t:t + 1])
                c.inc('A', i)
                c.wait(nc.vector, 'A')
                i = nc.vector.tensor_copy(rawb[:, t, :], raw[:, t, :])
                c.inc('V', i)
                wvh.append(c.cnt['V'])
            c.wait(nc.scalar, 'V', 'L')
            emit_ln(c, ln2, [rawb[:, t, :] for t in range(8)], M)
            c.wait(nc.vector, 'L')
            emit_normalize(c, ln2, [raw[:, t, :] for t in range(8)], None, M, tmpf,
                           add_tiles=[xf[:, t, :] for t in range(8)],
                           out3=[x4[:, t, :] for t in range(8)])
            if ch + 1 < 16:
                c.wait(nc.sync, 'V')
                c.inc('L', nc.sync.dma_start(out=xf[:, :, :], in_=xr[:, :, nsl]), 16)
            for ms in range(2):
                c.wait(nc.tensor, 'V', 'S')
                for t in range(4):
                    i = nc.tensor.transpose(ps_tf[:, t * 128:(t + 1) * 128],
                                            x4[:, t, ms * 128:(ms + 1) * 128], idf[:, :])
                c.inc('P', i)
                c.wait(nc.scalar, 'P')
                for t in range(4):
                    i = nc.scalar.copy(ytm[:, t * 128:(t + 1) * 128], ps_tf[:, t * 128:(t + 1) * 128])
                c.inc('A', i)
                c.wait(nc.tensor, 'A')
                for t in range(4, 8):
                    i = nc.tensor.transpose(ps_tf[:, (t - 4) * 128:(t - 3) * 128],
                                            x4[:, t, ms * 128:(ms + 1) * 128], idf[:, :])
                c.inc('P', i)
                c.wait(nc.scalar, 'P')
                for t in range(4, 8):
                    i = nc.scalar.copy(ytm[:, t * 128:(t + 1) * 128],
                                       ps_tf[:, (t - 4) * 128:(t - 3) * 128])
                c.inc('A', i)
                c.wait(nc.sync, 'A')
                c.inc('S', nc.sync.dma_start(
                    out=y_d[ch * M + ms * 128: ch * M + ms * 128 + 128, :], in_=ytm[:, :]), 16)
        c.wait(nc.sync, 'S')
        c.wait(nc.gpsimd, 'S')
    es.close()
    return nc


def _cols(b, n):
    return np.ascontiguousarray(b.reshape(n, 128).T)


def build_exchange(env):
    nc, es, c, blk = begin(env, "ex_")
    pes = env['c'].es if env is not None else es
    stats_in = c.dram("stats_in", [STATS_ROWS, D], F32, "in")
    stats_out = c.dram("stats_out", [STATS_ROWS, D], F32, "out")
    par_d = c.dram("parin", [1, 1], I32, "in")
    non_d = c.dram("noncein", [1, 1], I32, "in")
    sh_st = nc.dram_tensor(c.prefix + "sh_st", [2 * STATS_ROWS, D], F32, addr_space="Shared")
    sh_fl = nc.dram_tensor(c.prefix + "sh_fl", [2, 128], I32, addr_space="Shared")
    st8 = c.sb("st8", [STATS_ROWS, D], F32)
    pst = c.sb("pst", [STATS_ROWS, D], F32)
    kmA = c.sb("kmA", [1, 16], F32)
    kmB = c.sb("kmB", [1, 16], F32)
    pari = c.sb("pari", [1, 1], I32)
    nonc = c.sb("nonc", [1, 1], I32)
    flagb = c.sb("flagb", [1, 1], I32)
    exs = pes.enter_context(nc.semaphore("exs"))
    exp = pes.enter_context(nc.semaphore("exp"))
    g = nc.gpsimd
    with blk:
        c.inc('L', nc.sync.dma_start(out=st8[:, :], in_=stats_in), 16)
        c.inc('L', nc.sync.dma_start(out=kmA[:, :], in_=stats_in[73:74, 0:16]), 16)
        c.inc('G', g.dma_start(out=pari[:, :], in_=par_d), 16)
        c.inc('G', g.dma_start(out=nonc[:, :], in_=non_d), 16)
        c.wait(g, 'G', 'L')
        with (
            g.register("preg") as preg,
            g.register("df") as df,
            g.register("tgt") as tgt,
        ):
            g.reg_load(preg, pari[0:1, 0:1])
            want_v = g.value_load(nonc[0:1, 0:1])
            for parval in (0, 1):
                my0 = parval * STATS_ROWS
                pe0 = (1 - parval) * STATS_ROWS
                guard = g.If_eq(preg, 0) if parval == 0 else g.Else()
                with guard:
                    g.dma_start(out=sh_st.ap()[my0:my0 + STATS_ROWS, :],
                                in_=st8[:, :]).then_inc(exs, 16)
                    g.wait_ge(exs, 16)
                    g.dma_start(out=sh_fl.ap()[parval:parval + 1, 0:1],
                                in_=nonc[0:1, 0:1]).then_inc(exs, 16)
                    g.wait_ge(exs, 32)
                    g.reg_mov(tgt, 0)
                    g.reg_mov(df, 1)
                    with g.While(df):
                        g.reg_alu(tgt, tgt, 16, ALU.add)
                        g.dma_start(out=flagb[:, :],
                                    in_=sh_fl.ap()[1 - parval:2 - parval, 0:1]).then_inc(exp, 16)
                        g.wait_ge(exp, g.snap(tgt))
                        got_v = g.value_load(flagb[0:1, 0:1])
                        g.reg_alu(df, got_v, want_v, ALU.subtract)
                    g.dma_start(out=pst[:, :],
                                in_=sh_st.ap()[pe0:pe0 + STATS_ROWS, :]).then_inc(exs, 16)
                    g.dma_start(out=kmB[:, :],
                                in_=sh_st.ap()[pe0 + 73:pe0 + 74, 0:16]).then_inc(exs, 16)
                    g.wait_ge(exs, 64)
            i = g.nop()
            c.inc('G', i)
        c.wait(nc.vector, 'G')
        nc.vector.tensor_tensor(st8[0:64, :], st8[0:64, :], pst[0:64, :], op=ALU.add)
        i = nc.vector.tensor_tensor(st8[64:80, :], st8[64:80, :], pst[64:80, :], op=ALU.add)
        c.inc('V', i)
        i = nc.vector.tensor_tensor(kmA[:, :], kmA[:, :], kmB[:, :], op=ALU.max)
        c.inc('V', i)
        c.wait(nc.sync, 'V')
        c.inc('S', nc.sync.dma_start(out=stats_out[:, :], in_=st8[:, :]), 16)
        c.wait(nc.sync, 'S')
        c.inc('S', nc.sync.dma_start(out=stats_out[73:74, 0:16], in_=kmA[:, :]), 16)
        c.wait(nc.sync, 'S')
    es.close()
    return nc


def build_exch_write(env, sh_st, sh_fl, exs):
    nc, es, c, blk = begin(env, "ew_")
    stats_in = c.dram("stats_in", [STATS_ROWS, D], F32, "in")
    par_d = c.dram("parin", [1, 1], I32, "in")
    non_d = c.dram("noncein", [1, 1], I32, "in")
    st8 = c.sb("st8", [STATS_ROWS, D], F32)
    pari = c.sb("pari", [1, 1], I32)
    nonc = c.sb("nonc", [1, 1], I32)
    g = nc.gpsimd
    with blk:
        c.inc('L', nc.sync.dma_start(out=st8[:, :], in_=stats_in), 16)
        c.inc('G', g.dma_start(out=pari[:, :], in_=par_d), 16)
        c.inc('G', g.dma_start(out=nonc[:, :], in_=non_d), 16)
        c.wait(g, 'G', 'L')
        with g.register("ewpreg") as preg:
            g.reg_load(preg, pari[0:1, 0:1])
            for parval in (0, 1):
                my0 = parval * STATS_ROWS
                guard = g.If_eq(preg, 0) if parval == 0 else g.Else()
                with guard:
                    g.dma_start(out=sh_st.ap()[my0:my0 + STATS_ROWS, :],
                                in_=st8[:, :]).then_inc(exs, 16)
                    g.wait_ge(exs, 16)
                    g.dma_start(out=sh_fl.ap()[parval:parval + 1, 0:1],
                                in_=nonc[0:1, 0:1]).then_inc(exs, 16)
            i = g.nop()
            c.inc('G', i)
    es.close()
    return nc


def build_exch_poll(env, sh_st, sh_fl, exs, exp):
    nc, es, c, blk = begin(env, "ep_")
    stats_in = c.dram("stats_in", [STATS_ROWS, D], F32, "in")
    stats_out = c.dram("stats_out", [STATS_ROWS, D], F32, "out")
    par_d = c.dram("parin", [1, 1], I32, "in")
    non_d = c.dram("noncein", [1, 1], I32, "in")
    st8 = c.sb("st8", [STATS_ROWS, D], F32)
    pst = c.sb("pst", [STATS_ROWS, D], F32)
    kmA = c.sb("kmA", [1, 16], F32)
    kmB = c.sb("kmB", [1, 16], F32)
    pari = c.sb("pari", [1, 1], I32)
    nonc = c.sb("nonc", [1, 1], I32)
    flagb = c.sb("flagb", [1, 1], I32)
    g = nc.gpsimd
    with blk:
        c.inc('L', nc.sync.dma_start(out=st8[:, :], in_=stats_in), 16)
        c.inc('L', nc.sync.dma_start(out=kmA[:, :], in_=stats_in[73:74, 0:16]), 16)
        c.inc('G', g.dma_start(out=pari[:, :], in_=par_d), 16)
        c.inc('G', g.dma_start(out=nonc[:, :], in_=non_d), 16)
        c.wait(g, 'G', 'L')
        with (
            g.register("eppreg") as preg,
            g.register("epdf") as df,
            g.register("eptgt") as tgt,
        ):
            g.reg_load(preg, pari[0:1, 0:1])
            want_v = g.value_load(nonc[0:1, 0:1])
            for parval in (0, 1):
                pe0 = (1 - parval) * STATS_ROWS
                guard = g.If_eq(preg, 0) if parval == 0 else g.Else()
                with guard:
                    g.wait_ge(exs, 32)
                    g.reg_mov(tgt, 0)
                    g.reg_mov(df, 1)
                    with g.While(df):
                        g.reg_alu(tgt, tgt, 16, ALU.add)
                        g.dma_start(out=flagb[:, :],
                                    in_=sh_fl.ap()[1 - parval:2 - parval, 0:1]).then_inc(exp, 16)
                        g.wait_ge(exp, g.snap(tgt))
                        got_v = g.value_load(flagb[0:1, 0:1])
                        g.reg_alu(df, got_v, want_v, ALU.subtract)
                    g.dma_start(out=pst[:, :],
                                in_=sh_st.ap()[pe0:pe0 + STATS_ROWS, :]).then_inc(exs, 16)
                    g.dma_start(out=kmB[:, :],
                                in_=sh_st.ap()[pe0 + 73:pe0 + 74, 0:16]).then_inc(exs, 16)
                    g.wait_ge(exs, 64)
            i = g.nop()
            c.inc('G', i)
        c.wait(nc.vector, 'G')
        nc.vector.tensor_tensor(st8[0:64, :], st8[0:64, :], pst[0:64, :], op=ALU.add)
        i = nc.vector.tensor_tensor(st8[64:80, :], st8[64:80, :], pst[64:80, :], op=ALU.add)
        c.inc('V', i)
        i = nc.vector.tensor_tensor(kmA[:, :], kmA[:, :], kmB[:, :], op=ALU.max)
        c.inc('V', i)
        c.wait(nc.sync, 'V')
        c.inc('S', nc.sync.dma_start(out=stats_out[:, :], in_=st8[:, :]), 16)
        c.wait(nc.sync, 'S')
        c.inc('S', nc.sync.dma_start(out=stats_out[73:74, 0:16], in_=kmA[:, :]), 16)
        c.wait(nc.sync, 'S')
    es.close()
    return nc


def build_APPLY1(env, qp_all):
    nc, es, c, blk = begin(env, "p1_")
    M = 512
    xT = c.dram("xT", [D, ROWS], F32, "in")
    xhT = c.dram("xhT", [D, HDR], F32, "in")
    WqT_d = c.dram("WqT", [D, D], F32, "in")
    P_d = c.dram("Pstage", [128, D], F32, "in")
    bqc_d = c.dram("bqc", [128, 8], F32, "in")
    dia_d = c.dram("diagones", [128, 8, 16], F32, "in")
    idf_d = c.dram("idf", [128, 128], F32, "in")
    idb_d = c.dram("idb", [128, 128], F32, "in")

    WqT = c.sb("WqT", [128, 8, D], BF16)
    Pst = c.sb("Pst", [128, D], BF16)
    bqc = c.sb("bqc", [128, 8], F32)
    bqcs = c.sb("bqcs", [128, 8], F32)
    dia = c.sb("dia", [128, 8, 16], BF16)
    idf = c.sb("idf", [128, 128], F32)
    idb = c.sb("idb", [128, 128], BF16)
    xb = c.sb("xb", [128, 8, M], BF16)
    xn = c.sb("xn", [128, 8, M], BF16)
    tmpf = c.sb("tmpf", [128, M], F32)
    qbf = c.sb("qbf", [128, 8, M], BF16)
    sqq = c.sb("sqq", [128, 8, M], BF16)
    diag_fm = c.sb("diag_fm", [16, M], F32)
    diag_tm = c.sb("diag_tm", [128, 4, 16], F32)
    stab = c.sb("stab", [128, 16], F32)
    pre = c.sb("pre", [128, D], F32)
    qp_tm = c.sb("qp_tm", [128, 4, D], BF16)
    ln = mk_ln(c, M)
    ps_big = c.ps("ps_big", [128, 1024])
    ps_trb = c.ps("ps_trb", [128, 1024], BF16)
    ps_trf = c.ps("ps_trf", [128, 512])

    with blk:
        ln_init(c, ln)
        c.inc('G', nc.gpsimd.dma_start(out=WqT[:, :, :], in_=WqT_d.rearrange("(k p) n -> p k n", p=128)), 16)
        c.inc('G', nc.gpsimd.dma_start(out=Pst[:, :], in_=P_d), 16)
        c.inc('G', nc.gpsimd.dma_start(out=dia[:, :, :], in_=dia_d), 16)
        c.inc('G', nc.gpsimd.dma_start(out=idb[:, :], in_=idb_d), 16)
        c.inc('L', nc.sync.dma_start(out=idf[:, :], in_=idf_d), 16)
        c.inc('L', nc.sync.dma_start(out=bqc[:, :], in_=bqc_d), 16)
        nc.scalar.wait_ge(c.sems['L'], c.cnt['L'])
        i = nc.scalar.activation(bqcs[:, :], bqc[:, :], AF.Identity, scale=np.float32(CFAV))
        c.inc('A', i)

        xr = xT.rearrange("(k p) n -> p k n", p=128)
        xhr = xhT.rearrange("(k p) n -> p k n", p=128)
        for ch in range(9):
            hdr = (ch == 8)
            m = 128 if hdr else M
            nms = m // 128
            base = ch * M
            sl = None if hdr else slice(ch * M, (ch + 1) * M)
            if ch:
                c.wait(nc.gpsimd, 'V')
                c.wait(nc.sync, 'V')
            emit_chunk_load(c, xb, None, m, xr=xr, sl=sl, xhr=(xhr if hdr else None),
                            need_f32=False)
            emit_ln(c, ln, [xb[:, t, 0:m] for t in range(8)], m)
            c.wait(nc.vector, 'L')
            emit_normalize(c, ln, [xb[:, t, 0:m] for t in range(8)],
                           [xn[:, t, 0:m] for t in range(8)], m, tmpf)
            c.wait(nc.tensor, 'V', 'A')
            emit_proj_fm(c, WqT, xn, qbf, m, ps_big, bias_col=bqcs, scale=CFAV)
            emit_diag(c, dia, sqq, qbf, diag_fm, diag_tm, idf, ps_trf, ps_trf, m)
            for ms in range(nms):
                msl = slice(ms * 128, ms * 128 + 128)
                c.wait(nc.tensor, 'A', 'V')
                emit_favorp_head_mm(c, qbf, Pst, ps_big, msl)
                c.wait(nc.vector, 'P', 'A')
                nc.vector.reduce_max(stab[:, :].unsqueeze(2),
                                     ps_big[:, :].rearrange("p (h j) -> p h j", h=16),
                                     axis=AXX)
                i = nc.vector.scalar_tensor_tensor(stab[:, :], diag_tm[:, ms, :], -0.5,
                                                   stab[:, :], op0=ALU.mult, op1=ALU.subtract)
                c.inc('V', i)
                nc.vector.wait_ge(c.sems['V'], c.cnt['V'])
                i = nc.vector.tensor_tensor(
                    pre[:, :].rearrange("p (h j) -> p h j", h=16),
                    ps_big[:, :].rearrange("p (h j) -> p h j", h=16),
                    stab[:, :].unsqueeze(2).broadcast_to([128, 16, 64]), op=ALU.add)
                c.inc('V', i)
                c.wait(nc.scalar, 'V')
                i = nc.scalar.activation(pre[:, :], pre[:, :], AF.Exp)
                c.inc('A', i)
                c.wait(nc.vector, 'A')
                i = nc.vector.tensor_scalar(qp_tm[:, ms, :], pre[:, :], 1.0, float(EPS_FEAT),
                                            op0=ALU.mult, op1=ALU.add)
                c.inc('V', i)
                c.wait(nc.tensor, 'V')
                for t in range(8):
                    i = nc.tensor.transpose(ps_trb[:, t * 128:(t + 1) * 128],
                                            qp_tm[:, ms, t * 128:(t + 1) * 128], idb[:, :])
                c.inc('P', i)
                c.wait(nc.vector, 'P')
                for t in range(8):
                    i = nc.vector.tensor_copy(
                        qp_all[:, t, base + ms * 128:base + ms * 128 + 128],
                        ps_trb[:, t * 128:(t + 1) * 128])
                c.inc('V', i)
    es.close()
    return nc


def build_APPLY2(env, qp_all):
    nc, es, c, blk = begin(env, "p2_")
    M = 512
    xT = c.dram("xT", [D, ROWS], F32, "in")
    xhT = c.dram("xhT", [D, HDR], F32, "in")
    statsall_d = c.dram("statsall", [NCORE * STATS_ROWS, D], F32, "in")
    WoT_d = c.dram("WoT", [D, D], F32, "in")
    boc_d = c.dram("boc", [128, 8], F32, "in")
    idf_d = c.dram("idf", [128, 128], F32, "in")
    idb_d = c.dram("idb", [128, 128], F32, "in")
    x1T_d = c.dram("x1T", [D, ROWS], F32, "out")
    x1hT_d = c.dram("x1hT", [D, HDR], F32, "out")

    f_dram = nc.dram_tensor(c.prefix + "f_dram", [1, 16], F32)
    vs_dram = nc.dram_tensor(c.prefix + "vs_dram", [1, D], F32)
    ks_dram = nc.dram_tensor(c.prefix + "ks_dram", [8, 128], F32)

    WoT = c.sb("WoT", [128, 8, D], BF16)
    boc = c.sb("boc", [128, 8], F32)
    idf = c.sb("idf", [128, 128], F32)
    idb = c.sb("idb", [128, 128], BF16)
    stsum = c.sb("stsum", [80, D], F32)
    stabk = c.sb("stabk", [16, 1], F32)
    fexp = c.sb("fexp", [16, 1], F32)
    kv_stage = c.sb("kv_stage", [128, 512], F32)
    f_pack = c.sb("f_pack", [128, 8], F32)
    vs_pack = c.sb("vs_pack", [128, 8, 64], F32)
    kv_sb = c.sb("kv_sb", [128, 8, 64], BF16)
    ks_t = c.sb("ks_t", [8, 128], F32)
    f_e = c.sb("f_e", [8, 1], F32)
    f_o = c.sb("f_o", [8, 1], F32)
    BDk = c.sb("BDk", [128, 8, 16], BF16)

    xf = c.sb("xf", [128, 8, M], F32)
    tmpf = c.sb("tmpf", [128, M], F32)
    nrm = c.sb("nrm", [16, M], F32)
    rec_tm = c.sb("rec_tm", [128, 4, 16], F32)
    at_tm = c.sb("at_tm", [128, D], BF16)
    at_fm = c.sb("at_fm", [128, 8, M], BF16)
    raw = c.sb("raw", [128, 8, M], F32)
    rawb = c.sb("rawb", [128, 8, M], BF16)
    x1 = c.sb("x1", [128, 8, M], F32)
    ln = mk_ln(c, M)
    ps_aux = c.ps("ps_aux", [128, 512])
    ps_big = c.ps("ps_big", [128, 1024])
    ps_trb = c.ps("ps_trb", [128, 1024], BF16)
    ps_trf = c.ps("ps_trf", [128, 512])

    with blk:
        ln_init(c, ln)
        c.inc('G', nc.gpsimd.dma_start(out=WoT[:, :, :], in_=WoT_d.rearrange("(k p) n -> p k n", p=128)), 16)
        c.inc('G', nc.gpsimd.dma_start(out=idb[:, :], in_=idb_d), 16)
        c.inc('L', nc.sync.dma_start(out=idf[:, :], in_=idf_d), 16)
        c.inc('L', nc.sync.dma_start(out=boc[:, :], in_=boc_d), 16)
        c.inc('L', nc.sync.dma_start(out=stsum[:, :], in_=statsall_d[0:STATS_ROWS, :]), 16)
        with nc.allow_non_contiguous_dma(reason="16-element kmax row"):
            c.inc('L', nc.sync.dma_start(out=stabk[:, :],
                                         in_=statsall_d[73:74, 0:16].transpose([1, 0])), 16)
        c.wait(nc.vector, 'L')
        c.wait(nc.scalar, 'V', 'L')
        i = nc.scalar.activation(fexp[:, :], stabk[:, :], AF.Exp, scale=-1.0)
        c.inc('A', i)
        c.wait(nc.gpsimd, 'A', 'V')
        stk = stsum[0:64, :].rearrange("p (a q x) -> p a q x", a=8, q=2)
        c.inc('G', nc.gpsimd.dma_start(
            out=kv_stage[0:64, :].rearrange("p (a x) -> p a x", a=8),
            in_=stk[:, :, 0:1, :].squeeze(2)), 16)
        c.inc('G', nc.gpsimd.dma_start(
            out=kv_stage[64:128, :].rearrange("p (a x) -> p a x", a=8),
            in_=stk[:, :, 1:2, :].squeeze(2)), 16)
        c.inc('G', nc.gpsimd.dma_start(out=f_dram.ap().transpose([1, 0]), in_=fexp[:, :]), 16)
        c.inc('G', nc.gpsimd.dma_start(out=ks_dram[:, :], in_=stsum[64:72, 0:128]), 16)
        c.inc('G', nc.gpsimd.dma_start(out=vs_dram[:, :], in_=stsum[72:73, :]), 16)
        c.wait(nc.gpsimd, 'G')
        fap = f_dram.ap().squeeze(0).rearrange("(a p) -> a p", a=8)
        with nc.allow_non_contiguous_dma(reason="16-element parity unpack"):
            c.inc('G', nc.gpsimd.dma_start(
                out=f_pack[0:64, :].unsqueeze(2),
                in_=fap[:, 0:1].unsqueeze(0).partition_broadcast(64).squeeze(1)), 16)
            c.inc('G', nc.gpsimd.dma_start(
                out=f_pack[64:128, :].unsqueeze(2),
                in_=fap[:, 1:2].unsqueeze(0).partition_broadcast(64).squeeze(1)), 16)
            c.inc('G', nc.gpsimd.dma_start(out=f_e[:, :], in_=fap[:, 0:1]), 16)
            c.inc('G', nc.gpsimd.dma_start(out=f_o[:, :], in_=fap[:, 1:2]), 16)
        vap = vs_dram.ap().squeeze(0).rearrange("(a p x) -> a p x", a=8, p=2)
        c.inc('G', nc.gpsimd.dma_start(
            out=vs_pack[0:64, :, :], in_=vap[:, 0:1, :].squeeze(1).unsqueeze(0).partition_broadcast(64).squeeze(1)), 16)
        c.inc('G', nc.gpsimd.dma_start(
            out=vs_pack[64:128, :, :], in_=vap[:, 1:2, :].squeeze(1).unsqueeze(0).partition_broadcast(64).squeeze(1)), 16)
        c.inc('G', nc.gpsimd.dma_start(out=ks_t[:, :], in_=ks_dram[:, :]), 16)
        c.wait(nc.vector, 'G')
        nc.vector.tensor_scalar(vs_pack[:, :, :], vs_pack[:, :, :], float(EPS_FEAT), 0.0,
                                op0=ALU.mult, op1=ALU.add)
        nc.vector.tensor_tensor(
            kv_stage[:, :].rearrange("p (a x) -> p a x", a=8),
            kv_stage[:, :].rearrange("p (a x) -> p a x", a=8),
            f_pack[:, :].unsqueeze(2).broadcast_to([128, 8, 64]), op=ALU.mult)
        i = nc.vector.tensor_tensor(kv_sb[:, :, :],
                                    kv_stage[:, :].rearrange("p (a x) -> p a x", a=8),
                                    vs_pack[:, :, :], op=ALU.add)
        nc.vector.tensor_scalar(ks_t[:, 0:64], ks_t[:, 0:64], f_e[0:8, 0:1],
                                float(EPS_FEAT * 16384.0), op0=ALU.mult, op1=ALU.add)
        i = nc.vector.tensor_scalar(ks_t[:, 64:128], ks_t[:, 64:128], f_o[0:8, 0:1],
                                    float(EPS_FEAT * 16384.0), op0=ALU.mult, op1=ALU.add)
        c.inc('V', i)
        i = nc.gpsimd.memset(BDk[:, :, :], 0.0)
        c.inc('G', i)
        c.wait(nc.gpsimd, 'V')
        c.inc('G', nc.gpsimd.dma_start(out=ks_dram[:, :], in_=ks_t[:, :]), 16)
        c.wait(nc.gpsimd, 'G')
        for t in range(8):
            for par in range(2):
                i = nc.gpsimd.dma_start(
                    out=BDk[64 * par:64 * par + 64, t, 2 * t + par:2 * t + par + 1],
                    in_=ks_dram.ap()[t:t + 1, 64 * par:64 * par + 64].transpose([1, 0]))
                c.inc('G', i, 16)

        xr = xT.rearrange("(k p) n -> p k n", p=128)
        x1r = x1T_d.rearrange("(k p) n -> p k n", p=128)
        xhr = xhT.rearrange("(k p) n -> p k n", p=128)
        x1hr = x1hT_d.rearrange("(k p) n -> p k n", p=128)
        c.wait(nc.tensor, 'G')
        for ch in range(9):
            hdr = (ch == 8)
            m = 128 if hdr else M
            nms = m // 128
            base = ch * M
            sl = None if hdr else slice(ch * M, (ch + 1) * M)
            if ch:
                c.wait(nc.gpsimd, 'V')
                c.wait(nc.sync, 'V')
            if hdr:
                i = nc.gpsimd.memset(xf[:, :, 0:m], 0.0)
                c.inc('G', i)
                c.wait(nc.sync, 'G')
                c.inc('L', nc.sync.dma_start(out=xf[:, :, 0:HDR], in_=xhr), 16)
            else:
                c.inc('L', nc.sync.dma_start(out=xf[:, :, :], in_=xr[:, :, sl]), 16)
            c.wait(nc.tensor, 'A', 'V')
            for t in range(8):
                i = nc.tensor.matmul(ps_aux[32:48, 0:m], BDk[:, t, :],
                                     qp_all[:, t, base:base + m],
                                     start=(t == 0), stop=(t == 7), skip_group_check=True)
            c.inc('P', i)
            c.wait(nc.scalar, 'P')
            i = nc.scalar.copy(nrm[:, 0:m], ps_aux[32:48, 0:m])
            c.inc('A', i)
            c.wait(nc.vector, 'A')
            nc.vector.tensor_scalar_max(nrm[:, 0:m], nrm[:, 0:m], float(EPS_DEN))
            i = nc.vector.reciprocal(nrm[:, 0:m], nrm[:, 0:m])
            c.inc('V', i)
            c.wait(nc.tensor, 'V')
            for ms in range(nms):
                i = nc.tensor.transpose(ps_trf[:, 256 + ms * 16:256 + ms * 16 + 16],
                                        nrm[:, ms * 128:(ms + 1) * 128], idf[0:16, 0:16])
                c.inc('P', i)
            c.wait(nc.scalar, 'P')
            for ms in range(nms):
                i = nc.scalar.copy(rec_tm[:, ms, :], ps_trf[:, 256 + ms * 16:256 + ms * 16 + 16])
            c.inc('A', i)
            for ms in range(nms):
                msl = slice(ms * 128, ms * 128 + 128)
                gsl = slice(base + ms * 128, base + ms * 128 + 128)
                c.wait(nc.tensor, 'A', 'V')
                for par_pass in range(2):
                    for h in range(par_pass, NHEAD, 2):
                        par = 64 * (h % 2)
                        i = nc.tensor.matmul(ps_big[:, h * 64:(h + 1) * 64],
                                             qp_all[par:par + 64, h // 2, gsl],
                                             kv_sb[par:par + 64, h // 2, :],
                                             start=(h in (0, 8)), stop=(h in (7, 15)),
                                             skip_group_check=True)
                    if par_pass == 0:
                        c.inc('P', i)
                        nc.tensor.wait_ge(c.sems['P'], c.cnt['P'])
                c.inc('P', i)
                c.wait(nc.vector, 'P', 'A')
                i = nc.vector.tensor_tensor(
                    at_tm[:, :].rearrange("p (h j) -> p h j", h=16),
                    ps_big[:, :].rearrange("p (h j) -> p h j", h=16),
                    rec_tm[:, ms, :].unsqueeze(2).broadcast_to([128, 16, 64]), op=ALU.mult)
                c.inc('V', i)
                c.wait(nc.tensor, 'V')
                for t in range(8):
                    i = nc.tensor.transpose(ps_trb[:, t * 128:(t + 1) * 128],
                                            at_tm[:, t * 128:(t + 1) * 128], idb[:, :])
                c.inc('P', i)
                c.wait(nc.vector, 'P')
                for t in range(8):
                    i = nc.vector.tensor_copy(at_fm[:, t, msl], ps_trb[:, t * 128:(t + 1) * 128])
                c.inc('V', i)
            c.wait(nc.tensor, 'V')
            emit_proj_fm2(c, WoT, at_fm, raw, rawb, m, ps_big, bias_col=boc)
            nc.vector.wait_ge(c.sems['L'], c.cnt['L'])
            c.wait(nc.scalar, 'V')
            emit_ln(c, ln, [rawb[:, t, 0:m] for t in range(8)], m)
            c.wait(nc.vector, 'L')
            emit_normalize(c, ln, [raw[:, t, 0:m] for t in range(8)], None, m, tmpf,
                           add_tiles=[xf[:, t, 0:m] for t in range(8)],
                           out3=[x1[:, t, 0:m] for t in range(8)])
            c.wait(nc.sync, 'V')
            if hdr:
                c.inc('S', nc.sync.dma_start(out=x1hr, in_=x1[:, :, 0:HDR]), 16)
            else:
                c.inc('S', nc.sync.dma_start(out=x1r[:, :, sl], in_=x1[:, :, :]), 16)
        c.wait(nc.sync, 'S')
        c.wait(nc.gpsimd, 'S')
    es.close()
    return nc


def build_fused():
    nc = bass.Bass("TRN2", target_bir_lowering=False)
    es = ExitStack()
    c = Ctx(nc, es)
    BROWS = 2 * ROWS
    EI = lambda n, sh, dt=F32: nc.dram_tensor(n, sh, dt, kind="ExternalInput").ap()
    IN = lambda n, sh, dt=F32: nc.dram_tensor(n, sh, dt).ap()
    xT = EI("xT", [D, ROWS])
    xbT = EI("xbT", [D, BROWS])
    xhdrT = EI("xhdrT", [D, HDR])
    parin = EI("parin", [1, 1], I32)
    noncein = EI("noncein", [1, 1], I32)
    WkT = EI("WkT", [D, D]); WvT = EI("WvT", [D, D])
    WqT = EI("WqT", [D, D]); WoT = EI("WoT", [D, D])
    Pstage = EI("Pstage", [128, D])
    bkc = EI("bkc", [128, 8]); bqc = EI("bqc", [128, 8]); boc = EI("boc", [128, 8])
    bvr = EI("bvr", [1, D])
    dia = EI("diagones", [128, 8, 16])
    idf = EI("idf", [128, 128])
    wgT = EI("wgT", [D, 3072], I32); wuT = EI("wuT", [D, 3072], I32)
    wdT = EI("wdT", [3072, D], I32)
    WaqT = EI("WaqT", [D, D]); WakvT = EI("WakvT", [D, 2 * D]); WpT = EI("WpT", [D, D])
    l1mask = EI("l1mask", [128, 256])
    w1T = EI("w1T", [D, 4096]); w2T = EI("w2T", [4096, D])
    b1c = EI("b1c", [128, 32]); b2c = EI("b2c", [128, 8])
    y = nc.dram_tensor("y", [ROWS, D], F32, kind="ExternalOutput").ap()
    stats = IN("i_stats", [STATS_ROWS, D])
    stats2 = IN("i_stats2", [STATS_ROWS, D])
    x1T = IN("i_x1T", [D, ROWS]); x1hT = IN("i_x1hT", [D, HDR])
    x2T = IN("i_x2T", [D, ROWS]); x2hT = IN("i_x2hT", [D, HDR])
    x3T = IN("i_x3T", [D, ROWS])
    sh_st = nc.dram_tensor("fx_sh_st", [2 * STATS_ROWS, D], F32, addr_space="Shared")
    sh_fl = nc.dram_tensor("fx_sh_fl", [2, 128], I32, addr_space="Shared")
    exs = es.enter_context(nc.semaphore("fx_exs"))
    exp = es.enter_context(nc.semaphore("fx_exp"))
    with nc.Block():
        env = {'nc': nc, 'c': c}
        build_L0A_stats({**env, 'io': dict(xT=xbT, WkT=WkT, WvT=WvT, Pstage=Pstage, bkc=bkc,
                                           bvr=bvr, diagones=dia, idf=idf, stats=stats)},
                        rows=BROWS)
        full_barrier(c)
        build_exch_write({**env, 'io': dict(stats_in=stats, parin=parin, noncein=noncein)},
                         sh_st, sh_fl, exs)
        full_barrier(c)
        with nc.sbuf_tensor("fx_qp_all", [128, 8, 4224], BF16) as qp_all:
            build_APPLY1({**env, 'io': dict(xT=xT, xhT=xhdrT, WqT=WqT, Pstage=Pstage,
                                            bqc=bqc, diagones=dia, idf=idf, idb=idf)},
                         qp_all)
            full_barrier(c)
            build_exch_poll({**env, 'io': dict(stats_in=stats, stats_out=stats2,
                                               parin=parin, noncein=noncein)},
                            sh_st, sh_fl, exs, exp)
            full_barrier(c)
            build_APPLY2({**env, 'io': dict(xT=xT, xhT=xhdrT, statsall=stats2, WoT=WoT,
                                            boc=boc, idf=idf, idb=idf,
                                            x1T=x1T, x1hT=x1hT)},
                         qp_all)
        full_barrier(c)
        build_L0F({**env, 'io': dict(xT=x1T, xhT=x1hT, wgT=wgT, wuT=wuT, wdT=wdT,
                                     x2T=x2T, x2hT=x2hT)})
        full_barrier(c)
        build_L1A({**env, 'io': dict(xT=x2T, xhT=x2hT, WaqT=WaqT, WakvT=WakvT, WpT=WpT,
                                     l1mask=l1mask, idb=idf, x3T=x3T)})
        full_barrier(c)
        build_L1F({**env, 'io': dict(xT=x3T, w1T=w1T, w2T=w2T, b1c=b1c, b2c=b2c, idf=idf, y=y)})
    es.close()
    return nc


def kernel(L0_attn_query, L0_attn_Wq, L0_attn_bq, L0_attn_Wk, L0_attn_bk,
           L0_attn_Wv, L0_attn_bv, L0_attn_Wo, L0_attn_bo, L0_attn_proj,
           L0_ffn_w_gate, L0_ffn_w_up, L0_ffn_w_down,
           L1_attn_W_attn, L1_attn_W_proj, L1_attn_W_rel,
           L1_ffn_w1, L1_ffn_b1, L1_ffn_w2, L1_ffn_b2):
    _ = L1_attn_W_rel
    x = np.ascontiguousarray(np.asarray(L0_attn_query, dtype=np.float32).reshape(NCORE * ROWS, D))
    cores = list(range(NCORE))
    xT = [np.ascontiguousarray(x[c * ROWS:(c + 1) * ROWS].T) for c in cores]
    xbT = {(b, p): np.ascontiguousarray(
               x[b * 4 * ROWS + p * 2 * ROWS:b * 4 * ROWS + (p + 1) * 2 * ROWS].T)
           for b in range(2) for p in range(2)}
    xhdrT = [np.ascontiguousarray(x[b * 4 * ROWS:b * 4 * ROWS + HDR].T) for b in range(2)]
    nonce = int(np.random.default_rng().integers(1, 2**31 - 1))
    idf = np.eye(128, dtype=np.float32)
    P_stage = np.zeros((128, D), np.float32)
    proj = np.asarray(L0_attn_proj, dtype=np.float32)
    for h in range(NHEAD):
        par = 64 * (h % 2)
        P_stage[par:par + 64, h * 64:(h + 1) * 64] = proj[h].T
    diagones = np.zeros((128, 8, 16), np.float32)
    for t in range(8):
        diagones[0:64, t, 2 * t] = 1.0
        diagones[64:128, t, 2 * t + 1] = 1.0

    def T(w):
        return np.ascontiguousarray(np.asarray(w).T)

    Wa = np.asarray(L1_attn_W_attn, np.float32)
    l1mask = []
    for cc in cores:
        mk = np.zeros((128, 256), np.float32)
        if cc % 4 == 0:
            t_ar = np.arange(128)[:, None]
            kk = np.tile(np.arange(16)[None, :], (1, 16)).reshape(1, 256)
            mk = np.where(kk <= t_ar, 0.0, -1e30).astype(np.float32)
        l1mask.append(np.ascontiguousarray(mk))
    common = {
        "WkT": T(L0_attn_Wk), "WvT": T(L0_attn_Wv), "WqT": T(L0_attn_Wq), "WoT": T(L0_attn_Wo),
        "Pstage": P_stage, "diagones": diagones, "idf": idf,
        "bkc": _cols(np.asarray(L0_attn_bk, np.float32), 8),
        "bqc": _cols(np.asarray(L0_attn_bq, np.float32), 8),
        "boc": _cols(np.asarray(L0_attn_bo, np.float32), 8),
        "bvr": np.asarray(L0_attn_bv, np.float32).reshape(1, D),
        "wgT": T(L0_ffn_w_gate), "wuT": T(L0_ffn_w_up), "wdT": T(L0_ffn_w_down),
        "WaqT": T(Wa[0:D]), "WakvT": T(Wa[D:3 * D]), "WpT": T(L1_attn_W_proj),
        "w1T": T(L1_ffn_w1), "w2T": T(L1_ffn_w2),
        "b1c": _cols(np.asarray(L1_ffn_b1, np.float32), 32),
        "b2c": _cols(np.asarray(L1_ffn_b2, np.float32), 8),
    }
    nc = build_fused()
    in_maps = [dict(common, xT=xT[cc], xbT=xbT[(cc // 4, cc % 2)],
                    xhdrT=xhdrT[cc // 4], l1mask=l1mask[cc],
                    parin=np.array([[cc % 2]], np.int32),
                    noncein=np.array([[nonce]], np.int32)) for cc in cores]
    res = run_bass_kernel_spmd(nc, in_maps, core_ids=cores)
    yy = np.concatenate([res.results[cc]["y"] for cc in cores], axis=0)
    return yy.reshape(2, 16384, 1024)


def kernel_unfused(L0_attn_query, L0_attn_Wq, L0_attn_bq, L0_attn_Wk, L0_attn_bk,
           L0_attn_Wv, L0_attn_bv, L0_attn_Wo, L0_attn_bo, L0_attn_proj,
           L0_ffn_w_gate, L0_ffn_w_up, L0_ffn_w_down,
           L1_attn_W_attn, L1_attn_W_proj, L1_attn_W_rel,
           L1_ffn_w1, L1_ffn_b1, L1_ffn_w2, L1_ffn_b2):
    x = np.ascontiguousarray(np.asarray(L0_attn_query, dtype=np.float32).reshape(NCORE * ROWS, D))
    cores = list(range(NCORE))
    xT = [np.ascontiguousarray(x[c * ROWS:(c + 1) * ROWS].T) for c in cores]
    xhT = [np.ascontiguousarray(x[(c // 4) * 4 * ROWS:(c // 4) * 4 * ROWS + HDR].T) for c in cores]

    idf = np.eye(128, dtype=np.float32)
    P_stage = np.zeros((128, D), np.float32)
    proj = np.asarray(L0_attn_proj, dtype=np.float32)
    for h in range(NHEAD):
        par = 64 * (h % 2)
        P_stage[par:par + 64, h * 64:(h + 1) * 64] = proj[h].T
    diagones = np.zeros((128, 8, 16), np.float32)
    for t in range(8):
        diagones[0:64, t, 2 * t] = 1.0
        diagones[64:128, t, 2 * t + 1] = 1.0

    def T(w):
        return np.ascontiguousarray(np.asarray(w).T)

    nc = build_L0A_stats()
    common_a = {"WkT": T(L0_attn_Wk), "WvT": T(L0_attn_Wv), "Pstage": P_stage,
                "bkc": _cols(np.asarray(L0_attn_bk, np.float32), 8),
                "bvr": np.asarray(L0_attn_bv, np.float32).reshape(1, D),
                "diagones": diagones, "idf": idf}
    res = run_bass_kernel_spmd(nc, [dict(common_a, xT=xT[c]) for c in cores], core_ids=cores)
    statsall = np.ascontiguousarray(
        np.concatenate([res.results[c]["stats"] for c in cores], axis=0))

    nc = build_L0A_apply()
    w8 = []
    kmask = []
    for c in cores:
        g = c // 4
        w = np.zeros((128, 8), np.float32)
        w[:, 4 * g:4 * g + 4] = 1.0
        w8.append(w)
        km = np.full((8, 16), -1e30, np.float32)
        km[4 * g:4 * g + 4, :] = 0.0
        kmask.append(km)
    common_b = {"statsall": statsall, "WqT": T(L0_attn_Wq), "WoT": T(L0_attn_Wo),
                "Pstage": P_stage, "bqc": _cols(np.asarray(L0_attn_bq, np.float32), 8),
                "boc": _cols(np.asarray(L0_attn_bo, np.float32), 8),
                "diagones": diagones, "idf": idf, "idb": idf}
    res = run_bass_kernel_spmd(
        nc, [dict(common_b, xT=xT[c], xhT=xhT[c], w8=w8[c], kmask8=kmask[c]) for c in cores],
        core_ids=cores)
    x1T = [res.results[c]["x1T"] for c in cores]
    x1hT = [res.results[c]["x1hT"] for c in cores]

    nc = build_L0F()
    common_c = {"wgT": T(L0_ffn_w_gate), "wuT": T(L0_ffn_w_up), "wdT": T(L0_ffn_w_down)}
    res = run_bass_kernel_spmd(
        nc, [dict(common_c, xT=x1T[c], xhT=x1hT[c]) for c in cores], core_ids=cores)
    x2T = [res.results[c]["x2T"] for c in cores]
    x2hT = [res.results[c]["x2hT"] for c in cores]

    nc = build_L1A()
    Wa = np.asarray(L1_attn_W_attn, np.float32)
    l1mask = []
    for c in cores:
        mk = np.zeros((128, 256), np.float32)
        if c % 4 == 0:
            t_ar = np.arange(128)[:, None]
            kk = np.tile(np.arange(16)[None, :], (1, 16)).reshape(1, 256)
            mk = np.where(kk <= t_ar, 0.0, -1e30).astype(np.float32)
        l1mask.append(np.ascontiguousarray(mk))
    common_d = {"WaqT": T(Wa[0:D]), "WakvT": T(Wa[D:3 * D]), "WpT": T(L1_attn_W_proj),
                "idb": idf}
    res = run_bass_kernel_spmd(
        nc, [dict(common_d, xT=x2T[c], xhT=x2hT[c], l1mask=l1mask[c]) for c in cores],
        core_ids=cores)
    x3T = [res.results[c]["x3T"] for c in cores]

    nc = build_L1F()
    common_e = {"w1T": T(L1_ffn_w1), "w2T": T(L1_ffn_w2),
                "b1c": _cols(np.asarray(L1_ffn_b1, np.float32), 32),
                "b2c": _cols(np.asarray(L1_ffn_b2, np.float32), 8), "idf": idf}
    res = run_bass_kernel_spmd(
        nc, [dict(common_e, xT=x3T[c]) for c in cores], core_ids=cores)
    y = np.concatenate([res.results[c]["y"] for c in cores], axis=0)
    return y.reshape(2, 16384, 1024)


# version 15
# speedup vs baseline: 1.2496x; 1.0460x over previous
import numpy as np
import ml_dtypes
from contextlib import ExitStack
import concourse.bass as bass
import concourse.mybir as mybir
from concourse.bass_utils import run_bass_kernel_spmd

F32 = mybir.dt.float32
BF16 = mybir.dt.bfloat16
I32 = mybir.dt.int32
AF = mybir.ActivationFunctionType
ALU = mybir.AluOpType
AXX = mybir.AxisListType.X

NCORE = 8
D = 1024
ROWS = 4096
HDR = 16
NHEAD = 16
CFAV = float(64.0 ** -0.25)
LN2_3 = float(3 * np.log(2.0))
EPS_FEAT = 1e-4
EPS_DEN = 0.01
SCALE_FFN0 = 1.0 / 127.0
STATS_ROWS = 80


class Ctx:
    def __init__(self, nc, es, parent=None, prefix="", io=None):
        self.nc = nc
        self.es = es
        self.prefix = prefix
        self.io = io or {}
        if parent is None:
            self.sems = {}
            self.cnt = {}
            for s in ['L', 'G', 'P', 'A', 'V', 'S']:
                self.sems[s] = es.enter_context(nc.semaphore(s))
                self.cnt[s] = 0
            self.solo = True
        else:
            self.sems = parent.sems
            self.cnt = parent.cnt
            self.solo = False

    def inc(self, name, instr, n=1):
        instr.then_inc(self.sems[name], n)
        self.cnt[name] += n

    def wait(self, eng, *names):
        for n in names:
            eng.wait_ge(self.sems[n], self.cnt[n])

    def sb(self, name, shape, dt):
        return self.es.enter_context(self.nc.sbuf_tensor("sb_" + self.prefix + name, shape, dt))

    def ps(self, name, shape, dt=F32):
        return self.es.enter_context(self.nc.psum_tensor(self.prefix + name, shape, dt))

    def dram(self, name, shape, dt, kind):
        if name in self.io:
            return self.io[name]
        if self.solo:
            k = {"in": "ExternalInput", "out": "ExternalOutput"}[kind]
            return self.nc.dram_tensor(name, shape, dt, kind=k).ap()
        return self.nc.dram_tensor(self.prefix + name, shape, dt).ap()


from contextlib import nullcontext


def full_barrier(c):
    nc = c.nc
    for eng in [nc.sync, nc.gpsimd, nc.tensor, nc.scalar, nc.vector]:
        for s in ['L', 'G', 'P', 'A', 'V', 'S']:
            eng.wait_ge(c.sems[s], c.cnt[s])


def begin(env, prefix):
    if env is None:
        nc = bass.Bass("TRN2", target_bir_lowering=False)
        es = ExitStack()
        c = Ctx(nc, es)
        blk = nc.Block()
    else:
        nc = env['nc']
        es = ExitStack()
        c = Ctx(nc, es, parent=env['c'], prefix=prefix, io=env.get('io', {}))
        blk = nullcontext()
    return nc, es, c, blk


def mk_ln(c, mmax, tag="ln", sq=None, ps_stat=None):
    return dict(
        mmax=mmax,
        ones=c.sb(tag + "_ones", [128, 1], BF16),
        eps=c.sb(tag + "_eps", [1, 1], F32),
        sq=sq if sq is not None else c.sb(tag + "_sq", [128, 8, mmax], BF16),
        mean=c.sb(tag + "_mean", [1, mmax], F32),
        msq=c.sb(tag + "_msq", [1, mmax], F32),
        var=c.sb(tag + "_var", [1, mmax], F32),
        rstd=c.sb(tag + "_rstd", [1, 2 * mmax], F32),
        bc=c.sb(tag + "_bc", [128, 2, mmax], F32),
        bounce=c.nc.dram_tensor(c.prefix + tag + "_bounce", [1, 2 * mmax], F32),
        ps_stat=ps_stat if ps_stat is not None else c.ps(tag + "_psstat", [128, 512], F32),
    )


def ln_init(c, ln):
    nc = c.nc
    nc.gpsimd.memset(ln['ones'][:, :], 2.0 ** -10)
    i = nc.gpsimd.memset(ln['eps'][:, :], 1e-5)
    c.inc('G', i)


def emit_ln(c, ln, xb_tiles, m):
    nc = c.nc
    mx_ = ln['mmax']
    for t in range(8):
        i = nc.scalar.activation(ln['sq'][:, t, :m], xb_tiles[t], AF.Square)
    c.inc('A', i)
    c.wait(nc.tensor, 'A')
    for t in range(8):
        i = nc.tensor.matmul(ln['ps_stat'][0:1, :m], ln['ones'][:, :], xb_tiles[t],
                             start=(t == 0), stop=(t == 7), skip_group_check=True)
    for t in range(8):
        i = nc.tensor.matmul(ln['ps_stat'][64:65, :m], ln['ones'][:, :], ln['sq'][:, t, :m],
                             start=(t == 0), stop=(t == 7), skip_group_check=True)
    c.inc('P', i)
    c.wait(nc.scalar, 'P')
    nc.scalar.copy(ln['mean'][:, :m], ln['ps_stat'][0:1, :m])
    i = nc.scalar.copy(ln['msq'][:, :m], ln['ps_stat'][64:65, :m])
    c.inc('A', i)
    c.wait(nc.vector, 'A')
    nc.vector.tensor_tensor(ln['var'][:, :m], ln['mean'][:, :m], ln['mean'][:, :m], op=ALU.mult)
    i = nc.vector.tensor_tensor(ln['var'][:, :m], ln['msq'][:, :m], ln['var'][:, :m], op=ALU.subtract)
    c.inc('V', i)
    c.wait(nc.scalar, 'V')
    nc.scalar.activation(ln['rstd'][:, 0:m], ln['var'][:, :m], AF.Ln, bias=ln['eps'][0:1, :])
    i = nc.scalar.activation(ln['rstd'][:, 0:m], ln['rstd'][:, 0:m], AF.Exp, scale=-0.5)
    c.inc('A', i)
    c.wait(nc.vector, 'A')
    i = nc.vector.scalar_tensor_tensor(ln['rstd'][:, mx_:mx_ + m], ln['mean'][:, :m], -1.0,
                                       ln['rstd'][:, 0:m], op0=ALU.mult, op1=ALU.mult)
    c.inc('V', i)
    c.wait(nc.sync, 'V', 'A')
    c.inc('L', nc.sync.dma_start(out=ln['bounce'][0:1, 0:mx_ + m], in_=ln['rstd'][:, 0:mx_ + m]), 16)
    c.wait(nc.sync, 'L')
    i = nc.sync.dma_start(
        out=ln['bc'][:, :, :m],
        in_=ln['bounce'].ap().squeeze(0).rearrange("(a x) -> a x", a=2)[:, 0:m]
             .unsqueeze(0).partition_broadcast(128).squeeze(1))
    c.inc('L', i, 16)


def emit_normalize(c, ln, src_tiles, out_tiles, m, tmpf, add_tiles=None, out3=None):
    nc = c.nc
    for t in range(8):
        nc.vector.scalar_tensor_tensor(tmpf[:, :m], src_tiles[t], 1.0,
                                       ln['bc'][:, 0, :m], op0=ALU.mult, op1=ALU.mult)
        if add_tiles is None:
            i = nc.vector.tensor_tensor(out_tiles[t], tmpf[:, :m], ln['bc'][:, 1, :m], op=ALU.add)
        else:
            nc.vector.tensor_tensor(tmpf[:, :m], tmpf[:, :m], ln['bc'][:, 1, :m], op=ALU.add)
            i = nc.vector.tensor_tensor(out3[t], tmpf[:, :m], add_tiles[t], op=ALU.add)
    c.inc('V', i)


def emit_chunk_load(c, xb, xf, m, xr=None, sl=None, xhr=None, need_f32=True):
    nc = c.nc
    if xhr is not None:
        i = nc.gpsimd.memset(xb[:, :, 0:m], 0.0)
        c.inc('G', i)
        if need_f32:
            i = nc.gpsimd.memset(xf[:, :, 0:m], 0.0)
            c.inc('G', i)
        c.inc('G', nc.gpsimd.dma_start(out=xb[:, :, 0:HDR], in_=xhr), 16)
        if need_f32:
            c.wait(nc.sync, 'G')
            c.inc('L', nc.sync.dma_start(out=xf[:, :, 0:HDR], in_=xhr), 16)
    else:
        c.inc('G', nc.gpsimd.dma_start(out=xb[:, :, :], in_=xr[:, :, sl]), 16)
        if need_f32:
            c.inc('L', nc.sync.dma_start(out=xf[:, :, :], in_=xr[:, :, sl]), 16)
    nc.scalar.wait_ge(c.sems['G'], c.cnt['G'])
    nc.tensor.wait_ge(c.sems['G'], c.cnt['G'])
    nc.vector.wait_ge(c.sems['G'], c.cnt['G'])


def emit_proj_fm(c, W, src, dst, m, ps_big, bias_col=None, scale=None):
    nc = c.nc
    kw = {}
    if scale is not None:
        kw['scale'] = np.float32(scale)
    ah = []
    for t in range(8):
        psl = slice((t % 2) * 512, (t % 2) * 512 + 512)
        if t >= 2:
            nc.tensor.wait_ge(c.sems['A'], ah[t - 2])
        for k in range(8):
            i = nc.tensor.matmul(ps_big[:, psl][:, 0:m], W[:, k, t * 128:(t + 1) * 128],
                                 src[:, k, 0:m], start=(k == 0), stop=(k == 7),
                                 skip_group_check=True)
        c.inc('P', i)
        c.wait(nc.scalar, 'P')
        if bias_col is not None:
            i = nc.scalar.activation(dst[:, t, 0:m], ps_big[:, psl][:, 0:m], AF.Identity,
                                     bias=bias_col[:, t:t + 1], **kw)
        elif scale is not None:
            i = nc.scalar.activation(dst[:, t, 0:m], ps_big[:, psl][:, 0:m], AF.Identity, **kw)
        else:
            i = nc.scalar.copy(dst[:, t, 0:m], ps_big[:, psl][:, 0:m])
        c.inc('A', i)
        ah.append(c.cnt['A'])


def emit_proj_fm2(c, W, src, raw, rawb, m, ps_big, bias_col=None, scale=None):
    nc = c.nc
    kw = {}
    if scale is not None:
        kw['scale'] = np.float32(scale)
    vh = []
    for t in range(8):
        psl = slice((t % 2) * 512, (t % 2) * 512 + 512)
        if t >= 2:
            nc.tensor.wait_ge(c.sems['V'], vh[t - 2])
        for k in range(8):
            i = nc.tensor.matmul(ps_big[:, psl][:, 0:m], W[:, k, t * 128:(t + 1) * 128],
                                 src[:, k, 0:m], start=(k == 0), stop=(k == 7),
                                 skip_group_check=True)
        c.inc('P', i)
        c.wait(nc.scalar, 'P')
        bkw = dict(kw)
        if bias_col is not None:
            bkw['bias'] = bias_col[:, t:t + 1]
        i = nc.scalar.activation(raw[:, t, 0:m], ps_big[:, psl][:, 0:m], AF.Identity, **bkw)
        c.inc('A', i)
        c.wait(nc.vector, 'A')
        i = nc.vector.tensor_copy(rawb[:, t, 0:m], raw[:, t, 0:m])
        c.inc('V', i)
        vh.append(c.cnt['V'])


def pe_barrier(c):
    nc = c.nc
    i = nc.tensor.nop()
    c.inc('P', i)
    nc.tensor.wait_ge(c.sems['P'], c.cnt['P'])


def emit_favorp_head_mm(c, src_bf, Pst, ps_big, msl):
    nc = c.nc
    for par_pass in range(2):
        for h in range(par_pass, NHEAD, 2):
            par = 64 * (h % 2)
            i = nc.tensor.matmul(ps_big[:, h * 64:(h + 1) * 64],
                                 src_bf[par:par + 64, h // 2, msl],
                                 Pst[par:par + 64, h * 64:(h + 1) * 64],
                                 start=(h in (0, 8)), stop=(h in (7, 15)),
                                 skip_group_check=True)
        if par_pass == 0:
            c.inc('P', i)
            nc.tensor.wait_ge(c.sems['P'], c.cnt['P'])
    c.inc('P', i)


def emit_diag(c, dia, sq_src, qbf, diag_fm, diag_tm, idf, ps_aux, ps_trf, m):
    nc = c.nc
    c.wait(nc.scalar, 'A')
    for t in range(8):
        i = nc.scalar.activation(sq_src[:, t, 0:m], qbf[:, t, 0:m], AF.Square)
    c.inc('A', i)
    c.wait(nc.tensor, 'A')
    for t in range(8):
        i = nc.tensor.matmul(ps_aux[0:16, 0:m], dia[:, t, :], sq_src[:, t, 0:m],
                             start=(t == 0), stop=(t == 7), skip_group_check=True)
    c.inc('P', i)
    c.wait(nc.scalar, 'P')
    i = nc.scalar.activation(diag_fm[:, 0:m], ps_aux[0:16, 0:m], AF.Identity,
                             bias=np.float32(2 * LN2_3))
    c.inc('A', i)
    c.wait(nc.tensor, 'A')
    nms = m // 128
    for ms in range(nms):
        i = nc.tensor.transpose(ps_trf[:, ms * 16:ms * 16 + 16],
                                diag_fm[:, ms * 128:(ms + 1) * 128], idf[0:16, 0:16])
    c.inc('P', i)
    c.wait(nc.scalar, 'P')
    for ms in range(nms):
        i = nc.scalar.copy(diag_tm[:, ms, :], ps_trf[:, ms * 16:ms * 16 + 16])
    c.inc('A', i)


def build_L0A_stats(env=None, rows=ROWS):
    nc, es, c, blk = begin(env, "pa_")
    M = 512
    NCH = rows // M
    xT = c.dram("xT", [D, rows], F32, "in")
    WkT_d = c.dram("WkT", [D, D], F32, "in")
    WvT_d = c.dram("WvT", [D, D], F32, "in")
    P_d = c.dram("Pstage", [128, D], F32, "in")
    bkc_d = c.dram("bkc", [128, 8], F32, "in")
    bvr_d = c.dram("bvr", [1, D], F32, "in")
    dia_d = c.dram("diagones", [128, 8, 16], F32, "in")
    idf_d = c.dram("idf", [128, 128], F32, "in")
    stats_d = c.dram("stats", [STATS_ROWS, D], F32, "out")

    WkT = c.sb("WkT", [128, 8, D], BF16)
    WvT = c.sb("WvT", [128, 8, D], BF16)
    Pst = c.sb("Pst", [128, D], BF16)
    bkc = c.sb("bkc", [128, 8], F32)
    bkcs = c.sb("bkcs", [128, 8], F32)
    bvr = c.sb("bvr", [1, D], BF16)
    dia = c.sb("dia", [128, 8, 16], BF16)
    idf = c.sb("idf", [128, 128], F32)
    ones_row = c.sb("ones_row", [1, 128], BF16)
    ones_col = c.sb("ones_colb", [128, 1], BF16)

    xb = c.sb("xb", [128, 8, M], BF16)
    xn = c.sb("xn", [128, 8, M], BF16)
    tmpf = c.sb("tmpf", [128, M], F32)
    kbf = c.sb("kbf", [128, 8, M], BF16)
    sqk = c.sb("sqk", [128, 8, M], BF16)
    vtm = c.sb("vtm", [128, 4, D], BF16)
    diag_fm = c.sb("diag_fm", [16, M], F32)
    diag_tm = c.sb("diag_tm", [128, 4, 16], F32)
    stab = c.sb("stab", [128, 16], F32)
    pre = c.sb("pre", [128, D], F32)
    keq = c.sb("keq", [128, 16, 65], BF16)
    kmax_acc = c.sb("kmax_acc", [128, 16], F32)
    km_t = c.sb("km_t", [128, 16], F32)
    kv_ev = c.sb("kv_ev", [128, 1024], F32)
    ks_ev = c.sb("ks_ev", [128, 8], F32)
    ks_tr = c.sb("ks_tr", [8, 128], F32)
    kmx_tr = c.sb("kmx_tr", [16, 128], F32)
    kmx_f = c.sb("kmx_f", [16, 1], F32)
    zerot = c.sb("zerot", [16, 1024], F32)

    ln = mk_ln(c, M)
    ps_kv = c.ps("ps_kv", [128, 1024])
    ps_ks = c.ps("ps_ks", [128, 512])
    ps_big = c.ps("ps_big", [128, 1024])
    ps_trf = c.ps("ps_trf", [128, 512])

    with blk:
        ln_init(c, ln)
        nc.gpsimd.memset(ones_row[:, :], 1.0)
        nc.gpsimd.memset(ones_col[:, :], 1.0)
        nc.gpsimd.memset(zerot[:, :], 0.0)
        nc.gpsimd.memset(keq[:, :, :], 1.0)
        i = nc.gpsimd.memset(kmax_acc[:, :], -1e30)
        c.inc('G', i)
        c.inc('G', nc.gpsimd.dma_start(out=WkT[:, :, :], in_=WkT_d.rearrange("(k p) n -> p k n", p=128)), 16)
        c.inc('G', nc.gpsimd.dma_start(out=WvT[:, :, :], in_=WvT_d.rearrange("(k p) n -> p k n", p=128)), 16)
        c.inc('G', nc.gpsimd.dma_start(out=Pst[:, :], in_=P_d), 16)
        c.inc('G', nc.gpsimd.dma_start(out=dia[:, :, :], in_=dia_d), 16)
        c.inc('G', nc.gpsimd.dma_start(out=bvr[:, :], in_=bvr_d), 16)
        c.inc('L', nc.sync.dma_start(out=bkc[:, :], in_=bkc_d), 16)
        c.inc('L', nc.sync.dma_start(out=idf[:, :], in_=idf_d), 16)
        nc.scalar.wait_ge(c.sems['L'], c.cnt['L'])
        i = nc.scalar.activation(bkcs[:, :], bkc[:, :], AF.Identity, scale=np.float32(CFAV))
        c.inc('A', i)

        xr = xT.rearrange("(k p) n -> p k n", p=128)
        for ch in range(NCH):
            sl = slice(ch * M, (ch + 1) * M)
            if ch:
                c.wait(nc.gpsimd, 'V', 'P')
                c.wait(nc.sync, 'V', 'P')
            emit_chunk_load(c, xb, None, M, xr=xr, sl=sl, need_f32=False)
            emit_ln(c, ln, [xb[:, t, :] for t in range(8)], M)
            c.wait(nc.vector, 'L')
            emit_normalize(c, ln, [xb[:, t, :] for t in range(8)],
                           [xn[:, t, :] for t in range(8)], M, tmpf)
            c.wait(nc.tensor, 'V', 'A')
            emit_proj_fm(c, WkT, xn, kbf, M, ps_big, bias_col=bkcs, scale=CFAV)
            vah = []
            va_base = c.cnt['A']
            for ms in range(4):
                msl = slice(ms * 128, ms * 128 + 128)
                for half in range(2):
                    g = ms * 2 + half
                    psl = slice(half * 512, half * 512 + 512)
                    dsl = slice(half * 512, half * 512 + 512)
                    nc.tensor.wait_ge(c.sems['A'], vah[g - 2] if g >= 2 else va_base)
                    for k in range(8):
                        nc.tensor.matmul(ps_big[:, psl], xn[:, k, msl], WvT[:, k, dsl],
                                         start=(k == 0), stop=False, skip_group_check=True)
                    i = nc.tensor.matmul(ps_big[:, psl], ones_row[:, :], bvr[:, dsl],
                                         start=False, stop=True, skip_group_check=True)
                    c.inc('P', i)
                    c.wait(nc.scalar, 'P')
                    i = nc.scalar.copy(vtm[:, ms, dsl], ps_big[:, psl])
                    c.inc('A', i)
                    vah.append(c.cnt['A'])
            emit_diag(c, dia, sqk, kbf, diag_fm, diag_tm, idf, ps_trf, ps_trf, M)
            for ms in range(4):
                msl = slice(ms * 128, ms * 128 + 128)
                c.wait(nc.tensor, 'A', 'V')
                emit_favorp_head_mm(c, kbf, Pst, ps_big, msl)
                c.wait(nc.vector, 'P', 'A')
                nc.vector.reduce_max(km_t[:, :].unsqueeze(2),
                                     ps_big[:, :].rearrange("p (h j) -> p h j", h=16),
                                     axis=AXX)
                nc.vector.tensor_tensor(kmax_acc[:, :], kmax_acc[:, :], km_t[:, :], op=ALU.max)
                i = nc.vector.tensor_scalar(stab[:, :], diag_tm[:, ms, :], -0.5, 0.0,
                                            op0=ALU.mult, op1=ALU.add)
                c.inc('V', i)
                nc.vector.wait_ge(c.sems['V'], c.cnt['V'])
                i = nc.vector.tensor_tensor(
                    pre[:, :].rearrange("p (h j) -> p h j", h=16),
                    ps_big[:, :].rearrange("p (h j) -> p h j", h=16),
                    stab[:, :].unsqueeze(2).broadcast_to([128, 16, 64]), op=ALU.add)
                c.inc('V', i)
                c.wait(nc.scalar, 'V')
                i = nc.scalar.activation(keq[:, :, 0:64], pre[:, :].rearrange("p (h j) -> p h j", h=16), AF.Exp)
                c.inc('A', i)
                c.wait(nc.tensor, 'A')
                first = (ch == 0 and ms == 0)
                last = (ch == NCH - 1 and ms == 3)
                for h in range(NHEAD):
                    par = 64 * (h % 2)
                    pr = h // 2
                    nc.tensor.matmul(ps_kv[0:65, h * 64:(h + 1) * 64],
                                     keq[:, h, :],
                                     vtm[:, ms, h * 64:(h + 1) * 64],
                                     start=(first and h in (0, 8)), stop=(last and h in (7, 15)),
                                     skip_group_check=True)
                    i = nc.tensor.matmul(ps_ks[par:par + 64, pr:pr + 1],
                                         keq[:, h, 0:64], ones_col[:, :],
                                         start=(first and h == 0), stop=(last and h == 15),
                                         skip_group_check=True)
                c.inc('P', i)
        c.wait(nc.scalar, 'P')
        nc.scalar.copy(kv_ev[0:65, :], ps_kv[0:65, :])
        i = nc.scalar.copy(ks_ev[:, :], ps_ks[:, 0:8])
        c.inc('A', i)
        c.wait(nc.tensor, 'A')
        i = nc.tensor.transpose(ps_trf[0:8, 0:128], ks_ev[:, :], idf[:, :])
        c.inc('P', i)
        i = nc.tensor.transpose(ps_trf[0:16, 128:256], kmax_acc[:, :], idf[:, :])
        c.inc('P', i)
        c.wait(nc.scalar, 'P')
        nc.scalar.copy(ks_tr[:, :], ps_trf[0:8, 0:128])
        i = nc.scalar.copy(kmx_tr[:, :], ps_trf[0:16, 128:256])
        c.inc('A', i)
        c.wait(nc.vector, 'A')
        i = nc.vector.reduce_max(kmx_f[:, :], kmx_tr[:, :], axis=AXX)
        c.inc('V', i)
        c.wait(nc.sync, 'A', 'V', 'G')
        c.inc('S', nc.sync.dma_start(out=stats_d[0:64, :], in_=kv_ev[0:64, :]), 16)
        c.inc('S', nc.sync.dma_start(out=stats_d[64:72, 0:128], in_=ks_tr[:, :]), 16)
        c.inc('S', nc.sync.dma_start(out=stats_d[72:73, :], in_=kv_ev[64:65, :]), 16)
        c.inc('S', nc.sync.dma_start(out=stats_d[73:74, 0:16].transpose([1, 0]), in_=kmx_f[:, :]), 16)
        c.inc('S', nc.sync.dma_start(out=stats_d[64:72, 128:1024], in_=zerot[0:8, 0:896]), 16)
        c.inc('S', nc.sync.dma_start(out=stats_d[73:74, 16:1024], in_=zerot[0:1, 0:1008]), 16)
        c.inc('S', nc.sync.dma_start(out=stats_d[74:80, :], in_=zerot[0:6, :]), 16)
        c.wait(nc.sync, 'S')
        c.wait(nc.gpsimd, 'S')
    es.close()
    return nc


def build_L0A_apply(env=None, local_stats=False):
    nc, es, c, blk = begin(env, "pb_")
    M = 512
    xT = c.dram("xT", [D, ROWS], F32, "in")
    xhT = c.dram("xhT", [D, HDR], F32, "in")
    statsall_d = c.dram("statsall", [NCORE * STATS_ROWS, D], F32, "in")
    if not local_stats:
        w8_d = c.dram("w8", [128, 8], F32, "in")
        kmask_d = c.dram("kmask8", [8, 16], F32, "in")
    WqT_d = c.dram("WqT", [D, D], F32, "in")
    WoT_d = c.dram("WoT", [D, D], F32, "in")
    P_d = c.dram("Pstage", [128, D], F32, "in")
    bqc_d = c.dram("bqc", [128, 8], F32, "in")
    boc_d = c.dram("boc", [128, 8], F32, "in")
    dia_d = c.dram("diagones", [128, 8, 16], F32, "in")
    idf_d = c.dram("idf", [128, 128], F32, "in")
    idb_d = c.dram("idb", [128, 128], F32, "in")
    x1T_d = c.dram("x1T", [D, ROWS], F32, "out")
    x1hT_d = c.dram("x1hT", [D, HDR], F32, "out")

    f_dram = nc.dram_tensor(c.prefix + "f_dram", [1, 16], F32)
    vs_dram = nc.dram_tensor(c.prefix + "vs_dram", [1, D], F32)
    ks_dram = nc.dram_tensor(c.prefix + "ks_dram", [8, 128], F32)

    WqT = c.sb("WqT", [128, 8, D], BF16)
    WoT = c.sb("WoT", [128, 8, D], BF16)
    Pst = c.sb("Pst", [128, D], BF16)
    bqc = c.sb("bqc", [128, 8], F32)
    bqcs = c.sb("bqcs", [128, 8], F32)
    boc = c.sb("boc", [128, 8], F32)
    dia = c.sb("dia", [128, 8, 16], BF16)
    idf = c.sb("idf", [128, 128], F32)
    idb = c.sb("idb", [128, 128], BF16)
    w8 = c.sb("w8", [128, 8], F32)
    kmask = c.sb("kmask", [8, 16], F32)

    st1 = c.sb("st1", [80, D], F32)
    stsum = c.sb("stsum", [80, D], F32)
    km8 = c.sb("km8", [8, 16], F32)
    kmx_tr = c.sb("kmx_tr2", [16, 8], F32)
    stabk = c.sb("stabk", [16, 1], F32)
    fexp = c.sb("fexp", [16, 1], F32)
    kv_stage = c.sb("kv_stage", [128, 512], F32)
    f_pack = c.sb("f_pack", [128, 8], F32)
    vs_pack = c.sb("vs_pack", [128, 8, 64], F32)
    kv_sb = c.sb("kv_sb", [128, 8, 64], BF16)
    ks_t = c.sb("ks_t", [8, 128], F32)
    f_e = c.sb("f_e", [8, 1], F32)
    f_o = c.sb("f_o", [8, 1], F32)
    BDk = c.sb("BDk", [128, 8, 16], BF16)

    xb = c.sb("xb", [128, 8, M], BF16)
    xf = c.sb("xf", [128, 8, M], F32)
    xn = c.sb("xn", [128, 8, M], BF16)
    tmpf = c.sb("tmpf", [128, M], F32)
    qbf = c.sb("qbf", [128, 8, M], BF16)
    sqq = c.sb("sqq", [128, 8, M], BF16)
    diag_fm = c.sb("diag_fm", [16, M], F32)
    diag_tm = c.sb("diag_tm", [128, 4, 16], F32)
    stab = c.sb("stab", [128, 16], F32)
    pre = c.sb("pre", [128, D], F32)
    qp_tm = c.sb("qp_tm", [128, 4, D], BF16)
    qp_fm = c.sb("qp_fm", [128, 8, M], BF16)
    nrm = c.sb("nrm", [16, M], F32)
    rec_tm = c.sb("rec_tm", [128, 4, 16], F32)
    at_tm = c.sb("at_tm", [128, D], BF16)
    at_fm = c.sb("at_fm", [128, 8, M], BF16)
    raw = c.sb("raw", [128, 8, M], F32)
    rawb = c.sb("rawb", [128, 8, M], BF16)
    x1 = c.sb("x1", [128, 8, M], F32)

    ln = mk_ln(c, M)
    ps_aux = c.ps("ps_aux", [128, 512])
    ps_big = c.ps("ps_big", [128, 1024])
    ps_trb = c.ps("ps_trb", [128, 1024], BF16)
    ps_trf = c.ps("ps_trf", [128, 512])

    with blk:
        ln_init(c, ln)
        c.inc('G', nc.gpsimd.dma_start(out=WqT[:, :, :], in_=WqT_d.rearrange("(k p) n -> p k n", p=128)), 16)
        c.inc('G', nc.gpsimd.dma_start(out=WoT[:, :, :], in_=WoT_d.rearrange("(k p) n -> p k n", p=128)), 16)
        c.inc('G', nc.gpsimd.dma_start(out=Pst[:, :], in_=P_d), 16)
        c.inc('G', nc.gpsimd.dma_start(out=dia[:, :, :], in_=dia_d), 16)
        c.inc('G', nc.gpsimd.dma_start(out=idb[:, :], in_=idb_d), 16)
        c.inc('L', nc.sync.dma_start(out=idf[:, :], in_=idf_d), 16)
        c.inc('L', nc.sync.dma_start(out=bqc[:, :], in_=bqc_d), 16)
        c.inc('L', nc.sync.dma_start(out=boc[:, :], in_=boc_d), 16)
        if not local_stats:
            c.inc('L', nc.sync.dma_start(out=w8[:, :], in_=w8_d), 16)
            c.inc('L', nc.sync.dma_start(out=kmask[:, :], in_=kmask_d), 16)
        nc.scalar.wait_ge(c.sems['L'], c.cnt['L'])
        i = nc.scalar.activation(bqcs[:, :], bqc[:, :], AF.Identity, scale=np.float32(CFAV))
        c.inc('A', i)
        if local_stats:
            c.inc('L', nc.sync.dma_start(out=stsum[:, :], in_=statsall_d[0:STATS_ROWS, :]), 16)
            with nc.allow_non_contiguous_dma(reason="16-element kmax row"):
                c.inc('L', nc.sync.dma_start(out=stabk[:, :],
                                             in_=statsall_d[73:74, 0:16].transpose([1, 0])), 16)
            c.wait(nc.vector, 'L')
        else:
            sall = statsall_d.rearrange("(r s) n -> r s n", s=STATS_ROWS)
            for r in range(8):
                c.wait(nc.sync, 'V')
                c.inc('L', nc.sync.dma_start(out=st1[:, :], in_=sall[r:r + 1, :, :].squeeze(0)), 16)
                c.wait(nc.vector, 'L')
                if r == 0:
                    i = nc.vector.tensor_scalar(stsum[:, :], st1[:, :], w8[0:80, 0:1], 0.0,
                                                op0=ALU.mult, op1=ALU.add)
                else:
                    i = nc.vector.scalar_tensor_tensor(stsum[:, :], st1[:, :], w8[0:80, r:r + 1],
                                                       stsum[:, :], op0=ALU.mult, op1=ALU.add)
                c.inc('V', i)
            c.inc('L', nc.sync.dma_start(
                out=km8[:, :],
                in_=statsall_d.rearrange("(r s) n -> r s n", s=STATS_ROWS)[:, 73:74, 0:16].squeeze(1)), 16)
            c.wait(nc.vector, 'L')
            i = nc.vector.tensor_tensor(km8[:, :], km8[:, :], kmask[:, :], op=ALU.add)
            c.inc('V', i)
            c.wait(nc.tensor, 'V', 'L')
            i = nc.tensor.transpose(ps_trf[0:16, 0:8], km8[:, :], idf[0:8, 0:8])
            c.inc('P', i)
            c.wait(nc.scalar, 'P')
            i = nc.scalar.copy(kmx_tr[:, :], ps_trf[0:16, 0:8])
            c.inc('A', i)
            c.wait(nc.vector, 'A')
            i = nc.vector.reduce_max(stabk[:, :], kmx_tr[:, :], axis=AXX)
            c.inc('V', i)
        c.wait(nc.scalar, 'V', 'L')
        i = nc.scalar.activation(fexp[:, :], stabk[:, :], AF.Exp, scale=-1.0)
        c.inc('A', i)
        c.wait(nc.gpsimd, 'A', 'V')
        stk = stsum[0:64, :].rearrange("p (a q x) -> p a q x", a=8, q=2)
        c.inc('G', nc.gpsimd.dma_start(
            out=kv_stage[0:64, :].rearrange("p (a x) -> p a x", a=8),
            in_=stk[:, :, 0:1, :].squeeze(2)), 16)
        c.inc('G', nc.gpsimd.dma_start(
            out=kv_stage[64:128, :].rearrange("p (a x) -> p a x", a=8),
            in_=stk[:, :, 1:2, :].squeeze(2)), 16)
        c.inc('G', nc.gpsimd.dma_start(out=f_dram.ap().transpose([1, 0]), in_=fexp[:, :]), 16)
        c.inc('G', nc.gpsimd.dma_start(out=ks_dram[:, :], in_=stsum[64:72, 0:128]), 16)
        c.inc('G', nc.gpsimd.dma_start(out=vs_dram[:, :], in_=stsum[72:73, :]), 16)
        c.wait(nc.gpsimd, 'G')
        fap = f_dram.ap().squeeze(0).rearrange("(a p) -> a p", a=8)
        with nc.allow_non_contiguous_dma(reason="16-element parity unpack"):
            c.inc('G', nc.gpsimd.dma_start(
                out=f_pack[0:64, :].unsqueeze(2),
                in_=fap[:, 0:1].unsqueeze(0).partition_broadcast(64).squeeze(1)), 16)
            c.inc('G', nc.gpsimd.dma_start(
                out=f_pack[64:128, :].unsqueeze(2),
                in_=fap[:, 1:2].unsqueeze(0).partition_broadcast(64).squeeze(1)), 16)
            c.inc('G', nc.gpsimd.dma_start(out=f_e[:, :], in_=fap[:, 0:1]), 16)
            c.inc('G', nc.gpsimd.dma_start(out=f_o[:, :], in_=fap[:, 1:2]), 16)
        vap = vs_dram.ap().squeeze(0).rearrange("(a p x) -> a p x", a=8, p=2)
        c.inc('G', nc.gpsimd.dma_start(
            out=vs_pack[0:64, :, :], in_=vap[:, 0:1, :].squeeze(1).unsqueeze(0).partition_broadcast(64).squeeze(1)), 16)
        c.inc('G', nc.gpsimd.dma_start(
            out=vs_pack[64:128, :, :], in_=vap[:, 1:2, :].squeeze(1).unsqueeze(0).partition_broadcast(64).squeeze(1)), 16)
        c.inc('G', nc.gpsimd.dma_start(out=ks_t[:, :], in_=ks_dram[:, :]), 16)
        c.wait(nc.vector, 'G')
        nc.vector.tensor_scalar(vs_pack[:, :, :], vs_pack[:, :, :], float(EPS_FEAT), 0.0,
                                op0=ALU.mult, op1=ALU.add)
        nc.vector.tensor_tensor(
            kv_stage[:, :].rearrange("p (a x) -> p a x", a=8),
            kv_stage[:, :].rearrange("p (a x) -> p a x", a=8),
            f_pack[:, :].unsqueeze(2).broadcast_to([128, 8, 64]), op=ALU.mult)
        i = nc.vector.tensor_tensor(kv_sb[:, :, :],
                                    kv_stage[:, :].rearrange("p (a x) -> p a x", a=8),
                                    vs_pack[:, :, :], op=ALU.add)
        nc.vector.tensor_scalar(ks_t[:, 0:64], ks_t[:, 0:64], f_e[0:8, 0:1],
                                float(EPS_FEAT * 16384.0), op0=ALU.mult, op1=ALU.add)
        i = nc.vector.tensor_scalar(ks_t[:, 64:128], ks_t[:, 64:128], f_o[0:8, 0:1],
                                    float(EPS_FEAT * 16384.0), op0=ALU.mult, op1=ALU.add)
        c.inc('V', i)
        i = nc.gpsimd.memset(BDk[:, :, :], 0.0)
        c.inc('G', i)
        c.wait(nc.gpsimd, 'V')
        c.inc('G', nc.gpsimd.dma_start(out=ks_dram[:, :], in_=ks_t[:, :]), 16)
        c.wait(nc.gpsimd, 'G')
        for t in range(8):
            for par in range(2):
                i = nc.gpsimd.dma_start(
                    out=BDk[64 * par:64 * par + 64, t, 2 * t + par:2 * t + par + 1],
                    in_=ks_dram.ap()[t:t + 1, 64 * par:64 * par + 64].transpose([1, 0]))
                c.inc('G', i, 16)

        xr = xT.rearrange("(k p) n -> p k n", p=128)
        x1r = x1T_d.rearrange("(k p) n -> p k n", p=128)
        xhr = xhT.rearrange("(k p) n -> p k n", p=128)
        x1hr = x1hT_d.rearrange("(k p) n -> p k n", p=128)
        for ch in range(9):
            hdr = (ch == 8)
            m = 128 if hdr else M
            nms = m // 128
            sl = None if hdr else slice(ch * M, (ch + 1) * M)
            if ch:
                c.wait(nc.gpsimd, 'V')
                c.wait(nc.sync, 'V')
            emit_chunk_load(c, xb, xf, m, xr=xr, sl=sl, xhr=(xhr if hdr else None))
            emit_ln(c, ln, [xb[:, t, 0:m] for t in range(8)], m)
            c.wait(nc.vector, 'L')
            emit_normalize(c, ln, [xb[:, t, 0:m] for t in range(8)],
                           [xn[:, t, 0:m] for t in range(8)], m, tmpf)
            c.wait(nc.tensor, 'V', 'A')
            emit_proj_fm(c, WqT, xn, qbf, m, ps_big, bias_col=bqcs, scale=CFAV)
            emit_diag(c, dia, sqq, qbf, diag_fm, diag_tm, idf, ps_trf, ps_trf, m)
            for ms in range(nms):
                msl = slice(ms * 128, ms * 128 + 128)
                c.wait(nc.tensor, 'A', 'V')
                emit_favorp_head_mm(c, qbf, Pst, ps_big, msl)
                c.wait(nc.vector, 'P', 'A')
                nc.vector.reduce_max(stab[:, :].unsqueeze(2),
                                     ps_big[:, :].rearrange("p (h j) -> p h j", h=16),
                                     axis=AXX)
                i = nc.vector.scalar_tensor_tensor(stab[:, :], diag_tm[:, ms, :], -0.5,
                                                   stab[:, :], op0=ALU.mult, op1=ALU.subtract)
                c.inc('V', i)
                nc.vector.wait_ge(c.sems['V'], c.cnt['V'])
                i = nc.vector.tensor_tensor(
                    pre[:, :].rearrange("p (h j) -> p h j", h=16),
                    ps_big[:, :].rearrange("p (h j) -> p h j", h=16),
                    stab[:, :].unsqueeze(2).broadcast_to([128, 16, 64]), op=ALU.add)
                c.inc('V', i)
                c.wait(nc.scalar, 'V')
                i = nc.scalar.activation(pre[:, :], pre[:, :], AF.Exp)
                c.inc('A', i)
                c.wait(nc.vector, 'A')
                i = nc.vector.tensor_scalar(qp_tm[:, ms, :], pre[:, :], 1.0, float(EPS_FEAT),
                                            op0=ALU.mult, op1=ALU.add)
                c.inc('V', i)
                c.wait(nc.tensor, 'V')
                for t in range(8):
                    i = nc.tensor.transpose(ps_trb[:, t * 128:(t + 1) * 128],
                                            qp_tm[:, ms, t * 128:(t + 1) * 128], idb[:, :])
                c.inc('P', i)
                c.wait(nc.vector, 'P')
                for t in range(8):
                    i = nc.vector.tensor_copy(qp_fm[:, t, msl], ps_trb[:, t * 128:(t + 1) * 128])
                c.inc('V', i)
            c.wait(nc.tensor, 'V')
            for t in range(8):
                i = nc.tensor.matmul(ps_aux[32:48, 0:m], BDk[:, t, :], qp_fm[:, t, 0:m],
                                     start=(t == 0), stop=(t == 7), skip_group_check=True)
            c.inc('P', i)
            c.wait(nc.scalar, 'P')
            i = nc.scalar.copy(nrm[:, 0:m], ps_aux[32:48, 0:m])
            c.inc('A', i)
            c.wait(nc.vector, 'A')
            nc.vector.tensor_scalar_max(nrm[:, 0:m], nrm[:, 0:m], float(EPS_DEN))
            i = nc.vector.reciprocal(nrm[:, 0:m], nrm[:, 0:m])
            c.inc('V', i)
            c.wait(nc.tensor, 'V')
            for ms in range(nms):
                i = nc.tensor.transpose(ps_trf[:, 256 + ms * 16:256 + ms * 16 + 16],
                                        nrm[:, ms * 128:(ms + 1) * 128], idf[0:16, 0:16])
                c.inc('P', i)
            c.wait(nc.scalar, 'P')
            for ms in range(nms):
                i = nc.scalar.copy(rec_tm[:, ms, :], ps_trf[:, 256 + ms * 16:256 + ms * 16 + 16])
            c.inc('A', i)
            for ms in range(nms):
                msl = slice(ms * 128, ms * 128 + 128)
                c.wait(nc.tensor, 'A', 'V')
                for par_pass in range(2):
                    for h in range(par_pass, NHEAD, 2):
                        par = 64 * (h % 2)
                        i = nc.tensor.matmul(ps_big[:, h * 64:(h + 1) * 64],
                                             qp_fm[par:par + 64, h // 2, msl],
                                             kv_sb[par:par + 64, h // 2, :],
                                             start=(h in (0, 8)), stop=(h in (7, 15)),
                                             skip_group_check=True)
                    if par_pass == 0:
                        c.inc('P', i)
                        nc.tensor.wait_ge(c.sems['P'], c.cnt['P'])
                c.inc('P', i)
                c.wait(nc.vector, 'P', 'A')
                i = nc.vector.tensor_tensor(
                    at_tm[:, :].rearrange("p (h j) -> p h j", h=16),
                    ps_big[:, :].rearrange("p (h j) -> p h j", h=16),
                    rec_tm[:, ms, :].unsqueeze(2).broadcast_to([128, 16, 64]), op=ALU.mult)
                c.inc('V', i)
                c.wait(nc.tensor, 'V')
                for t in range(8):
                    i = nc.tensor.transpose(ps_trb[:, t * 128:(t + 1) * 128],
                                            at_tm[:, t * 128:(t + 1) * 128], idb[:, :])
                c.inc('P', i)
                c.wait(nc.vector, 'P')
                for t in range(8):
                    i = nc.vector.tensor_copy(at_fm[:, t, msl], ps_trb[:, t * 128:(t + 1) * 128])
                c.inc('V', i)
            c.wait(nc.tensor, 'V')
            emit_proj_fm2(c, WoT, at_fm, raw, rawb, m, ps_big, bias_col=boc)
            nc.vector.wait_ge(c.sems['L'], c.cnt['L'])
            c.wait(nc.scalar, 'V')
            emit_ln(c, ln, [rawb[:, t, 0:m] for t in range(8)], m)
            c.wait(nc.vector, 'L')
            emit_normalize(c, ln, [raw[:, t, 0:m] for t in range(8)], None, m, tmpf,
                           add_tiles=[xf[:, t, 0:m] for t in range(8)],
                           out3=[x1[:, t, 0:m] for t in range(8)])
            c.wait(nc.sync, 'V')
            if hdr:
                c.inc('S', nc.sync.dma_start(out=x1hr, in_=x1[:, :, 0:HDR]), 16)
            else:
                c.inc('S', nc.sync.dma_start(out=x1r[:, :, sl], in_=x1[:, :, :]), 16)
        c.wait(nc.sync, 'S')
        c.wait(nc.gpsimd, 'S')
    es.close()
    return nc


def build_L0F(env=None):
    nc, es, c, blk = begin(env, "pc_")
    M = 256
    H = 3072
    HT = H // 128
    xT = c.dram("xT", [D, ROWS], F32, "in")
    xhT = c.dram("xhT", [D, HDR], F32, "in")
    wgT_d = c.dram("wgT", [D, H], I32, "in")
    wuT_d = c.dram("wuT", [D, H], I32, "in")
    wdT_d = c.dram("wdT", [H, D], I32, "in")
    x2T_d = c.dram("x2T", [D, ROWS], F32, "out")
    x2hT_d = c.dram("x2hT", [D, HDR], F32, "out")

    wg = c.sb("wg", [128, 8, H], BF16)
    wu = c.sb("wu", [128, 8, H], BF16)
    wd = c.sb("wd", [128, HT, D], BF16)
    xb = c.sb("xb", [128, 8, M], BF16)
    xf = c.sb("xf", [128, 8, M], F32)
    xn = c.sb("xn", [128, 8, M], BF16)
    tmpf = c.sb("tmpf", [128, M], F32)
    sil = c.sb("sil", [128, 2, M], F32)
    hid = c.sb("hid", [128, HT, M], BF16)
    raw = c.sb("raw", [128, 8, M], F32)
    rawb = c.sb("rawb", [128, 8, M], BF16)
    ln = mk_ln(c, M)
    ln2 = dict(ln, bc=c.sb("ln2_bc", [128, 2, M], F32),
               ps_stat=c.ps("ln2_psstat", [128, 512], F32))
    ps_g = [c.ps("ps_g0", [128, 512]), c.ps("ps_g1", [128, 512])]
    ps_u = [c.ps("ps_u0", [128, 512]), c.ps("ps_u1", [128, 512])]
    ps_d = [c.ps("ps_d0", [128, 512]), c.ps("ps_d1", [128, 512])]

    def gate_up(m):
        vhist = []
        for ht in range(HT):
            hsl = slice(ht * 128, (ht + 1) * 128)
            pg, pu = ps_g[ht % 2], ps_u[ht % 2]
            if ht >= 2:
                nc.tensor.wait_ge(c.sems['V'], vhist[ht - 2])
            for k in range(8):
                nc.tensor.matmul(pg[:, 0:m], wg[:, k, hsl], xn[:, k, 0:m],
                                 start=(k == 0), stop=(k == 7), skip_group_check=True)
            for k in range(8):
                i = nc.tensor.matmul(pu[:, 0:m], wu[:, k, hsl], xn[:, k, 0:m],
                                     start=(k == 0), stop=(k == 7), skip_group_check=True)
            c.inc('P', i)
            c.wait(nc.scalar, 'P')
            i = nc.scalar.activation(sil[:, ht % 2, 0:m], pg[:, 0:m], AF.Silu,
                                     scale=np.float32(SCALE_FFN0))
            c.inc('A', i)
            c.wait(nc.vector, 'A')
            i = nc.vector.scalar_tensor_tensor(hid[:, ht, 0:m], pu[:, 0:m], SCALE_FFN0,
                                               sil[:, ht % 2, 0:m], op0=ALU.mult, op1=ALU.mult)
            c.inc('V', i)
            vhist.append(c.cnt['V'])

    def down(m):
        c.wait(nc.tensor, 'V')
        dvh = []
        for t in range(8):
            dsl = slice(t * 128, (t + 1) * 128)
            pd = ps_d[t % 2]
            if t >= 2:
                nc.tensor.wait_ge(c.sems['V'], dvh[t - 2])
            for ht in range(HT):
                i = nc.tensor.matmul(pd[:, 0:m], wd[:, ht, dsl], hid[:, ht, 0:m],
                                     start=(ht == 0), stop=(ht == HT - 1), skip_group_check=True)
            c.inc('P', i)
            c.wait(nc.scalar, 'P')
            i = nc.scalar.activation(raw[:, t, 0:m], pd[:, 0:m], AF.Identity,
                                     scale=np.float32(SCALE_FFN0))
            c.inc('A', i)
            c.wait(nc.vector, 'A')
            i = nc.vector.tensor_copy(rawb[:, t, 0:m], raw[:, t, 0:m])
            c.inc('V', i)
            dvh.append(c.cnt['V'])

    with blk:
        ln_init(c, ln)
        c.inc('G', nc.gpsimd.dma_start(out=wg[:, :, :], in_=wgT_d.rearrange("(k p) h -> p k h", p=128)), 16)
        c.inc('G', nc.gpsimd.dma_start(out=wu[:, :, :], in_=wuT_d.rearrange("(k p) h -> p k h", p=128)), 16)
        c.inc('G', nc.gpsimd.dma_start(out=wd[:, :, :], in_=wdT_d.rearrange("(k p) h -> p k h", p=128)), 16)

        xr = xT.rearrange("(k p) n -> p k n", p=128)
        x2r = x2T_d.rearrange("(k p) n -> p k n", p=128)
        xhr = xhT.rearrange("(k p) n -> p k n", p=128)
        x2hr = x2hT_d.rearrange("(k p) n -> p k n", p=128)
        NCH = 16
        c.inc('G', nc.gpsimd.dma_start(out=xb[:, :, :], in_=xr[:, :, 0:M]), 16)
        c.inc('L', nc.sync.dma_start(out=xf[:, :, :], in_=xr[:, :, 0:M]), 16)
        nc.scalar.wait_ge(c.sems['G'], c.cnt['G'])
        nc.tensor.wait_ge(c.sems['G'], c.cnt['G'])
        nc.vector.wait_ge(c.sems['G'], c.cnt['G'])
        emit_ln(c, ln, [xb[:, t, :] for t in range(8)], M)
        for ch in range(NCH):
            sl = slice(ch * M, (ch + 1) * M)
            c.wait(nc.vector, 'L')
            emit_normalize(c, ln, [xb[:, t, :] for t in range(8)],
                           [xn[:, t, :] for t in range(8)], M, tmpf)
            v_norm = c.cnt['V']
            c.wait(nc.tensor, 'V')
            gate_up(M)
            if ch + 1 < NCH:
                nsl = slice((ch + 1) * M, (ch + 2) * M)
                nc.gpsimd.wait_ge(c.sems['V'], v_norm)
                c.inc('G', nc.gpsimd.dma_start(out=xb[:, :, :], in_=xr[:, :, nsl]), 16)
                nc.scalar.wait_ge(c.sems['G'], c.cnt['G'])
                nc.tensor.wait_ge(c.sems['G'], c.cnt['G'])
                nc.vector.wait_ge(c.sems['G'], c.cnt['G'])
                emit_ln(c, ln, [xb[:, t, :] for t in range(8)], M)
            down(M)
            c.wait(nc.scalar, 'V', 'L')
            emit_ln(c, ln2, [rawb[:, t, :] for t in range(8)], M)
            c.wait(nc.vector, 'L')
            emit_normalize(c, ln2, [raw[:, t, :] for t in range(8)], None, M, tmpf,
                           add_tiles=[xf[:, t, :] for t in range(8)],
                           out3=[xf[:, t, :] for t in range(8)])
            c.wait(nc.sync, 'V')
            c.inc('S', nc.sync.dma_start(out=x2r[:, :, sl], in_=xf[:, :, :]), 16)
            if ch + 1 < NCH:
                c.wait(nc.sync, 'S')
                c.inc('L', nc.sync.dma_start(out=xf[:, :, :], in_=xr[:, :, nsl]), 16)
        m = HDR * 8
        c.wait(nc.sync, 'V', 'S')
        c.wait(nc.gpsimd, 'V', 'S')
        emit_chunk_load(c, xb, xf, m, xhr=xhr)
        emit_ln(c, ln, [xb[:, t, 0:m] for t in range(8)], m)
        c.wait(nc.vector, 'L')
        emit_normalize(c, ln, [xb[:, t, 0:m] for t in range(8)],
                       [xn[:, t, 0:m] for t in range(8)], m, tmpf)
        c.wait(nc.tensor, 'V')
        gate_up(m)
        down(m)
        c.wait(nc.scalar, 'V', 'L')
        emit_ln(c, ln2, [rawb[:, t, 0:m] for t in range(8)], m)
        c.wait(nc.vector, 'L')
        emit_normalize(c, ln2, [raw[:, t, 0:m] for t in range(8)], None, m, tmpf,
                       add_tiles=[xf[:, t, 0:m] for t in range(8)],
                       out3=[xf[:, t, 0:m] for t in range(8)])
        c.wait(nc.sync, 'V')
        c.inc('S', nc.sync.dma_start(out=x2hr, in_=xf[:, :, 0:HDR]), 16)
        c.wait(nc.sync, 'S')
        c.wait(nc.gpsimd, 'S')
    es.close()
    return nc


def build_L1A(env=None):
    nc, es, c, blk = begin(env, "pd_")
    M = 512
    xT = c.dram("xT", [D, ROWS], F32, "in")
    xhT = c.dram("xhT", [D, HDR], F32, "in")
    WaqT_d = c.dram("WaqT", [D, D], F32, "in")
    WakvT_d = c.dram("WakvT", [D, 2 * D], F32, "in")
    WpT_d = c.dram("WpT", [D, D], F32, "in")
    mask_d = c.dram("l1mask", [128, 256], F32, "in")
    idb_d = c.dram("idb", [128, 128], F32, "in")
    x3T_d = c.dram("x3T", [D, ROWS], F32, "out")
    vh_dram = nc.dram_tensor(c.prefix + "vh_dram", [HDR, D], F32)

    Waq = c.sb("Waq", [128, 8, D], BF16)
    WpT = c.sb("WpT", [128, 8, D], BF16)
    mask = c.sb("mask", [128, 256], F32)
    idb = c.sb("idb", [128, 128], BF16)
    khdr = c.sb("khdr", [128, 8, HDR], BF16)
    vblk = c.sb("vblk", [128, 2, 512], BF16)
    xb = c.sb("xb", [128, 8, M], BF16)
    xf = c.sb("xf", [128, 8, M], F32)
    xn = c.sb("xn", [128, 8, M], BF16)
    tmpf = c.sb("tmpf", [128, M], F32)
    qfm = c.sb("qfm", [128, 8, M], BF16)
    lg = c.sb("lg", [128, 256], F32)
    mx = c.sb("mx", [128, 16], F32)
    we = c.sb("we", [128, 256], BF16)
    sm = c.sb("sm", [128, 16], F32)
    wn = c.sb("wn", [128, 256], BF16)
    wfm = c.sb("wfm", [128, 2, 128], BF16)
    ofm = c.sb("ofm", [128, 8, M], BF16)
    raw = c.sb("raw", [128, 8, M], F32)
    rawb = c.sb("rawb", [128, 8, M], BF16)
    x3 = c.sb("x3", [128, 8, M], F32)
    ln = mk_ln(c, M)
    ln2 = dict(ln, bc=c.sb("ln2_bc", [128, 2, M], F32),
               ps_stat=c.ps("ln2_psstat", [128, 512], F32))
    ps_big = c.ps("ps_big", [128, 1024])
    ps_lg = c.ps("ps_lg", [128, 512])
    ps_trb = c.ps("ps_trb", [128, 1024], BF16)
    ps_o = c.ps("ps_o", [128, 1024])

    with blk:
        ln_init(c, ln)
        c.inc('G', nc.gpsimd.dma_start(out=Waq[:, :, :], in_=WaqT_d.rearrange("(k p) n -> p k n", p=128)), 16)
        c.inc('G', nc.gpsimd.dma_start(out=WpT[:, :, :], in_=WpT_d.rearrange("(k p) n -> p k n", p=128)), 16)
        c.inc('G', nc.gpsimd.dma_start(out=idb[:, :], in_=idb_d), 16)
        c.inc('L', nc.sync.dma_start(out=mask[:, :], in_=mask_d), 16)

        xr = xT.rearrange("(k p) n -> p k n", p=128)
        x3r = x3T_d.rearrange("(k p) n -> p k n", p=128)
        xhr = xhT.rearrange("(k p) n -> p k n", p=128)

        with nc.sbuf_tensor(c.prefix + "Wakv", [128, 8, 2 * D], BF16) as Wakv:
            c.inc('G', nc.gpsimd.dma_start(out=Wakv[:, :, :],
                                           in_=WakvT_d.rearrange("(k p) n -> p k n", p=128)), 16)
            emit_chunk_load(c, xb, xf, 128, xhr=xhr)
            emit_ln(c, ln, [xb[:, t, 0:128] for t in range(8)], 128)
            c.wait(nc.vector, 'L')
            emit_normalize(c, ln, [xb[:, t, 0:128] for t in range(8)],
                           [xn[:, t, 0:128] for t in range(8)], 128, tmpf)
            c.wait(nc.tensor, 'V')
            for t in range(8):
                for k in range(8):
                    i = nc.tensor.matmul(ps_big[:, t * 64:t * 64 + HDR],
                                         Wakv[:, k, t * 128:(t + 1) * 128],
                                         xn[:, k, 0:HDR], start=(k == 0), stop=(k == 7),
                                         skip_group_check=True)
                c.inc('P', i)
            c.wait(nc.scalar, 'P')
            for t in range(8):
                i = nc.scalar.copy(khdr[:, t, :], ps_big[:, t * 64:t * 64 + HDR])
            c.inc('A', i)
            c.wait(nc.tensor, 'A')
            for half in range(2):
                for k in range(8):
                    i = nc.tensor.matmul(ps_lg[0:HDR, :], xn[:, k, 0:HDR],
                                         Wakv[:, k, D + half * 512: D + half * 512 + 512],
                                         start=(k == 0), stop=(k == 7), skip_group_check=True)
                c.inc('P', i)
                c.wait(nc.scalar, 'P')
                i = nc.scalar.copy(tmpf[0:HDR, 0:512], ps_lg[0:HDR, :])
                c.inc('A', i)
                c.wait(nc.sync, 'A')
                c.inc('S', nc.sync.dma_start(out=vh_dram[:, half * 512:half * 512 + 512],
                                             in_=tmpf[0:HDR, 0:512]), 16)
                nc.tensor.wait_ge(c.sems['S'], c.cnt['S'])
            i = nc.gpsimd.memset(vblk[:, :, :], 0.0)
            c.inc('G', i)
            c.wait(nc.gpsimd, 'S')
            vap = vh_dram.ap().rearrange("t (h x) -> t h x", h=16)
            for g in range(2):
                for hp in range(8):
                    i = nc.gpsimd.dma_start(
                        out=vblk[16 * hp:16 * hp + 16, g, 64 * hp:64 * hp + 64],
                        in_=vap[:, 8 * g + hp, :])
                    c.inc('G', i, 16)

        c.wait(nc.gpsimd, 'V')
        c.wait(nc.sync, 'V')
        c.inc('G', nc.gpsimd.dma_start(out=xb[:, :, :], in_=xr[:, :, 0:M]), 16)
        c.inc('L', nc.sync.dma_start(out=xf[:, :, :], in_=xr[:, :, 0:M]), 16)
        nc.scalar.wait_ge(c.sems['G'], c.cnt['G'])
        nc.tensor.wait_ge(c.sems['G'], c.cnt['G'])
        nc.vector.wait_ge(c.sems['G'], c.cnt['G'])
        emit_ln(c, ln, [xb[:, t, :] for t in range(8)], M)
        for ch in range(8):
            sl = slice(ch * M, (ch + 1) * M)
            c.wait(nc.vector, 'L')
            emit_normalize(c, ln, [xb[:, t, :] for t in range(8)],
                           [xn[:, t, :] for t in range(8)], M, tmpf)
            v_norm = c.cnt['V']
            c.wait(nc.tensor, 'V', 'A')
            emit_proj_fm(c, Waq, xn, qfm, M, ps_big)
            vh_lg, vh_wfm, ah_ofm = [], [], []
            c.wait(nc.tensor, 'A', 'V')
            for ms in range(4):
                msl = slice(ms * 128, ms * 128 + 128)
                if ms >= 1:
                    nc.tensor.wait_ge(c.sems['V'], vh_lg[ms - 1])
                for par_pass in range(2):
                    for h in range(par_pass, NHEAD, 2):
                        par = 64 * (h % 2)
                        i = nc.tensor.matmul(ps_lg[:, h * 16:(h + 1) * 16],
                                             qfm[par:par + 64, h // 2, msl],
                                             khdr[par:par + 64, h // 2, :],
                                             start=(h == 0), stop=(h == 15),
                                             skip_group_check=True)
                    if par_pass == 0:
                        c.inc('P', i)
                        nc.tensor.wait_ge(c.sems['P'], c.cnt['P'])
                c.inc('P', i)
                c.wait(nc.vector, 'P')
                if ch == 0 and ms == 0:
                    i = nc.vector.tensor_tensor(lg[:, :], ps_lg[:, 0:256], mask[:, :], op=ALU.add)
                else:
                    i = nc.vector.tensor_copy(lg[:, :], ps_lg[:, 0:256])
                c.inc('V', i)
                vh_lg.append(c.cnt['V'])
                i = nc.vector.reduce_max(mx[:, :].unsqueeze(2),
                                         lg[:, :].rearrange("p (h j) -> p h j", h=16), axis=AXX)
                c.inc('V', i)
                nc.vector.wait_ge(c.sems['V'], c.cnt['V'])
                i = nc.vector.tensor_tensor(
                    lg[:, :].rearrange("p (h j) -> p h j", h=16),
                    lg[:, :].rearrange("p (h j) -> p h j", h=16),
                    mx[:, :].unsqueeze(2).broadcast_to([128, 16, 16]), op=ALU.subtract)
                c.inc('V', i)
                c.wait(nc.scalar, 'V')
                i = nc.scalar.activation(we[:, :], lg[:, :], AF.Exp, scale=0.125)
                c.inc('A', i)
                c.wait(nc.vector, 'A')
                i = nc.vector.reduce_sum(sm[:, :].unsqueeze(2),
                                         we[:, :].rearrange("p (h j) -> p h j", h=16), axis=AXX)
                c.inc('V', i)
                nc.vector.wait_ge(c.sems['V'], c.cnt['V'])
                i = nc.vector.reciprocal(sm[:, :], sm[:, :])
                c.inc('V', i)
                nc.vector.wait_ge(c.sems['V'], c.cnt['V'])
                i = nc.vector.tensor_tensor(
                    wn[:, :].rearrange("p (h j) -> p h j", h=16),
                    we[:, :].rearrange("p (h j) -> p h j", h=16),
                    sm[:, :].unsqueeze(2).broadcast_to([128, 16, 16]), op=ALU.mult)
                c.inc('V', i)
                if ms >= 1:
                    nc.tensor.wait_ge(c.sems['V'], vh_wfm[ms - 1])
                nc.tensor.wait_ge(c.sems['V'], c.cnt['V'])
                for g in range(2):
                    i = nc.tensor.transpose(ps_trb[:, g * 128:(g + 1) * 128],
                                            wn[:, g * 128:(g + 1) * 128], idb[:, :])
                c.inc('P', i)
                c.wait(nc.vector, 'P')
                for g in range(2):
                    i = nc.vector.tensor_copy(wfm[:, g, :], ps_trb[:, g * 128:(g + 1) * 128])
                c.inc('V', i)
                vh_wfm.append(c.cnt['V'])
                if ms >= 1:
                    nc.tensor.wait_ge(c.sems['A'], ah_ofm[ms - 1])
                nc.tensor.wait_ge(c.sems['V'], c.cnt['V'])
                for t in range(8):
                    g = t // 4
                    i = nc.tensor.matmul(ps_o[:, t * 128:(t + 1) * 128],
                                         vblk[:, g, (t % 4) * 128:(t % 4) * 128 + 128],
                                         wfm[:, g, :],
                                         start=(t in (0, 4)), stop=(t in (3, 7)),
                                         skip_group_check=True)
                c.inc('P', i)
                c.wait(nc.scalar, 'P')
                for t in range(8):
                    i = nc.scalar.copy(ofm[:, t, msl], ps_o[:, t * 128:(t + 1) * 128])
                c.inc('A', i)
                ah_ofm.append(c.cnt['A'])
            if ch + 1 < 8:
                nsl = slice((ch + 1) * M, (ch + 2) * M)
                nc.gpsimd.wait_ge(c.sems['V'], v_norm)
                c.inc('G', nc.gpsimd.dma_start(out=xb[:, :, :], in_=xr[:, :, nsl]), 16)
                nc.scalar.wait_ge(c.sems['G'], c.cnt['G'])
                nc.tensor.wait_ge(c.sems['G'], c.cnt['G'])
                nc.vector.wait_ge(c.sems['G'], c.cnt['G'])
                emit_ln(c, ln, [xb[:, t, :] for t in range(8)], M)
            c.wait(nc.tensor, 'A')
            emit_proj_fm2(c, WpT, ofm, raw, rawb, M, ps_big)
            c.wait(nc.scalar, 'V', 'L')
            emit_ln(c, ln2, [rawb[:, t, :] for t in range(8)], M)
            c.wait(nc.vector, 'L')
            emit_normalize(c, ln2, [raw[:, t, :] for t in range(8)], None, M, tmpf,
                           add_tiles=[xf[:, t, :] for t in range(8)],
                           out3=[x3[:, t, :] for t in range(8)])
            c.wait(nc.sync, 'V')
            c.inc('S', nc.sync.dma_start(out=x3r[:, :, sl], in_=x3[:, :, :]), 16)
            if ch + 1 < 8:
                c.wait(nc.sync, 'S')
                c.inc('L', nc.sync.dma_start(out=xf[:, :, :], in_=xr[:, :, nsl]), 16)
        c.wait(nc.sync, 'S')
        c.wait(nc.gpsimd, 'S')
    es.close()
    return nc


def build_L1F(env=None):
    nc, es, c, blk = begin(env, "pe_")
    M = 256
    H = 4096
    HT = H // 128
    xT = c.dram("xT", [D, ROWS], F32, "in")
    w1T_d = c.dram("w1T", [D, H], F32, "in")
    w2T_d = c.dram("w2T", [H, D], F32, "in")
    b1c_d = c.dram("b1c", [128, 32], F32, "in")
    b2c_d = c.dram("b2c", [128, 8], F32, "in")
    idf_d = c.dram("idf", [128, 128], F32, "in")
    y_d = c.dram("y", [ROWS, D], F32, "out")
    a1_dram = nc.dram_tensor(c.prefix + "a1_dram", [32, 128], F32)
    a2_dram = nc.dram_tensor(c.prefix + "a2_dram", [8, 128], F32)

    w1s = c.sb("w1s", [128, 8, H], BF16)
    w2s = c.sb("w2s", [128, HT, D], BF16)
    b1c = c.sb("b1c", [128, 32], F32)
    b2c = c.sb("b2c", [128, 8], F32)
    a1c = c.sb("a1c", [128, 32], F32)
    a2c = c.sb("a2c", [128, 8], F32)
    idf = c.sb("idf", [128, 128], F32)
    ones_col = c.sb("ones_colc", [128, 1], BF16)
    stage = c.sb("stage", [1, 512], F32)
    stage8 = c.sb("stage8", [8, 512], F32)
    sel8 = c.sb("sel8", [128, 8, 8], BF16)
    a1t = c.sb("a1t", [32, 128], F32)
    a2t = c.sb("a2t", [8, 128], F32)
    xb = c.sb("xb", [128, 8, M], BF16)
    xf = c.sb("xf", [128, 8, M], F32)
    xn = c.sb("xn", [128, 8, M], BF16)
    tmpf = c.sb("tmpf", [128, M], F32)
    hid = c.sb("hid", [128, HT, M], BF16)
    raw = c.sb("raw", [128, 8, M], F32)
    rawb = c.sb("rawb", [128, 8, M], BF16)
    x4 = c.sb("x4", [128, 8, M], F32)
    ytm = c.sb("ytm", [128, D], F32)
    ln = mk_ln(c, M)
    ps_h = [c.ps("ps_h0", [128, 512]), c.ps("ps_h1", [128, 512]),
            c.ps("ps_h2", [128, 512]), c.ps("ps_h3", [128, 512])]
    ps_d = [c.ps("ps_d0", [128, 512]), c.ps("ps_d1", [128, 512])]
    ps_tf = c.ps("ps_tf", [128, 512])
    ln2 = dict(ln, bc=c.sb("ln2_bc", [128, 2, M], F32), ps_stat=ps_tf)

    with blk:
        ln_init(c, ln)
        i = nc.gpsimd.memset(ones_col[:, :], 2.0 ** -10)
        c.inc('G', i)
        c.inc('L', nc.sync.dma_start(out=b1c[:, :], in_=b1c_d), 16)
        c.inc('L', nc.sync.dma_start(out=b2c[:, :], in_=b2c_d), 16)
        c.inc('L', nc.sync.dma_start(out=idf[:, :], in_=idf_d), 16)
        wst = [raw[:, :, :].rearrange("p a b -> p (a b)"),
               x4[:, :, :].rearrange("p a b -> p (a b)")]
        wab = [hid[:, 0:8, :].rearrange("p a b -> p (a b)"),
               hid[:, 8:16, :].rearrange("p a b -> p (a b)")]
        nc.gpsimd.memset(sel8[:, :, :], 0.0)
        for s in range(8):
            i = nc.gpsimd.memset(sel8[:, s, s:s + 1], 2.0 ** -10)
        c.inc('G', i)
        w1r = w1T_d.rearrange("(k p) h -> p k h", p=128)
        w2r = w2T_d.rearrange("(k p) h -> p k h", p=128)
        ahist, vhist, phist = [], [], []
        qsn = []
        for kk in range(16 + HT):
            w1p = kk < 16
            if w1p:
                k, half = kk // 2, kk % 2
                hs = slice(half * 2048, half * 2048 + 2048)
                wsl = slice(0, 2048)
            else:
                k = kk - 16
                wsl = slice(0, D)
            b = kk % 2
            deng, dsem = (nc.gpsimd, 'G') if b == 0 else (nc.sync, 'L')
            if kk >= 2:
                deng.wait_ge(c.sems['A'], ahist[kk - 2])
                deng.wait_ge(c.sems['V'], vhist[kk - 2])
            src = w1r[:, k, hs] if w1p else w2r[:, k, :]
            c.inc(dsem, deng.dma_start(out=wst[b][:, wsl], in_=src), 16)
            qsn.append((dsem, c.cnt[dsem]))
            nc.scalar.wait_ge(c.sems[dsem], c.cnt[dsem])
            dst = w1s[:, k, hs] if w1p else w2s[:, k, :]
            i = nc.scalar.activation(dst, wst[b][:, wsl], AF.Sign)
            c.inc('A', i)
            ahist.append(c.cnt['A'])
            nc.vector.wait_ge(c.sems[dsem], qsn[kk][1])
            if kk >= 2:
                nc.vector.wait_ge(c.sems['P'], phist[kk - 2])
            i = nc.vector.scalar_tensor_tensor(wab[b][:, wsl], wst[b][:, wsl], -1.0,
                                               wst[b][:, wsl], op0=ALU.mult, op1=ALU.max)
            c.inc('V', i)
            vhist.append(c.cnt['V'])
            nc.tensor.wait_ge(c.sems['V'], vhist[kk])
            if w1p:
                for q in range(4):
                    s = half * 4 + q
                    i = nc.tensor.matmul(ps_h[0][0:8, :], sel8[:, s, :],
                                         wab[b][:, q * 512:(q + 1) * 512],
                                         start=(kk == 0 and q == 0),
                                         stop=(kk == 15 and q == 3), skip_group_check=True)
            else:
                nc.tensor.matmul(ps_d[0][0:1, :], ones_col[:, :], wab[b][:, 0:512],
                                 start=(k == 0), stop=(k == HT - 1), skip_group_check=True)
                i = nc.tensor.matmul(ps_d[0][64:65, :], ones_col[:, :], wab[b][:, 512:1024],
                                     start=(k == 0), stop=(k == HT - 1), skip_group_check=True)
            c.inc('P', i)
            phist.append(c.cnt['P'])
            if kk == 15:
                c.wait(nc.scalar, 'P')
                i = nc.scalar.copy(stage8[:, :], ps_h[0][0:8, :])
                c.inc('A', i)
                ahist[-1] = c.cnt['A']
                c.wait(nc.sync, 'A')
                c.inc('S', nc.sync.dma_start(
                    out=a1_dram.ap().rearrange("(s f) cc -> s f cc", s=8),
                    in_=stage8[:, :].rearrange("p (f cc) -> p f cc", f=4)), 16)
        c.wait(nc.scalar, 'P')
        i = nc.scalar.activation(stage[:, :], ps_d[0][0:1, :], AF.Identity, scale=0.25)
        c.inc('A', i)
        c.wait(nc.gpsimd, 'A')
        c.inc('G', nc.gpsimd.dma_start(out=a2_dram[0:4, :].unsqueeze(0),
                                       in_=stage[:, :].rearrange("o (a b) -> o a b", a=4)), 16)
        nc.scalar.wait_ge(c.sems['G'], c.cnt['G'])
        i = nc.scalar.activation(stage[:, :], ps_d[0][64:65, :], AF.Identity, scale=0.25)
        c.inc('A', i)
        c.wait(nc.gpsimd, 'A')
        c.inc('G', nc.gpsimd.dma_start(out=a2_dram[4:8, :].unsqueeze(0),
                                       in_=stage[:, :].rearrange("o (a b) -> o a b", a=4)), 16)
        c.wait(nc.gpsimd, 'G', 'S')
        c.inc('G', nc.gpsimd.dma_start(out=a1t[:, :], in_=a1_dram.ap()), 16)
        c.inc('G', nc.gpsimd.dma_start(out=a2t[:, :], in_=a2_dram.ap()), 16)
        c.wait(nc.tensor, 'G')
        nc.tensor.transpose(ps_tf[:, 0:32], a1t[:, :], idf[0:32, 0:32])
        i = nc.tensor.transpose(ps_tf[:, 32:40], a2t[:, :], idf[0:8, 0:8])
        c.inc('P', i)
        c.wait(nc.scalar, 'P')
        nc.scalar.copy(a1c[:, :], ps_tf[:, 0:32])
        i = nc.scalar.copy(a2c[:, :], ps_tf[:, 32:40])
        c.inc('A', i)

        xr = xT.rearrange("(k p) n -> p k n", p=128)
        c.inc('G', nc.gpsimd.dma_start(out=xb[:, :, :], in_=xr[:, :, 0:M]), 16)
        c.inc('L', nc.sync.dma_start(out=xf[:, :, :], in_=xr[:, :, 0:M]), 16)
        nc.scalar.wait_ge(c.sems['G'], c.cnt['G'])
        nc.tensor.wait_ge(c.sems['G'], c.cnt['G'])
        nc.vector.wait_ge(c.sems['G'], c.cnt['G'])
        emit_ln(c, ln, [xb[:, t, :] for t in range(8)], M)
        for ch in range(16):
            sl = slice(ch * M, (ch + 1) * M)
            c.wait(nc.vector, 'L')
            emit_normalize(c, ln, [xb[:, t, :] for t in range(8)],
                           [xn[:, t, :] for t in range(8)], M, tmpf)
            v_norm = c.cnt['V']
            c.wait(nc.tensor, 'V', 'A')
            ah = []
            for ht in range(HT):
                hsl = slice(ht * 128, (ht + 1) * 128)
                ph = ps_h[ht % 4]
                if ht >= 4:
                    nc.tensor.wait_ge(c.sems['A'], ah[ht - 4])
                for k in range(8):
                    i = nc.tensor.matmul(ph[:, 0:M], w1s[:, k, hsl],
                                         xn[:, k, :], start=(k == 0), stop=(k == 7),
                                         skip_group_check=True)
                c.inc('P', i)
                c.wait(nc.scalar, 'P')
                i = nc.scalar.activation(hid[:, ht, :], ph[:, 0:M], AF.Gelu,
                                         scale=a1c[:, ht:ht + 1], bias=b1c[:, ht:ht + 1])
                c.inc('A', i)
                ah.append(c.cnt['A'])
            if ch + 1 < 16:
                nsl = slice((ch + 1) * M, (ch + 2) * M)
                nc.gpsimd.wait_ge(c.sems['V'], v_norm)
                c.inc('G', nc.gpsimd.dma_start(out=xb[:, :, :], in_=xr[:, :, nsl]), 16)
                nc.scalar.wait_ge(c.sems['G'], c.cnt['G'])
                nc.tensor.wait_ge(c.sems['G'], c.cnt['G'])
                nc.vector.wait_ge(c.sems['G'], c.cnt['G'])
                emit_ln(c, ln, [xb[:, t, :] for t in range(8)], M)
            c.wait(nc.tensor, 'A')
            wvh = []
            for t in range(8):
                dsl = slice(t * 128, (t + 1) * 128)
                pd = ps_d[t % 2]
                if t >= 2:
                    nc.tensor.wait_ge(c.sems['V'], wvh[t - 2])
                for ht in range(HT):
                    i = nc.tensor.matmul(pd[:, 0:M], w2s[:, ht, dsl], hid[:, ht, :],
                                         start=(ht == 0), stop=(ht == HT - 1), skip_group_check=True)
                c.inc('P', i)
                c.wait(nc.scalar, 'P')
                i = nc.scalar.activation(raw[:, t, :], pd[:, 0:M], AF.Identity,
                                         scale=a2c[:, t:t + 1], bias=b2c[:, t:t + 1])
                c.inc('A', i)
                c.wait(nc.vector, 'A')
                i = nc.vector.tensor_copy(rawb[:, t, :], raw[:, t, :])
                c.inc('V', i)
                wvh.append(c.cnt['V'])
            c.wait(nc.scalar, 'V', 'L')
            emit_ln(c, ln2, [rawb[:, t, :] for t in range(8)], M)
            c.wait(nc.vector, 'L')
            emit_normalize(c, ln2, [raw[:, t, :] for t in range(8)], None, M, tmpf,
                           add_tiles=[xf[:, t, :] for t in range(8)],
                           out3=[x4[:, t, :] for t in range(8)])
            if ch + 1 < 16:
                c.wait(nc.sync, 'V')
                c.inc('L', nc.sync.dma_start(out=xf[:, :, :], in_=xr[:, :, nsl]), 16)
            for ms in range(2):
                c.wait(nc.tensor, 'V', 'S')
                for t in range(4):
                    i = nc.tensor.transpose(ps_tf[:, t * 128:(t + 1) * 128],
                                            x4[:, t, ms * 128:(ms + 1) * 128], idf[:, :])
                c.inc('P', i)
                c.wait(nc.scalar, 'P')
                for t in range(4):
                    i = nc.scalar.copy(ytm[:, t * 128:(t + 1) * 128], ps_tf[:, t * 128:(t + 1) * 128])
                c.inc('A', i)
                c.wait(nc.tensor, 'A')
                for t in range(4, 8):
                    i = nc.tensor.transpose(ps_tf[:, (t - 4) * 128:(t - 3) * 128],
                                            x4[:, t, ms * 128:(ms + 1) * 128], idf[:, :])
                c.inc('P', i)
                c.wait(nc.scalar, 'P')
                for t in range(4, 8):
                    i = nc.scalar.copy(ytm[:, t * 128:(t + 1) * 128],
                                       ps_tf[:, (t - 4) * 128:(t - 3) * 128])
                c.inc('A', i)
                c.wait(nc.sync, 'A')
                c.inc('S', nc.sync.dma_start(
                    out=y_d[ch * M + ms * 128: ch * M + ms * 128 + 128, :], in_=ytm[:, :]), 16)
        c.wait(nc.sync, 'S')
        c.wait(nc.gpsimd, 'S')
    es.close()
    return nc


def _cols(b, n):
    return np.ascontiguousarray(b.reshape(n, 128).T)


def build_exchange(env):
    nc, es, c, blk = begin(env, "ex_")
    pes = env['c'].es if env is not None else es
    stats_in = c.dram("stats_in", [STATS_ROWS, D], F32, "in")
    stats_out = c.dram("stats_out", [STATS_ROWS, D], F32, "out")
    par_d = c.dram("parin", [1, 1], I32, "in")
    non_d = c.dram("noncein", [1, 1], I32, "in")
    sh_st = nc.dram_tensor(c.prefix + "sh_st", [2 * STATS_ROWS, D], F32, addr_space="Shared")
    sh_fl = nc.dram_tensor(c.prefix + "sh_fl", [2, 128], I32, addr_space="Shared")
    st8 = c.sb("st8", [STATS_ROWS, D], F32)
    pst = c.sb("pst", [STATS_ROWS, D], F32)
    kmA = c.sb("kmA", [1, 16], F32)
    kmB = c.sb("kmB", [1, 16], F32)
    pari = c.sb("pari", [1, 1], I32)
    nonc = c.sb("nonc", [1, 1], I32)
    flagb = c.sb("flagb", [1, 1], I32)
    exs = pes.enter_context(nc.semaphore("exs"))
    exp = pes.enter_context(nc.semaphore("exp"))
    g = nc.gpsimd
    with blk:
        c.inc('L', nc.sync.dma_start(out=st8[:, :], in_=stats_in), 16)
        c.inc('L', nc.sync.dma_start(out=kmA[:, :], in_=stats_in[73:74, 0:16]), 16)
        c.inc('G', g.dma_start(out=pari[:, :], in_=par_d), 16)
        c.inc('G', g.dma_start(out=nonc[:, :], in_=non_d), 16)
        c.wait(g, 'G', 'L')
        with (
            g.register("preg") as preg,
            g.register("df") as df,
            g.register("tgt") as tgt,
        ):
            g.reg_load(preg, pari[0:1, 0:1])
            want_v = g.value_load(nonc[0:1, 0:1])
            for parval in (0, 1):
                my0 = parval * STATS_ROWS
                pe0 = (1 - parval) * STATS_ROWS
                guard = g.If_eq(preg, 0) if parval == 0 else g.Else()
                with guard:
                    g.dma_start(out=sh_st.ap()[my0:my0 + STATS_ROWS, :],
                                in_=st8[:, :]).then_inc(exs, 16)
                    g.wait_ge(exs, 16)
                    g.dma_start(out=sh_fl.ap()[parval:parval + 1, 0:1],
                                in_=nonc[0:1, 0:1]).then_inc(exs, 16)
                    g.wait_ge(exs, 32)
                    g.reg_mov(tgt, 0)
                    g.reg_mov(df, 1)
                    with g.While(df):
                        g.reg_alu(tgt, tgt, 16, ALU.add)
                        g.dma_start(out=flagb[:, :],
                                    in_=sh_fl.ap()[1 - parval:2 - parval, 0:1]).then_inc(exp, 16)
                        g.wait_ge(exp, g.snap(tgt))
                        got_v = g.value_load(flagb[0:1, 0:1])
                        g.reg_alu(df, got_v, want_v, ALU.subtract)
                    g.dma_start(out=pst[:, :],
                                in_=sh_st.ap()[pe0:pe0 + STATS_ROWS, :]).then_inc(exs, 16)
                    g.dma_start(out=kmB[:, :],
                                in_=sh_st.ap()[pe0 + 73:pe0 + 74, 0:16]).then_inc(exs, 16)
                    g.wait_ge(exs, 64)
            i = g.nop()
            c.inc('G', i)
        c.wait(nc.vector, 'G')
        nc.vector.tensor_tensor(st8[0:64, :], st8[0:64, :], pst[0:64, :], op=ALU.add)
        i = nc.vector.tensor_tensor(st8[64:80, :], st8[64:80, :], pst[64:80, :], op=ALU.add)
        c.inc('V', i)
        i = nc.vector.tensor_tensor(kmA[:, :], kmA[:, :], kmB[:, :], op=ALU.max)
        c.inc('V', i)
        c.wait(nc.sync, 'V')
        c.inc('S', nc.sync.dma_start(out=stats_out[:, :], in_=st8[:, :]), 16)
        c.wait(nc.sync, 'S')
        c.inc('S', nc.sync.dma_start(out=stats_out[73:74, 0:16], in_=kmA[:, :]), 16)
        c.wait(nc.sync, 'S')
    es.close()
    return nc


def build_exch_write(env, sh_st, sh_fl, exs):
    nc, es, c, blk = begin(env, "ew_")
    stats_in = c.dram("stats_in", [STATS_ROWS, D], F32, "in")
    par_d = c.dram("parin", [1, 1], I32, "in")
    non_d = c.dram("noncein", [1, 1], I32, "in")
    st8 = c.sb("st8", [STATS_ROWS, D], F32)
    pari = c.sb("pari", [1, 1], I32)
    nonc = c.sb("nonc", [1, 1], I32)
    g = nc.gpsimd
    with blk:
        c.inc('L', nc.sync.dma_start(out=st8[:, :], in_=stats_in), 16)
        c.inc('G', g.dma_start(out=pari[:, :], in_=par_d), 16)
        c.inc('G', g.dma_start(out=nonc[:, :], in_=non_d), 16)
        c.wait(g, 'G', 'L')
        with g.register("ewpreg") as preg:
            g.reg_load(preg, pari[0:1, 0:1])
            for parval in (0, 1):
                my0 = parval * STATS_ROWS
                guard = g.If_eq(preg, 0) if parval == 0 else g.Else()
                with guard:
                    g.dma_start(out=sh_st.ap()[my0:my0 + STATS_ROWS, :],
                                in_=st8[:, :]).then_inc(exs, 16)
                    g.wait_ge(exs, 16)
                    g.dma_start(out=sh_fl.ap()[parval:parval + 1, 0:1],
                                in_=nonc[0:1, 0:1]).then_inc(exs, 16)
            i = g.nop()
            c.inc('G', i)
    es.close()
    return nc


def build_exch_poll(env, sh_st, sh_fl, exs, exp):
    nc, es, c, blk = begin(env, "ep_")
    stats_in = c.dram("stats_in", [STATS_ROWS, D], F32, "in")
    stats_out = c.dram("stats_out", [STATS_ROWS, D], F32, "out")
    par_d = c.dram("parin", [1, 1], I32, "in")
    non_d = c.dram("noncein", [1, 1], I32, "in")
    st8 = c.sb("st8", [STATS_ROWS, D], F32)
    pst = c.sb("pst", [STATS_ROWS, D], F32)
    kmA = c.sb("kmA", [1, 16], F32)
    kmB = c.sb("kmB", [1, 16], F32)
    pari = c.sb("pari", [1, 1], I32)
    nonc = c.sb("nonc", [1, 1], I32)
    flagb = c.sb("flagb", [1, 1], I32)
    g = nc.gpsimd
    with blk:
        c.inc('L', nc.sync.dma_start(out=st8[:, :], in_=stats_in), 16)
        c.inc('L', nc.sync.dma_start(out=kmA[:, :], in_=stats_in[73:74, 0:16]), 16)
        c.inc('G', g.dma_start(out=pari[:, :], in_=par_d), 16)
        c.inc('G', g.dma_start(out=nonc[:, :], in_=non_d), 16)
        c.wait(g, 'G', 'L')
        with (
            g.register("eppreg") as preg,
            g.register("epdf") as df,
            g.register("eptgt") as tgt,
        ):
            g.reg_load(preg, pari[0:1, 0:1])
            want_v = g.value_load(nonc[0:1, 0:1])
            for parval in (0, 1):
                pe0 = (1 - parval) * STATS_ROWS
                guard = g.If_eq(preg, 0) if parval == 0 else g.Else()
                with guard:
                    g.wait_ge(exs, 32)
                    g.reg_mov(tgt, 0)
                    g.reg_mov(df, 1)
                    with g.While(df):
                        g.reg_alu(tgt, tgt, 16, ALU.add)
                        g.dma_start(out=flagb[:, :],
                                    in_=sh_fl.ap()[1 - parval:2 - parval, 0:1]).then_inc(exp, 16)
                        g.wait_ge(exp, g.snap(tgt))
                        got_v = g.value_load(flagb[0:1, 0:1])
                        g.reg_alu(df, got_v, want_v, ALU.subtract)
                    g.dma_start(out=pst[:, :],
                                in_=sh_st.ap()[pe0:pe0 + STATS_ROWS, :]).then_inc(exs, 16)
                    g.dma_start(out=kmB[:, :],
                                in_=sh_st.ap()[pe0 + 73:pe0 + 74, 0:16]).then_inc(exs, 16)
                    g.wait_ge(exs, 64)
            i = g.nop()
            c.inc('G', i)
        c.wait(nc.vector, 'G')
        nc.vector.tensor_tensor(st8[0:64, :], st8[0:64, :], pst[0:64, :], op=ALU.add)
        i = nc.vector.tensor_tensor(st8[64:80, :], st8[64:80, :], pst[64:80, :], op=ALU.add)
        c.inc('V', i)
        i = nc.vector.tensor_tensor(kmA[:, :], kmA[:, :], kmB[:, :], op=ALU.max)
        c.inc('V', i)
        c.wait(nc.sync, 'V')
        c.inc('S', nc.sync.dma_start(out=stats_out[:, :], in_=st8[:, :]), 16)
        c.wait(nc.sync, 'S')
        c.inc('S', nc.sync.dma_start(out=stats_out[73:74, 0:16], in_=kmA[:, :]), 16)
        c.wait(nc.sync, 'S')
    es.close()
    return nc


def build_APPLY1(env, qp_all):
    nc, es, c, blk = begin(env, "p1_")
    M = 512
    xT = c.dram("xT", [D, ROWS], F32, "in")
    xhT = c.dram("xhT", [D, HDR], F32, "in")
    WqT_d = c.dram("WqT", [D, D], F32, "in")
    P_d = c.dram("Pstage", [128, D], F32, "in")
    bqc_d = c.dram("bqc", [128, 8], F32, "in")
    dia_d = c.dram("diagones", [128, 8, 16], F32, "in")
    idf_d = c.dram("idf", [128, 128], F32, "in")
    idb_d = c.dram("idb", [128, 128], F32, "in")

    WqT = c.sb("WqT", [128, 8, D], BF16)
    Pst = c.sb("Pst", [128, D], BF16)
    bqc = c.sb("bqc", [128, 8], F32)
    bqcs = c.sb("bqcs", [128, 8], F32)
    dia = c.sb("dia", [128, 8, 16], BF16)
    idf = c.sb("idf", [128, 128], F32)
    idb = c.sb("idb", [128, 128], BF16)
    xb = c.sb("xb", [128, 8, M], BF16)
    xn = c.sb("xn", [128, 8, M], BF16)
    tmpf = c.sb("tmpf", [128, M], F32)
    qbf = c.sb("qbf", [128, 8, M], BF16)
    sqq = c.sb("sqq", [128, 8, M], BF16)
    diag_fm = c.sb("diag_fm", [16, M], F32)
    diag_tm = c.sb("diag_tm", [128, 4, 16], F32)
    stab = c.sb("stab", [128, 16], F32)
    pre = c.sb("pre", [128, D], F32)
    qp_tm = c.sb("qp_tm", [128, 4, D], BF16)
    ln = mk_ln(c, M)
    ps_big = c.ps("ps_big", [128, 1024])
    ps_big2 = c.ps("ps_big2", [128, 1024])
    ps_trb = c.ps("ps_trb", [128, 1024], BF16)
    ps_trf = c.ps("ps_trf", [128, 512])

    with blk:
        ln_init(c, ln)
        c.inc('G', nc.gpsimd.dma_start(out=WqT[:, :, :], in_=WqT_d.rearrange("(k p) n -> p k n", p=128)), 16)
        c.inc('G', nc.gpsimd.dma_start(out=Pst[:, :], in_=P_d), 16)
        c.inc('G', nc.gpsimd.dma_start(out=dia[:, :, :], in_=dia_d), 16)
        c.inc('G', nc.gpsimd.dma_start(out=idb[:, :], in_=idb_d), 16)
        c.inc('L', nc.sync.dma_start(out=idf[:, :], in_=idf_d), 16)
        c.inc('L', nc.sync.dma_start(out=bqc[:, :], in_=bqc_d), 16)
        nc.scalar.wait_ge(c.sems['L'], c.cnt['L'])
        i = nc.scalar.activation(bqcs[:, :], bqc[:, :], AF.Identity, scale=np.float32(CFAV))
        c.inc('A', i)

        xr = xT.rearrange("(k p) n -> p k n", p=128)
        xhr = xhT.rearrange("(k p) n -> p k n", p=128)
        for ch in range(9):
            hdr = (ch == 8)
            m = 128 if hdr else M
            nms = m // 128
            base = ch * M
            sl = None if hdr else slice(ch * M, (ch + 1) * M)
            if ch:
                c.wait(nc.gpsimd, 'V')
                c.wait(nc.sync, 'V')
            emit_chunk_load(c, xb, None, m, xr=xr, sl=sl, xhr=(xhr if hdr else None),
                            need_f32=False)
            emit_ln(c, ln, [xb[:, t, 0:m] for t in range(8)], m)
            c.wait(nc.vector, 'L')
            emit_normalize(c, ln, [xb[:, t, 0:m] for t in range(8)],
                           [xn[:, t, 0:m] for t in range(8)], m, tmpf)
            c.wait(nc.tensor, 'V', 'A')
            emit_proj_fm(c, WqT, xn, qbf, m, ps_big, bias_col=bqcs, scale=CFAV)
            emit_diag(c, dia, sqq, qbf, diag_fm, diag_tm, idf, ps_trf, ps_trf, m)
            vbase = c.cnt['V']
            abase = c.cnt['A']
            vh_ps = []
            for ms in range(nms):
                msl = slice(ms * 128, ms * 128 + 128)
                psx = ps_big2 if (ms % 2) else ps_big
                if ms >= 2:
                    nc.tensor.wait_ge(c.sems['V'], vh_ps[ms - 2])
                else:
                    nc.tensor.wait_ge(c.sems['V'], vbase)
                    nc.tensor.wait_ge(c.sems['A'], abase)
                emit_favorp_head_mm(c, qbf, Pst, psx, msl)
                c.wait(nc.vector, 'P', 'A')
                nc.vector.reduce_max(stab[:, :].unsqueeze(2),
                                     psx[:, :].rearrange("p (h j) -> p h j", h=16),
                                     axis=AXX)
                i = nc.vector.scalar_tensor_tensor(stab[:, :], diag_tm[:, ms, :], -0.5,
                                                   stab[:, :], op0=ALU.mult, op1=ALU.subtract)
                c.inc('V', i)
                nc.vector.wait_ge(c.sems['V'], c.cnt['V'])
                i = nc.vector.tensor_tensor(
                    pre[:, :].rearrange("p (h j) -> p h j", h=16),
                    psx[:, :].rearrange("p (h j) -> p h j", h=16),
                    stab[:, :].unsqueeze(2).broadcast_to([128, 16, 64]), op=ALU.add)
                c.inc('V', i)
                vh_ps.append(c.cnt['V'])
                c.wait(nc.scalar, 'V')
                i = nc.scalar.activation(pre[:, :], pre[:, :], AF.Exp)
                c.inc('A', i)
                c.wait(nc.vector, 'A')
                i = nc.vector.tensor_scalar(qp_tm[:, ms, :], pre[:, :], 1.0, float(EPS_FEAT),
                                            op0=ALU.mult, op1=ALU.add)
                c.inc('V', i)
                c.wait(nc.tensor, 'V')
                for t in range(8):
                    i = nc.tensor.transpose(ps_trb[:, t * 128:(t + 1) * 128],
                                            qp_tm[:, ms, t * 128:(t + 1) * 128], idb[:, :])
                c.inc('P', i)
                c.wait(nc.vector, 'P')
                for t in range(8):
                    i = nc.vector.tensor_copy(
                        qp_all[:, t, base + ms * 128:base + ms * 128 + 128],
                        ps_trb[:, t * 128:(t + 1) * 128])
                c.inc('V', i)
    es.close()
    return nc


def build_APPLY2(env, qp_all):
    nc, es, c, blk = begin(env, "p2_")
    M = 512
    xT = c.dram("xT", [D, ROWS], F32, "in")
    xhT = c.dram("xhT", [D, HDR], F32, "in")
    statsall_d = c.dram("statsall", [NCORE * STATS_ROWS, D], F32, "in")
    WoT_d = c.dram("WoT", [D, D], F32, "in")
    boc_d = c.dram("boc", [128, 8], F32, "in")
    idf_d = c.dram("idf", [128, 128], F32, "in")
    idb_d = c.dram("idb", [128, 128], F32, "in")
    x1T_d = c.dram("x1T", [D, ROWS], F32, "out")
    x1hT_d = c.dram("x1hT", [D, HDR], F32, "out")

    f_dram = nc.dram_tensor(c.prefix + "f_dram", [1, 16], F32)
    vs_dram = nc.dram_tensor(c.prefix + "vs_dram", [1, D], F32)
    ks_dram = nc.dram_tensor(c.prefix + "ks_dram", [8, 128], F32)

    WoT = c.sb("WoT", [128, 8, D], BF16)
    boc = c.sb("boc", [128, 8], F32)
    idf = c.sb("idf", [128, 128], F32)
    idb = c.sb("idb", [128, 128], BF16)
    stsum = c.sb("stsum", [80, D], F32)
    stabk = c.sb("stabk", [16, 1], F32)
    fexp = c.sb("fexp", [16, 1], F32)
    kv_stage = c.sb("kv_stage", [128, 512], F32)
    f_pack = c.sb("f_pack", [128, 8], F32)
    vs_pack = c.sb("vs_pack", [128, 8, 64], F32)
    kv_sb = c.sb("kv_sb", [128, 8, 64], BF16)
    ks_t = c.sb("ks_t", [8, 128], F32)
    f_e = c.sb("f_e", [8, 1], F32)
    f_o = c.sb("f_o", [8, 1], F32)
    BDk = c.sb("BDk", [128, 8, 16], BF16)

    xf = c.sb("xf", [128, 8, M], F32)
    tmpf = c.sb("tmpf", [128, M], F32)
    nrm = c.sb("nrm", [16, M], F32)
    rec_tm = c.sb("rec_tm", [128, 4, 16], F32)
    at_tm = c.sb("at_tm", [128, D], BF16)
    at_fm = c.sb("at_fm", [128, 8, M], BF16)
    raw = c.sb("raw", [128, 8, M], F32)
    rawb = c.sb("rawb", [128, 8, M], BF16)
    x1 = c.sb("x1", [128, 8, M], F32)
    ln = mk_ln(c, M)
    ps_aux = c.ps("ps_aux", [128, 512])
    ps_big = c.ps("ps_big", [128, 1024])
    ps_big2 = c.ps("ps_big2", [128, 1024])
    ps_trb = c.ps("ps_trb", [128, 1024], BF16)
    ps_trf = c.ps("ps_trf", [128, 512])

    with blk:
        ln_init(c, ln)
        c.inc('G', nc.gpsimd.dma_start(out=WoT[:, :, :], in_=WoT_d.rearrange("(k p) n -> p k n", p=128)), 16)
        c.inc('G', nc.gpsimd.dma_start(out=idb[:, :], in_=idb_d), 16)
        c.inc('L', nc.sync.dma_start(out=idf[:, :], in_=idf_d), 16)
        c.inc('L', nc.sync.dma_start(out=boc[:, :], in_=boc_d), 16)
        c.inc('L', nc.sync.dma_start(out=stsum[:, :], in_=statsall_d[0:STATS_ROWS, :]), 16)
        with nc.allow_non_contiguous_dma(reason="16-element kmax row"):
            c.inc('L', nc.sync.dma_start(out=stabk[:, :],
                                         in_=statsall_d[73:74, 0:16].transpose([1, 0])), 16)
        c.wait(nc.vector, 'L')
        c.wait(nc.scalar, 'V', 'L')
        i = nc.scalar.activation(fexp[:, :], stabk[:, :], AF.Exp, scale=-1.0)
        c.inc('A', i)
        c.wait(nc.gpsimd, 'A', 'V')
        stk = stsum[0:64, :].rearrange("p (a q x) -> p a q x", a=8, q=2)
        c.inc('G', nc.gpsimd.dma_start(
            out=kv_stage[0:64, :].rearrange("p (a x) -> p a x", a=8),
            in_=stk[:, :, 0:1, :].squeeze(2)), 16)
        c.inc('G', nc.gpsimd.dma_start(
            out=kv_stage[64:128, :].rearrange("p (a x) -> p a x", a=8),
            in_=stk[:, :, 1:2, :].squeeze(2)), 16)
        c.inc('G', nc.gpsimd.dma_start(out=f_dram.ap().transpose([1, 0]), in_=fexp[:, :]), 16)
        c.inc('G', nc.gpsimd.dma_start(out=ks_dram[:, :], in_=stsum[64:72, 0:128]), 16)
        c.inc('G', nc.gpsimd.dma_start(out=vs_dram[:, :], in_=stsum[72:73, :]), 16)
        c.wait(nc.gpsimd, 'G')
        fap = f_dram.ap().squeeze(0).rearrange("(a p) -> a p", a=8)
        with nc.allow_non_contiguous_dma(reason="16-element parity unpack"):
            c.inc('G', nc.gpsimd.dma_start(
                out=f_pack[0:64, :].unsqueeze(2),
                in_=fap[:, 0:1].unsqueeze(0).partition_broadcast(64).squeeze(1)), 16)
            c.inc('G', nc.gpsimd.dma_start(
                out=f_pack[64:128, :].unsqueeze(2),
                in_=fap[:, 1:2].unsqueeze(0).partition_broadcast(64).squeeze(1)), 16)
            c.inc('G', nc.gpsimd.dma_start(out=f_e[:, :], in_=fap[:, 0:1]), 16)
            c.inc('G', nc.gpsimd.dma_start(out=f_o[:, :], in_=fap[:, 1:2]), 16)
        vap = vs_dram.ap().squeeze(0).rearrange("(a p x) -> a p x", a=8, p=2)
        c.inc('G', nc.gpsimd.dma_start(
            out=vs_pack[0:64, :, :], in_=vap[:, 0:1, :].squeeze(1).unsqueeze(0).partition_broadcast(64).squeeze(1)), 16)
        c.inc('G', nc.gpsimd.dma_start(
            out=vs_pack[64:128, :, :], in_=vap[:, 1:2, :].squeeze(1).unsqueeze(0).partition_broadcast(64).squeeze(1)), 16)
        c.inc('G', nc.gpsimd.dma_start(out=ks_t[:, :], in_=ks_dram[:, :]), 16)
        c.wait(nc.vector, 'G')
        nc.vector.tensor_scalar(vs_pack[:, :, :], vs_pack[:, :, :], float(EPS_FEAT), 0.0,
                                op0=ALU.mult, op1=ALU.add)
        nc.vector.tensor_tensor(
            kv_stage[:, :].rearrange("p (a x) -> p a x", a=8),
            kv_stage[:, :].rearrange("p (a x) -> p a x", a=8),
            f_pack[:, :].unsqueeze(2).broadcast_to([128, 8, 64]), op=ALU.mult)
        i = nc.vector.tensor_tensor(kv_sb[:, :, :],
                                    kv_stage[:, :].rearrange("p (a x) -> p a x", a=8),
                                    vs_pack[:, :, :], op=ALU.add)
        nc.vector.tensor_scalar(ks_t[:, 0:64], ks_t[:, 0:64], f_e[0:8, 0:1],
                                float(EPS_FEAT * 16384.0), op0=ALU.mult, op1=ALU.add)
        i = nc.vector.tensor_scalar(ks_t[:, 64:128], ks_t[:, 64:128], f_o[0:8, 0:1],
                                    float(EPS_FEAT * 16384.0), op0=ALU.mult, op1=ALU.add)
        c.inc('V', i)
        i = nc.gpsimd.memset(BDk[:, :, :], 0.0)
        c.inc('G', i)
        c.wait(nc.gpsimd, 'V')
        c.inc('G', nc.gpsimd.dma_start(out=ks_dram[:, :], in_=ks_t[:, :]), 16)
        c.wait(nc.gpsimd, 'G')
        for t in range(8):
            for par in range(2):
                i = nc.gpsimd.dma_start(
                    out=BDk[64 * par:64 * par + 64, t, 2 * t + par:2 * t + par + 1],
                    in_=ks_dram.ap()[t:t + 1, 64 * par:64 * par + 64].transpose([1, 0]))
                c.inc('G', i, 16)

        xr = xT.rearrange("(k p) n -> p k n", p=128)
        x1r = x1T_d.rearrange("(k p) n -> p k n", p=128)
        xhr = xhT.rearrange("(k p) n -> p k n", p=128)
        x1hr = x1hT_d.rearrange("(k p) n -> p k n", p=128)
        c.wait(nc.tensor, 'G')
        for ch in range(9):
            hdr = (ch == 8)
            m = 128 if hdr else M
            nms = m // 128
            base = ch * M
            sl = None if hdr else slice(ch * M, (ch + 1) * M)
            if ch:
                c.wait(nc.gpsimd, 'V')
                c.wait(nc.sync, 'V')
            if hdr:
                i = nc.gpsimd.memset(xf[:, :, 0:m], 0.0)
                c.inc('G', i)
                c.wait(nc.sync, 'G')
                c.inc('L', nc.sync.dma_start(out=xf[:, :, 0:HDR], in_=xhr), 16)
            else:
                c.inc('L', nc.sync.dma_start(out=xf[:, :, :], in_=xr[:, :, sl]), 16)
            c.wait(nc.tensor, 'A', 'V')
            for t in range(8):
                i = nc.tensor.matmul(ps_aux[32:48, 0:m], BDk[:, t, :],
                                     qp_all[:, t, base:base + m],
                                     start=(t == 0), stop=(t == 7), skip_group_check=True)
            c.inc('P', i)
            c.wait(nc.scalar, 'P')
            i = nc.scalar.copy(nrm[:, 0:m], ps_aux[32:48, 0:m])
            c.inc('A', i)
            c.wait(nc.vector, 'A')
            nc.vector.tensor_scalar_max(nrm[:, 0:m], nrm[:, 0:m], float(EPS_DEN))
            i = nc.vector.reciprocal(nrm[:, 0:m], nrm[:, 0:m])
            c.inc('V', i)
            c.wait(nc.tensor, 'V')
            for ms in range(nms):
                i = nc.tensor.transpose(ps_trf[:, 256 + ms * 16:256 + ms * 16 + 16],
                                        nrm[:, ms * 128:(ms + 1) * 128], idf[0:16, 0:16])
                c.inc('P', i)
            c.wait(nc.scalar, 'P')
            for ms in range(nms):
                i = nc.scalar.copy(rec_tm[:, ms, :], ps_trf[:, 256 + ms * 16:256 + ms * 16 + 16])
            c.inc('A', i)
            vbase = c.cnt['V']
            abase = c.cnt['A']
            vh_at = []
            for ms in range(nms):
                msl = slice(ms * 128, ms * 128 + 128)
                gsl = slice(base + ms * 128, base + ms * 128 + 128)
                psx = ps_big2 if (ms % 2) else ps_big
                if ms >= 2:
                    nc.tensor.wait_ge(c.sems['V'], vh_at[ms - 2])
                else:
                    nc.tensor.wait_ge(c.sems['V'], vbase)
                    nc.tensor.wait_ge(c.sems['A'], abase)
                for par_pass in range(2):
                    for h in range(par_pass, NHEAD, 2):
                        par = 64 * (h % 2)
                        i = nc.tensor.matmul(psx[:, h * 64:(h + 1) * 64],
                                             qp_all[par:par + 64, h // 2, gsl],
                                             kv_sb[par:par + 64, h // 2, :],
                                             start=(h in (0, 8)), stop=(h in (7, 15)),
                                             skip_group_check=True)
                    if par_pass == 0:
                        c.inc('P', i)
                        nc.tensor.wait_ge(c.sems['P'], c.cnt['P'])
                c.inc('P', i)
                c.wait(nc.vector, 'P', 'A')
                i = nc.vector.tensor_tensor(
                    at_tm[:, :].rearrange("p (h j) -> p h j", h=16),
                    psx[:, :].rearrange("p (h j) -> p h j", h=16),
                    rec_tm[:, ms, :].unsqueeze(2).broadcast_to([128, 16, 64]), op=ALU.mult)
                c.inc('V', i)
                vh_at.append(c.cnt['V'])
                c.wait(nc.tensor, 'V')
                for t in range(8):
                    i = nc.tensor.transpose(ps_trb[:, t * 128:(t + 1) * 128],
                                            at_tm[:, t * 128:(t + 1) * 128], idb[:, :])
                c.inc('P', i)
                c.wait(nc.vector, 'P')
                for t in range(8):
                    i = nc.vector.tensor_copy(at_fm[:, t, msl], ps_trb[:, t * 128:(t + 1) * 128])
                c.inc('V', i)
            c.wait(nc.tensor, 'V')
            emit_proj_fm2(c, WoT, at_fm, raw, rawb, m, ps_big, bias_col=boc)
            nc.vector.wait_ge(c.sems['L'], c.cnt['L'])
            c.wait(nc.scalar, 'V')
            emit_ln(c, ln, [rawb[:, t, 0:m] for t in range(8)], m)
            c.wait(nc.vector, 'L')
            emit_normalize(c, ln, [raw[:, t, 0:m] for t in range(8)], None, m, tmpf,
                           add_tiles=[xf[:, t, 0:m] for t in range(8)],
                           out3=[x1[:, t, 0:m] for t in range(8)])
            c.wait(nc.sync, 'V')
            if hdr:
                c.inc('S', nc.sync.dma_start(out=x1hr, in_=x1[:, :, 0:HDR]), 16)
            else:
                c.inc('S', nc.sync.dma_start(out=x1r[:, :, sl], in_=x1[:, :, :]), 16)
        c.wait(nc.sync, 'S')
        c.wait(nc.gpsimd, 'S')
    es.close()
    return nc


def build_fused():
    nc = bass.Bass("TRN2", target_bir_lowering=False)
    es = ExitStack()
    c = Ctx(nc, es)
    BROWS = 2 * ROWS
    EI = lambda n, sh, dt=F32: nc.dram_tensor(n, sh, dt, kind="ExternalInput").ap()
    IN = lambda n, sh, dt=F32: nc.dram_tensor(n, sh, dt).ap()
    xT = EI("xT", [D, ROWS])
    xbT = EI("xbT", [D, BROWS])
    xhdrT = EI("xhdrT", [D, HDR])
    parin = EI("parin", [1, 1], I32)
    noncein = EI("noncein", [1, 1], I32)
    WkT = EI("WkT", [D, D]); WvT = EI("WvT", [D, D])
    WqT = EI("WqT", [D, D]); WoT = EI("WoT", [D, D])
    Pstage = EI("Pstage", [128, D])
    bkc = EI("bkc", [128, 8]); bqc = EI("bqc", [128, 8]); boc = EI("boc", [128, 8])
    bvr = EI("bvr", [1, D])
    dia = EI("diagones", [128, 8, 16])
    idf = EI("idf", [128, 128])
    wgT = EI("wgT", [D, 3072], I32); wuT = EI("wuT", [D, 3072], I32)
    wdT = EI("wdT", [3072, D], I32)
    WaqT = EI("WaqT", [D, D]); WakvT = EI("WakvT", [D, 2 * D]); WpT = EI("WpT", [D, D])
    l1mask = EI("l1mask", [128, 256])
    w1T = EI("w1T", [D, 4096]); w2T = EI("w2T", [4096, D])
    b1c = EI("b1c", [128, 32]); b2c = EI("b2c", [128, 8])
    y = nc.dram_tensor("y", [ROWS, D], F32, kind="ExternalOutput").ap()
    stats = IN("i_stats", [STATS_ROWS, D])
    stats2 = IN("i_stats2", [STATS_ROWS, D])
    x1T = IN("i_x1T", [D, ROWS]); x1hT = IN("i_x1hT", [D, HDR])
    x2T = IN("i_x2T", [D, ROWS]); x2hT = IN("i_x2hT", [D, HDR])
    x3T = IN("i_x3T", [D, ROWS])
    sh_st = nc.dram_tensor("fx_sh_st", [2 * STATS_ROWS, D], F32, addr_space="Shared")
    sh_fl = nc.dram_tensor("fx_sh_fl", [2, 128], I32, addr_space="Shared")
    exs = es.enter_context(nc.semaphore("fx_exs"))
    exp = es.enter_context(nc.semaphore("fx_exp"))
    with nc.Block():
        env = {'nc': nc, 'c': c}
        build_L0A_stats({**env, 'io': dict(xT=xbT, WkT=WkT, WvT=WvT, Pstage=Pstage, bkc=bkc,
                                           bvr=bvr, diagones=dia, idf=idf, stats=stats)},
                        rows=BROWS)
        full_barrier(c)
        build_exch_write({**env, 'io': dict(stats_in=stats, parin=parin, noncein=noncein)},
                         sh_st, sh_fl, exs)
        full_barrier(c)
        with nc.sbuf_tensor("fx_qp_all", [128, 8, 4224], BF16) as qp_all:
            build_APPLY1({**env, 'io': dict(xT=xT, xhT=xhdrT, WqT=WqT, Pstage=Pstage,
                                            bqc=bqc, diagones=dia, idf=idf, idb=idf)},
                         qp_all)
            full_barrier(c)
            build_exch_poll({**env, 'io': dict(stats_in=stats, stats_out=stats2,
                                               parin=parin, noncein=noncein)},
                            sh_st, sh_fl, exs, exp)
            full_barrier(c)
            build_APPLY2({**env, 'io': dict(xT=xT, xhT=xhdrT, statsall=stats2, WoT=WoT,
                                            boc=boc, idf=idf, idb=idf,
                                            x1T=x1T, x1hT=x1hT)},
                         qp_all)
        full_barrier(c)
        build_L0F({**env, 'io': dict(xT=x1T, xhT=x1hT, wgT=wgT, wuT=wuT, wdT=wdT,
                                     x2T=x2T, x2hT=x2hT)})
        full_barrier(c)
        build_L1A({**env, 'io': dict(xT=x2T, xhT=x2hT, WaqT=WaqT, WakvT=WakvT, WpT=WpT,
                                     l1mask=l1mask, idb=idf, x3T=x3T)})
        full_barrier(c)
        build_L1F({**env, 'io': dict(xT=x3T, w1T=w1T, w2T=w2T, b1c=b1c, b2c=b2c, idf=idf, y=y)})
    es.close()
    return nc


def kernel(L0_attn_query, L0_attn_Wq, L0_attn_bq, L0_attn_Wk, L0_attn_bk,
           L0_attn_Wv, L0_attn_bv, L0_attn_Wo, L0_attn_bo, L0_attn_proj,
           L0_ffn_w_gate, L0_ffn_w_up, L0_ffn_w_down,
           L1_attn_W_attn, L1_attn_W_proj, L1_attn_W_rel,
           L1_ffn_w1, L1_ffn_b1, L1_ffn_w2, L1_ffn_b2):
    _ = L1_attn_W_rel
    x = np.ascontiguousarray(np.asarray(L0_attn_query, dtype=np.float32).reshape(NCORE * ROWS, D))
    cores = list(range(NCORE))
    xT = [np.ascontiguousarray(x[c * ROWS:(c + 1) * ROWS].T) for c in cores]
    xbT = {(b, p): np.ascontiguousarray(
               x[b * 4 * ROWS + p * 2 * ROWS:b * 4 * ROWS + (p + 1) * 2 * ROWS].T)
           for b in range(2) for p in range(2)}
    xhdrT = [np.ascontiguousarray(x[b * 4 * ROWS:b * 4 * ROWS + HDR].T) for b in range(2)]
    nonce = int(np.random.default_rng().integers(1, 2**31 - 1))
    idf = np.eye(128, dtype=np.float32)
    P_stage = np.zeros((128, D), np.float32)
    proj = np.asarray(L0_attn_proj, dtype=np.float32)
    for h in range(NHEAD):
        par = 64 * (h % 2)
        P_stage[par:par + 64, h * 64:(h + 1) * 64] = proj[h].T
    diagones = np.zeros((128, 8, 16), np.float32)
    for t in range(8):
        diagones[0:64, t, 2 * t] = 1.0
        diagones[64:128, t, 2 * t + 1] = 1.0

    def T(w):
        return np.ascontiguousarray(np.asarray(w).T)

    Wa = np.asarray(L1_attn_W_attn, np.float32)
    l1mask = []
    for cc in cores:
        mk = np.zeros((128, 256), np.float32)
        if cc % 4 == 0:
            t_ar = np.arange(128)[:, None]
            kk = np.tile(np.arange(16)[None, :], (1, 16)).reshape(1, 256)
            mk = np.where(kk <= t_ar, 0.0, -1e30).astype(np.float32)
        l1mask.append(np.ascontiguousarray(mk))
    common = {
        "WkT": T(L0_attn_Wk), "WvT": T(L0_attn_Wv), "WqT": T(L0_attn_Wq), "WoT": T(L0_attn_Wo),
        "Pstage": P_stage, "diagones": diagones, "idf": idf,
        "bkc": _cols(np.asarray(L0_attn_bk, np.float32), 8),
        "bqc": _cols(np.asarray(L0_attn_bq, np.float32), 8),
        "boc": _cols(np.asarray(L0_attn_bo, np.float32), 8),
        "bvr": np.asarray(L0_attn_bv, np.float32).reshape(1, D),
        "wgT": T(L0_ffn_w_gate), "wuT": T(L0_ffn_w_up), "wdT": T(L0_ffn_w_down),
        "WaqT": T(Wa[0:D]), "WakvT": T(Wa[D:3 * D]), "WpT": T(L1_attn_W_proj),
        "w1T": T(L1_ffn_w1), "w2T": T(L1_ffn_w2),
        "b1c": _cols(np.asarray(L1_ffn_b1, np.float32), 32),
        "b2c": _cols(np.asarray(L1_ffn_b2, np.float32), 8),
    }
    nc = build_fused()
    in_maps = [dict(common, xT=xT[cc], xbT=xbT[(cc // 4, cc % 2)],
                    xhdrT=xhdrT[cc // 4], l1mask=l1mask[cc],
                    parin=np.array([[cc % 2]], np.int32),
                    noncein=np.array([[nonce]], np.int32)) for cc in cores]
    res = run_bass_kernel_spmd(nc, in_maps, core_ids=cores)
    yy = np.concatenate([res.results[cc]["y"] for cc in cores], axis=0)
    return yy.reshape(2, 16384, 1024)


def kernel_unfused(L0_attn_query, L0_attn_Wq, L0_attn_bq, L0_attn_Wk, L0_attn_bk,
           L0_attn_Wv, L0_attn_bv, L0_attn_Wo, L0_attn_bo, L0_attn_proj,
           L0_ffn_w_gate, L0_ffn_w_up, L0_ffn_w_down,
           L1_attn_W_attn, L1_attn_W_proj, L1_attn_W_rel,
           L1_ffn_w1, L1_ffn_b1, L1_ffn_w2, L1_ffn_b2):
    x = np.ascontiguousarray(np.asarray(L0_attn_query, dtype=np.float32).reshape(NCORE * ROWS, D))
    cores = list(range(NCORE))
    xT = [np.ascontiguousarray(x[c * ROWS:(c + 1) * ROWS].T) for c in cores]
    xhT = [np.ascontiguousarray(x[(c // 4) * 4 * ROWS:(c // 4) * 4 * ROWS + HDR].T) for c in cores]

    idf = np.eye(128, dtype=np.float32)
    P_stage = np.zeros((128, D), np.float32)
    proj = np.asarray(L0_attn_proj, dtype=np.float32)
    for h in range(NHEAD):
        par = 64 * (h % 2)
        P_stage[par:par + 64, h * 64:(h + 1) * 64] = proj[h].T
    diagones = np.zeros((128, 8, 16), np.float32)
    for t in range(8):
        diagones[0:64, t, 2 * t] = 1.0
        diagones[64:128, t, 2 * t + 1] = 1.0

    def T(w):
        return np.ascontiguousarray(np.asarray(w).T)

    nc = build_L0A_stats()
    common_a = {"WkT": T(L0_attn_Wk), "WvT": T(L0_attn_Wv), "Pstage": P_stage,
                "bkc": _cols(np.asarray(L0_attn_bk, np.float32), 8),
                "bvr": np.asarray(L0_attn_bv, np.float32).reshape(1, D),
                "diagones": diagones, "idf": idf}
    res = run_bass_kernel_spmd(nc, [dict(common_a, xT=xT[c]) for c in cores], core_ids=cores)
    statsall = np.ascontiguousarray(
        np.concatenate([res.results[c]["stats"] for c in cores], axis=0))

    nc = build_L0A_apply()
    w8 = []
    kmask = []
    for c in cores:
        g = c // 4
        w = np.zeros((128, 8), np.float32)
        w[:, 4 * g:4 * g + 4] = 1.0
        w8.append(w)
        km = np.full((8, 16), -1e30, np.float32)
        km[4 * g:4 * g + 4, :] = 0.0
        kmask.append(km)
    common_b = {"statsall": statsall, "WqT": T(L0_attn_Wq), "WoT": T(L0_attn_Wo),
                "Pstage": P_stage, "bqc": _cols(np.asarray(L0_attn_bq, np.float32), 8),
                "boc": _cols(np.asarray(L0_attn_bo, np.float32), 8),
                "diagones": diagones, "idf": idf, "idb": idf}
    res = run_bass_kernel_spmd(
        nc, [dict(common_b, xT=xT[c], xhT=xhT[c], w8=w8[c], kmask8=kmask[c]) for c in cores],
        core_ids=cores)
    x1T = [res.results[c]["x1T"] for c in cores]
    x1hT = [res.results[c]["x1hT"] for c in cores]

    nc = build_L0F()
    common_c = {"wgT": T(L0_ffn_w_gate), "wuT": T(L0_ffn_w_up), "wdT": T(L0_ffn_w_down)}
    res = run_bass_kernel_spmd(
        nc, [dict(common_c, xT=x1T[c], xhT=x1hT[c]) for c in cores], core_ids=cores)
    x2T = [res.results[c]["x2T"] for c in cores]
    x2hT = [res.results[c]["x2hT"] for c in cores]

    nc = build_L1A()
    Wa = np.asarray(L1_attn_W_attn, np.float32)
    l1mask = []
    for c in cores:
        mk = np.zeros((128, 256), np.float32)
        if c % 4 == 0:
            t_ar = np.arange(128)[:, None]
            kk = np.tile(np.arange(16)[None, :], (1, 16)).reshape(1, 256)
            mk = np.where(kk <= t_ar, 0.0, -1e30).astype(np.float32)
        l1mask.append(np.ascontiguousarray(mk))
    common_d = {"WaqT": T(Wa[0:D]), "WakvT": T(Wa[D:3 * D]), "WpT": T(L1_attn_W_proj),
                "idb": idf}
    res = run_bass_kernel_spmd(
        nc, [dict(common_d, xT=x2T[c], xhT=x2hT[c], l1mask=l1mask[c]) for c in cores],
        core_ids=cores)
    x3T = [res.results[c]["x3T"] for c in cores]

    nc = build_L1F()
    common_e = {"w1T": T(L1_ffn_w1), "w2T": T(L1_ffn_w2),
                "b1c": _cols(np.asarray(L1_ffn_b1, np.float32), 32),
                "b2c": _cols(np.asarray(L1_ffn_b2, np.float32), 8), "idf": idf}
    res = run_bass_kernel_spmd(
        nc, [dict(common_e, xT=x3T[c]) for c in cores], core_ids=cores)
    y = np.concatenate([res.results[c]["y"] for c in cores], axis=0)
    return y.reshape(2, 16384, 1024)
